# version 01
# baseline (speedup 1.0000x reference)
import numpy as np
import ml_dtypes
import math
from contextlib import ExitStack

import concourse.bass as bass
import concourse.mybir as mybir
from concourse.bass_utils import run_bass_kernel_spmd

F32 = mybir.dt.float32
BF16 = mybir.dt.bfloat16
I32 = mybir.dt.int32
AF = mybir.ActivationFunctionType
ALU = mybir.AluOpType
AX = mybir.AxisListType

SEMR = range(155, 256)
D = 1024
S = 2048
NSEQ = 4
TT = 16
NT = NSEQ * TT
NEG = -1.0e30

BF = ml_dtypes.bfloat16


def host_consts():
    c = {}
    c["cid"] = np.eye(128, dtype=BF)
    c["cidf"] = np.eye(128, dtype=np.float32)
    md = np.zeros((128, 128), dtype=np.float32)
    md[np.triu_indices(128, 1)] = NEG
    c["cmdiag"] = md
    c["ctril"] = np.tril(np.ones((128, 128), dtype=np.float32)).astype(BF)
    c["conesf"] = np.ones((1, 128), dtype=np.float32)
    c["conesb"] = np.ones((1, 128), dtype=BF)
    DHR = 32
    t = np.arange(S, dtype=np.float32)
    inv = 1.0 / 10000.0 ** (np.arange(0, DHR, 2, dtype=np.float32) / DHR)
    freqs = t[:, None] * inv[None, :]
    emb = np.concatenate([freqs, freqs], -1)
    rope = (np.cos(emb) + np.sin(emb)).astype(np.float32)
    c["cropet"] = np.ascontiguousarray(rope.T[np.arange(128) % 32])
    NB, MAXD = 32, 128
    n = np.arange(S)
    max_exact = NB // 2
    nf = np.maximum(n, 1).astype(np.float32)
    vil = max_exact + (np.log(nf / max_exact) / math.log(MAXD / max_exact)
                       ).astype(np.float32)
    vil = max_exact + (np.log(nf / max_exact) / math.log(MAXD / max_exact)
                       * (NB - max_exact)).astype(np.int32)
    vil = np.minimum(vil, NB - 1)
    buck = np.where(n < max_exact, n, vil)
    oh = np.zeros((NB, 2 * S), dtype=np.float32)
    for t_ in range(S):
        oh[buck[2047 - t_], t_] = 1.0
    c["cohrev"] = oh
    return c


HOST_CONSTS = host_consts()


class Ser:

    def __init__(self, eng, sem):
        self.eng, self.sem, self.n = eng, sem, 0

    def do(self, ins):
        self.n += 1
        ins.then_inc(self.sem, 1)
        self.eng.wait_ge(self.sem, self.n)
        return ins

    def done(self, ins, milestone, k=1):
        self.do(ins)
        self.eng.sem_inc(milestone, k)
        return ins


class Bar:

    def __init__(self, eng, sem):
        self.eng, self.sem, self.p = eng, sem, 0

    def hit(self, wait=None):
        self.p += 1
        if wait is not None:
            self.eng.wait_ge(wait[0], wait[1])
        self.eng.sem_inc(self.sem, 1)
        self.eng.wait_ge(self.sem, 5 * self.p)


class Ctx:

    def __init__(self, nc, es):
        self.nc = nc
        self.es = es
        self.dram = {}
        self.sems = {}

    def sem(self, name):
        if name not in self.sems:
            self.sems[name] = self.es.enter_context(self.nc.semaphore(name))
        return self.sems[name]

    def boundary(self):
        nc = self.nc
        nc.all_engine_barrier()
        nc.gpsimd.dma_reset(SEMR)
        nc.gpsimd.sem_clear(SEMR)
        nc.all_engine_barrier()


def preamble(nc):
    nc.gpsimd.dma_reset(SEMR)
    for e in (nc.gpsimd, nc.vector, nc.scalar, nc.tensor, nc.sync):
        e.sem_clear(SEMR)
    nc.all_engine_barrier()


def ln_s1(scalar, ss, xt, junk, stat):
    ss.do(scalar.activation(junk[:, :], xt[:, :], AF.Square, accum_out=stat[:, 2:3]))
    ss.do(scalar.activation(junk[:, :], xt[:, :], AF.Copy, accum_out=stat[:, 0:1]))
    a3 = scalar.activation(stat[:, 1:2], stat[:, 0:1], AF.Copy, scale=1.0 / D)
    ss.do(a3)
    return a3


def ln_s2(scalar, ss, stat):
    a4 = scalar.activation(stat[:, 4:5], stat[:, 3:4], AF.Sqrt)
    ss.do(a4)
    return a4


def ln_v1(vector, sv, stat, eps):
    sv.do(vector.tensor_mul(stat[:, 6:7], stat[:, 1:2], stat[:, 1:2]))
    sv.do(vector.scalar_tensor_tensor(stat[:, 3:4], stat[:, 2:3], 1.0 / D,
                                      stat[:, 6:7], ALU.mult, ALU.subtract))
    ins = vector.tensor_scalar_add(stat[:, 3:4], stat[:, 3:4], eps[:, 0:1])
    sv.do(ins)
    return ins


def ln_v2(vector, sv, stat, x, out_bf):
    sv.do(vector.reciprocal(stat[:, 5:6], stat[:, 4:5]))
    sv.do(vector.tensor_mul(stat[:, 7:8], stat[:, 1:2], stat[:, 5:6]))
    v5 = vector.tensor_scalar(out_bf[:, :], x[:, :], stat[:, 5:6], stat[:, 7:8],
                              ALU.mult, ALU.subtract)
    sv.do(v5)
    return v5


def st_l0_attn(c, src, dst):
    nc = c.nc
    w = c.dram
    sem = c.sem
    names = ["sLD", "sW", "sCV", "sSTD", "sVAR", "sSTD2", "sLN", "sTR", "sTE", "sVP",
             "sVE", "sQK", "sQE", "sSC", "sMX", "sEX", "sRC", "sSCL", "sTA",
             "sTAE", "sAV", "sAE", "sU", "sOUT", "sST", "sX2", "sBAR",
             "sSV", "sSS", "sSG"]
    (sLD, sW, sCV, sSTD, sVAR, sSTD2, sLN, sTR, sTE, sVP, sVE, sQK, sQE, sSC, sMX,
     sEX, sRC, sSCL, sTA, sTAE, sAV, sAE, sU, sOUT, sST, sX2, sBAR, sSV, sSS,
     sSG) = (sem(x) for x in names)

    itemsB = []
    tau_after = []
    tau = 0
    for s in range(NSEQ):
        for qt in range(TT):
            for h in range(16):
                H = (s * TT + qt) * 16 + h
                tau += qt + 1
                itemsB.append((s, qt, h, H, (qt + 1) * 128, s * TT + qt, H // 2))
                tau_after.append(tau)

    def tau_start(H):
        return tau_after[H] - (itemsB[H][1] + 1)

    with ExitStack() as ctx:
        sb = lambda name, shape, dt: ctx.enter_context(nc.sbuf_tensor(name, shape, dt))
        xt = [sb(f"l0a_xt{i}", [128, D], F32) for i in range(2)]
        outt = [sb(f"l0a_out{i}", [128, D], F32) for i in range(2)]
        xc = outt[1]
        sq = outt[0]
        stat = sb("l0a_st", [128, 8], F32)
        eps = sb("l0a_eps", [128, 1], F32)
        lnb = [sb(f"l0a_lnb{i}", [128, D], BF16) for i in range(2)]
        lnT = sb("l0a_lnT", [128, 8 * 512], BF16)
        Vt = sb("l0a_V", [128, 16 * D], BF16)
        QT = sb("l0a_QT", [128, 8 * S], BF16)
        KT = sb("l0a_KT", [128, 8 * S], BF16)
        att = [sb(f"l0a_att{i}", [128, S], BF16) for i in range(2)]
        attT = [sb(f"l0a_attT{i}", [128, 128], BF16) for i in range(2)]
        avh = sb("l0a_avh", [128, 8 * 128], BF16)
        sm = sb("l0a_sm", [128, 8], F32)
        wq = sb("l0a_wq", [128, 8 * D], BF16)
        wk = sb("l0a_wk", [128, 8 * D], BF16)
        wv = sb("l0a_wv", [128, 8 * D], BF16)
        wu = sb("l0a_wu", [128, 8 * D], BF16)
        wstg = [sb("l0a_wstg0", [128, D], F32)]
        buf = xt[0]
        bu_bc = sb("l0a_bubc", [128, D], BF16)
        PF = ctx.enter_context(nc.psum_tensor("l0a_pf", [128, 3072], F32))
        PT = [ctx.enter_context(nc.psum_tensor(f"l0a_pt{i}", [128, 128], BF16))
              for i in range(2)]
        ident, onesf, mdiag = c.ident, c.onesf, c.mdiag

        wlist = [(wq, "L0_attn_w_q"), (wk, "L0_attn_w_k"),
                 (wv, "L0_attn_w_v"), (wu, "L0_attn_w_u")]

        with nc.Block() as block:
            @block.sync
            def _(sync):
                bar = Bar(sync, sBAR)
                nw = 0
                for wsb, nm in wlist:
                    for ch in range(8):
                        if nw >= 1:
                            sync.wait_ge(sCV, nw)
                        sync.dma_start(out=wstg[0][:, :],
                                       in_=w[nm][128 * ch:128 * (ch + 1), :]
                                       ).then_inc(sW, 16)
                        nw += 1
                sync.dma_start(out=buf[0:1, :],
                               in_=w["L0_attn_b_u"][None, :]).then_inc(sW, 16)
                sync.wait_ge(sCV, 32)
                sync.wait_ge(sW, 16 * 33)
                bar.hit()
                for s in range(NSEQ):
                    for t in range(TT):
                        g = TT * s + t
                        if g >= 2:
                            sync.wait_ge(sLN, g - 1)
                        sync.dma_start(out=xt[g % 2][:, :],
                                       in_=src[128 * g:128 * (g + 1), :]
                                       ).then_inc(sLD, 16)
                    sync.wait_ge(sLD, 16 * TT * (s + 1))
                    bar.hit()
                    for qt in range(TT):
                        g = TT * s + qt
                        qtg = g
                        if qtg >= 2:
                            sync.wait_ge(sOUT, qtg - 1)
                        sync.dma_start(out=xt[qt % 2][:, :],
                                       in_=src[128 * g:128 * (g + 1), :]
                                       ).then_inc(sX2, 16)
                    sync.wait_ge(sX2, 16 * TT * (s + 1))
                    bar.hit()

            @block.vector
            def _(vector):
                bar = Bar(vector, sBAR)
                sv = Ser(vector, sSV)
                sv.do(vector.memset(eps[:, :], 1.0e-5))
                nw = 0
                for wsb, nm in wlist:
                    for ch in range(8):
                        vector.wait_ge(sW, 16 * (nw + 1))
                        ins = vector.tensor_copy(wsb[:, D * ch:D * (ch + 1)],
                                                 wstg[0][:, :])
                        sv.done(ins, sCV)
                        nw += 1
                vector.wait_ge(sQK, 2)
                sv.do(vector.tensor_copy(bu_bc[:, 0:512], PF[:, 2048:2560]))
                sv.do(vector.tensor_copy(bu_bc[:, 512:1024], PF[:, 2560:3072]))
                bar.hit()
                for s in range(NSEQ):
                    for t in range(TT):
                        g = TT * s + t
                        vector.wait_ge(sSTD, g + 1)
                        ln_v1(vector, sv, stat, eps)
                        vector.sem_inc(sVAR, 1)
                        vector.wait_ge(sSTD2, g + 1)
                        if g >= 2:
                            vector.wait_ge(sTR, 8 * (g - 1))
                        ln_v2(vector, sv, stat, xt[g % 2], lnb[g % 2])
                        vector.sem_inc(sLN, 1)
                        for k in range(8):
                            tr = 8 * g + k
                            vector.wait_ge(sTR, tr + 1)
                            ins = vector.tensor_copy(
                                lnT[:, 512 * k + 128 * (t % 4):
                                    512 * k + 128 * (t % 4 + 1)],
                                PT[tr % 2][:, :])
                            sv.done(ins, sTE)
                    bar.hit()
                    for (s_, qt, h, H, L, qtg, pr) in itemsB[s * 256:(s + 1) * 256]:
                        vector.wait_ge(sSC, H + 1)
                        sv.do(vector.tensor_tensor(PF[:, 128 * qt:128 * (qt + 1)],
                                                   PF[:, 128 * qt:128 * (qt + 1)],
                                                   mdiag[:, :], ALU.add))
                        sv.do(vector.reduce_max(sm[:, 0:1], PF[:, 0:L], axis=AX.X))
                        ins = vector.tensor_scalar_mul(sm[:, 1:2], sm[:, 0:1],
                                                       -1.0)
                        sv.done(ins, sMX)
                        vector.wait_ge(sEX, H + 1)
                        if H >= 1:
                            vector.wait_ge(sSCL, H)
                        ins = vector.reciprocal(sm[:, 3:4], sm[:, 2:3])
                        sv.done(ins, sRC)
                        t2 = tau_start(H)
                        for kt in range(qt + 1):
                            vector.wait_ge(sTA, t2 + 1)
                            ins = vector.tensor_copy(attT[t2 % 2][:, :],
                                                     PT[t2 % 2][:, :])
                            sv.done(ins, sTAE)
                            t2 += 1
                        if h == 15:
                            vector.wait_ge(sU, 2 * (qtg + 1))
                            vector.wait_ge(sX2, 16 * (qt + 1) + 16 * TT * s)
                            if qtg >= 2:
                                vector.wait_ge(sST, 16 * (qtg - 1))
                            for nb in range(2):
                                ins = vector.scalar_tensor_tensor(
                                    outt[qt % 2][:, 512 * nb:512 * (nb + 1)],
                                    PF[:, 2048 + 512 * nb:2048 + 512 * (nb + 1)],
                                    1.0, xt[qt % 2][:, 512 * nb:512 * (nb + 1)],
                                    ALU.mult, ALU.add)
                                sv.do(ins)
                            ins = vector.tensor_add(outt[qt % 2][:, :],
                                                    outt[qt % 2][:, :], bu_bc[:, :])
                            sv.done(ins, sOUT)
                    bar.hit()

            @block.scalar
            def _(scalar):
                bar = Bar(scalar, sBAR)
                ss = Ser(scalar, sSS)
                bar.hit()
                for s in range(NSEQ):
                    for t in range(TT):
                        g = TT * s + t
                        scalar.wait_ge(sLD, 16 * (g + 1))
                        ln_s1(scalar, ss, xt[g % 2], sq, stat)
                        scalar.sem_inc(sSTD, 1)
                        scalar.wait_ge(sVAR, g + 1)
                        ln_s2(scalar, ss, stat)
                        scalar.sem_inc(sSTD2, 1)
                        for nb in range(2):
                            scalar.wait_ge(sVP, 2 * g + nb + 1)
                            ins = scalar.activation(
                                Vt[:, D * t + 512 * nb:D * t + 512 * (nb + 1)],
                                PF[:, 2048 + 512 * nb:2048 + 512 * (nb + 1)],
                                AF.Copy)
                            ss.done(ins, sVE)
                        if t % 4 == 3:
                            gb = t // 4
                            for wi in range(2):
                                dstT = QT if wi == 0 else KT
                                for hc in range(8):
                                    q = (s * 4 + gb) * 16 + wi * 8 + hc
                                    scalar.wait_ge(sQK, q + 1 + 2)
                                    ins = scalar.activation(
                                        dstT[:, 2048 * hc + 512 * gb:
                                             2048 * hc + 512 * (gb + 1)],
                                        PF[:, 512 * (q % 2):512 * (q % 2 + 1)],
                                        AF.Copy, scale=0.125)
                                    ss.done(ins, sQE)
                    bar.hit()
                    for (s_, qt, h, H, L, qtg, pr) in itemsB[s * 256:(s + 1) * 256]:
                        scalar.wait_ge(sMX, H + 1)
                        if H >= 2:
                            scalar.wait_ge(sTA, tau_after[H - 2])
                        ins = scalar.activation(att[h % 2][:, 0:L], PF[:, 0:L],
                                                AF.Exp, bias=sm[:, 1:2],
                                                scale=1.0,
                                                accum_out=sm[:, 2:3])
                        ss.done(ins, sEX)
                        if h % 2 == 1:
                            hc = h // 2
                            scalar.wait_ge(sAV, 2 * (pr + 1))
                            if qtg >= 1:
                                scalar.wait_ge(sU, 2 * qtg)
                            ins = scalar.activation(
                                avh[:, 128 * hc:128 * (hc + 1)],
                                PF[:, 2048 + 512 * (pr % 2):
                                   2048 + 512 * (pr % 2) + 128],
                                AF.Copy)
                            ss.done(ins, sAE)
                    bar.hit()

            @block.tensor
            def _(tensor):
                bar = Bar(tensor, sBAR)
                tensor.wait_ge(sW, 16 * 33)
                tensor.matmul(PF[:, 2048:2560], onesf[0:1, :], buf[0:1, 0:512],
                              start=True, stop=True).then_inc(sQK, 1)
                ins = tensor.matmul(PF[:, 2560:3072], onesf[0:1, :],
                                    buf[0:1, 512:1024], start=True, stop=True)
                ins.then_inc(sQK, 1)
                bar.hit(wait=(sQK, 2))
                for s in range(NSEQ):
                    for t in range(TT):
                        g = TT * s + t
                        tensor.wait_ge(sLN, g + 1)
                        if t % 4 == 0 and g > 0:
                            pass
                        for k in range(8):
                            tr = 8 * g + k
                            if tr >= 2:
                                tensor.wait_ge(sTE, tr - 1)
                            ins = tensor.transpose(PT[tr % 2][:, :],
                                                   lnb[g % 2][:, 128 * k:128 * (k + 1)],
                                                   ident[:, :])
                            ins.then_inc(sTR, 1)
                        tensor.wait_ge(sTE, 8 * (g + 1))
                        tensor.wait_ge(sCV, 24)
                        for nb in range(2):
                            if g >= 1:
                                tensor.wait_ge(sVE, 2 * (g - 1) + nb + 1)
                            for ch in range(8):
                                ins = tensor.matmul(
                                    PF[:, 2048 + 512 * nb:2048 + 512 * (nb + 1)],
                                    lnT[:, 512 * ch + 128 * (t % 4):
                                        512 * ch + 128 * (t % 4 + 1)],
                                    wv[:, D * ch + 512 * nb:D * ch + 512 * (nb + 1)],
                                    start=(ch == 0), stop=(ch == 7))
                            ins.then_inc(sVP, 1)
                        if t % 4 == 3:
                            gb = t // 4
                            tensor.wait_ge(sCV, 16)
                            for wi in range(2):
                                wsb = wq if wi == 0 else wk
                                for hc in range(8):
                                    q = (s * 4 + gb) * 16 + wi * 8 + hc
                                    if q >= 2:
                                        tensor.wait_ge(sQE, q - 1)
                                    for ch in range(8):
                                        ins = tensor.matmul(
                                            PF[:, 512 * (q % 2):512 * (q % 2 + 1)],
                                            wsb[:, D * ch + 128 * hc:
                                                D * ch + 128 * (hc + 1)],
                                            lnT[:, 512 * ch:512 * (ch + 1)],
                                            start=(ch == 0), stop=(ch == 7))
                                    ins.then_inc(sQK, 1)
                    bar.hit(wait=(sQK, 2 + 64 * (s + 1)))
                    for (s_, qt, h, H, L, qtg, pr) in itemsB[s * 256:(s + 1) * 256]:
                        hc, hb = h // 2, (h % 2) * 64
                        tensor.wait_ge(sEX, H)
                        nkb = (L + 511) // 512
                        for kb in range(nkb):
                            nn = min(512, L - 512 * kb)
                            ins = tensor.matmul(
                                PF[:, 512 * kb:512 * kb + nn],
                                QT[hb:hb + 64,
                                   2048 * hc + 128 * qt:2048 * hc + 128 * (qt + 1)],
                                KT[hb:hb + 64, 2048 * hc + 512 * kb:
                                   2048 * hc + 512 * kb + nn],
                                start=True, stop=True)
                        ins.then_inc(sSC, 1)
                        tensor.wait_ge(sSCL, H + 1)
                        if h % 2 == 0 and pr >= 2:
                            tensor.wait_ge(sAE, pr - 1)
                        if h == 0 and qtg >= 1:
                            tensor.wait_ge(sOUT, qtg)
                        t2 = tau_start(H)
                        for kt in range(qt + 1):
                            if t2 >= 2:
                                tensor.wait_ge(sTAE, t2 - 1)
                            ins = tensor.transpose(PT[t2 % 2][:, :],
                                                   att[h % 2][:, 128 * kt:128 * (kt + 1)],
                                                   ident[:, :])
                            ins.then_inc(sTA, 1)
                            tensor.wait_ge(sTAE, t2 + 1)
                            ins = tensor.matmul(
                                PF[hb:hb + 64,
                                   2048 + 512 * (pr % 2):2048 + 512 * (pr % 2) + 128],
                                Vt[:, D * kt + 64 * h:D * kt + 64 * (h + 1)],
                                attT[t2 % 2][:, :],
                                start=(kt == 0), stop=(kt == qt))
                            t2 += 1
                        ins.then_inc(sAV, 1)
                        if h == 15:
                            tensor.wait_ge(sAE, 8 * (qtg + 1))
                            for nb in range(2):
                                for hc2 in range(8):
                                    ins = tensor.matmul(
                                        PF[:, 2048 + 512 * nb:2048 + 512 * (nb + 1)],
                                        avh[:, 128 * hc2:128 * (hc2 + 1)],
                                        wu[:, D * hc2 + 512 * nb:
                                           D * hc2 + 512 * (nb + 1)],
                                        start=(hc2 == 0), stop=(hc2 == 7))
                                ins.then_inc(sU, 1)
                    bar.hit(wait=(sU, 32 * (s + 1)))

            @block.gpsimd
            def _(gpsimd):
                bar = Bar(gpsimd, sBAR)
                sg = Ser(gpsimd, sSG)
                bar.hit()
                nst = 0
                for s in range(NSEQ):
                    bar.hit()
                    for (s_, qt, h, H, L, qtg, pr) in itemsB[s * 256:(s + 1) * 256]:
                        gpsimd.wait_ge(sRC, H + 1)
                        ins = gpsimd.tensor_scalar_mul(att[h % 2][:, 0:L],
                                                       att[h % 2][:, 0:L],
                                                       sm[:, 3:4])
                        sg.done(ins, sSCL)
                        if h == 15:
                            gpsimd.wait_ge(sOUT, qtg + 1)
                            g = TT * s + qt
                            gpsimd.dma_start(out=dst[128 * g:128 * (g + 1), :],
                                             in_=outt[qt % 2][:, :]).then_inc(sST, 16)
                            nst += 1
                    gpsimd.wait_ge(sST, 16 * nst)
                    bar.hit()

    c.boundary()


def st_l0_ffn(c, src, dst):
    nc = c.nc
    w = c.dram
    sem = c.sem
    names = ["sLD", "sW", "sCV", "sTW", "sTWE", "sSTD", "sVAR", "sSTD2", "sLN",
             "sTR", "sTE", "sGP", "sUP", "sSIL", "sHM", "sTH", "sTHE", "sDN",
             "sOUT", "sST", "sBAR", "sSV", "sSS", "sSG"]
    (sLD, sW, sCV, sTW, sTWE, sSTD, sVAR, sSTD2, sLN, sTR, sTE, sGP, sUP,
     sSIL, sHM, sTH, sTHE, sDN, sOUT, sST, sBAR, sSV, sSS, sSG) = (
        sem(x) for x in names)

    HID = 3072
    prep = []
    for wi, nm in enumerate(["L0_ffn_w_gate", "L0_ffn_w_up"]):
        for r in range(24):
            blocks = [(cb, HID * cb + 128 * r) for cb in range(8)]
            prep.append((nm, 128 * r, 0, wi, blocks))
    for r in range(8):
        for p in range(3):
            blocks = [(cb, D * (8 * p + cb) + 128 * r) for cb in range(8)]
            prep.append(("L0_ffn_w_down", 128 * r, 1024 * p, 2, blocks))
    NPREP = len(prep)

    with ExitStack() as ctx:
        sb = lambda name, shape, dt: ctx.enter_context(nc.sbuf_tensor(name, shape, dt))
        xt = [sb(f"l0f_xt{i}", [128, D], F32) for i in range(2)]
        outt = [sb(f"l0f_out{i}", [128, D], F32) for i in range(2)]
        sq = sb("l0f_sq", [128, D], F32)
        stat = sb("l0f_st", [128, 8], F32)
        eps = sb("l0f_eps", [128, 1], F32)
        lnb = [sb(f"l0f_lnb{i}", [128, D], BF16) for i in range(2)]
        lnT = [sb(f"l0f_lnT{i}", [128, 8 * 128], BF16) for i in range(2)]
        gs = sb("l0f_gs", [128, 512], BF16)
        hsb = [sb(f"l0f_h{i}", [128, 512], BF16) for i in range(2)]
        hT = [sb(f"l0f_hT{i}", [128, 24 * 128], BF16) for i in range(2)]
        wgT = sb("l0f_wgT", [128, 8 * HID], BF16)
        wuT = sb("l0f_wuT", [128, 8 * HID], BF16)
        wdT = sb("l0f_wdT", [128, 24 * D], BF16)
        wstg = sb("l0f_wstg", [128, D], I32)
        wbst = sb("l0f_wbst", [128, D], BF16)
        tgts = [wgT, wuT, wdT]
        PF = ctx.enter_context(nc.psum_tensor("l0f_pf", [128, 2048], F32))
        PT = [ctx.enter_context(nc.psum_tensor(f"l0f_pt{i}", [128, 128], BF16))
              for i in range(2)]
        ident = c.ident

        with nc.Block() as block:
            @block.sync
            def _(sync):
                bar = Bar(sync, sBAR)
                for i, (nm, r0, c0, ti, blocks) in enumerate(prep):
                    if i >= 1:
                        sync.wait_ge(sCV, i)
                    sync.dma_start(out=wstg[:, :],
                                   in_=w[nm][r0:r0 + 128, c0:c0 + 1024]
                                   ).then_inc(sW, 16)
                sync.wait_ge(sCV, NPREP)
                bar.hit()
                for g in range(NT):
                    if g >= 2:
                        sync.wait_ge(sOUT, g - 1)
                    sync.dma_start(out=xt[g % 2][:, :],
                                   in_=src[128 * g:128 * (g + 1), :]
                                   ).then_inc(sLD, 16)
                sync.wait_ge(sLD, 16 * NT)
                bar.hit()

            @block.vector
            def _(vector):
                bar = Bar(vector, sBAR)
                sv = Ser(vector, sSV)
                sv.do(vector.memset(eps[:, :], 1.0e-5))
                for i in range(NPREP):
                    vector.wait_ge(sW, 16 * (i + 1))
                    if i >= 1:
                        vector.wait_ge(sTW, 8 * i)
                    ins = vector.tensor_scalar_mul(wbst[:, :], wstg[:, :],
                                                   1.0 / 127.0)
                    sv.done(ins, sCV)
                    nm, r0, c0, ti, blocks = prep[i]
                    for k, (cb, toff) in enumerate(blocks):
                        tr = 8 * i + k
                        vector.wait_ge(sTW, tr + 1)
                        ins = vector.tensor_copy(tgts[ti][:, toff:toff + 128],
                                                 PT[tr % 2][:, :])
                        sv.done(ins, sTWE)
                bar.hit()
                for g in range(NT):
                    vector.wait_ge(sSTD, g + 1)
                    ln_v1(vector, sv, stat, eps)
                    vector.sem_inc(sVAR, 1)
                    vector.wait_ge(sSTD2, g + 1)
                    if g >= 2:
                        vector.wait_ge(sTR, 8 * (g - 1))
                    ln_v2(vector, sv, stat, xt[g % 2], lnb[g % 2])
                    vector.sem_inc(sLN, 1)
                    if g >= 2:
                        vector.wait_ge(sGP, 6 * (g - 1))
                    for k in range(8):
                        tr = 8 * g + k
                        vector.wait_ge(sTE, tr + 1)
                        ins = vector.tensor_copy(lnT[g % 2][:, 128 * k:128 * (k + 1)],
                                                 PT[tr % 2][:, :])
                        sv.done(ins, sTR)
                    if g >= 2:
                        vector.wait_ge(sDN, 2 * (g - 1))
                    for hb in range(6):
                        vector.wait_ge(sSIL, 6 * g + hb + 1)
                        if 6 * g + hb >= 2:
                            vector.wait_ge(sTH, 4 * (6 * g + hb - 1))
                        ins = vector.tensor_tensor(hsb[hb % 2][:, :], gs[:, :],
                                                   PF[:, 512:1024], ALU.mult)
                        sv.done(ins, sHM)
                        for k in range(4):
                            th = (6 * g + hb) * 4 + k
                            vector.wait_ge(sTH, th + 1)
                            ins = vector.tensor_copy(
                                hT[g % 2][:, 512 * hb + 128 * k:512 * hb + 128 * (k + 1)],
                                PT[th % 2][:, :])
                            sv.done(ins, sTHE)
                    vector.wait_ge(sDN, 2 * (g + 1))
                    if g >= 2:
                        vector.wait_ge(sST, 16 * (g - 1))
                    for nb in range(2):
                        ins = vector.scalar_tensor_tensor(
                            outt[g % 2][:, 512 * nb:512 * (nb + 1)],
                            PF[:, 1024 + 512 * nb:1024 + 512 * (nb + 1)],
                            1.0, xt[g % 2][:, 512 * nb:512 * (nb + 1)],
                            ALU.mult, ALU.add)
                        sv.do(ins)
                    vector.sem_inc(sOUT, 1)
                bar.hit()

            @block.scalar
            def _(scalar):
                bar = Bar(scalar, sBAR)
                ss = Ser(scalar, sSS)
                bar.hit()
                for g in range(NT):
                    scalar.wait_ge(sLD, 16 * (g + 1))
                    ln_s1(scalar, ss, xt[g % 2], sq, stat)
                    scalar.sem_inc(sSTD, 1)
                    scalar.wait_ge(sVAR, g + 1)
                    ln_s2(scalar, ss, stat)
                    scalar.sem_inc(sSTD2, 1)
                    for hb in range(6):
                        scalar.wait_ge(sGP, 6 * g + hb + 1)
                        if hb >= 1:
                            scalar.wait_ge(sHM, 6 * g + hb)
                        elif g >= 1:
                            scalar.wait_ge(sHM, 6 * g)
                        ins = scalar.activation(gs[:, :], PF[:, 0:512], AF.Silu)
                        ss.done(ins, sSIL)
                bar.hit()

            @block.tensor
            def _(tensor):
                bar = Bar(tensor, sBAR)
                for i in range(NPREP):
                    tensor.wait_ge(sCV, i + 1)
                    for k in range(8):
                        tr = 8 * i + k
                        if tr >= 2:
                            tensor.wait_ge(sTWE, tr - 1)
                        ins = tensor.transpose(PT[tr % 2][:, :],
                                               wbst[:, 128 * k:128 * (k + 1)],
                                               ident[:, :])
                        ins.then_inc(sTW, 1)
                bar.hit(wait=(sTWE, 8 * NPREP))
                for g in range(NT):
                    tensor.wait_ge(sLN, g + 1)
                    if g >= 1:
                        tensor.wait_ge(sTHE, 24 * g)
                    for k in range(8):
                        tr = 8 * g + k
                        tensor.wait_ge(sTR, tr - 1) if tr >= 2 else None
                        ins = tensor.transpose(PT[tr % 2][:, :],
                                               lnb[g % 2][:, 128 * k:128 * (k + 1)],
                                               ident[:, :])
                        ins.then_inc(sTE, 1)
                    tensor.wait_ge(sTR, 8 * (g + 1))
                    for hb in range(6):
                        if 6 * g + hb >= 1:
                            tensor.wait_ge(sSIL, 6 * g + hb)
                            tensor.wait_ge(sHM, 6 * g + hb)
                        for ch in range(8):
                            tensor.matmul(
                                PF[:, 0:512],
                                lnT[g % 2][:, 128 * ch:128 * (ch + 1)],
                                wgT[:, HID * ch + 512 * hb:HID * ch + 512 * (hb + 1)],
                                start=(ch == 0), stop=(ch == 7))
                        ins = tensor.matmul(
                            PF[:, 512:1024],
                            lnT[g % 2][:, 0:128],
                            wuT[:, 512 * hb:512 * (hb + 1)],
                            start=True, stop=False)
                        for ch in range(1, 8):
                            ins = tensor.matmul(
                                PF[:, 512:1024],
                                lnT[g % 2][:, 128 * ch:128 * (ch + 1)],
                                wuT[:, HID * ch + 512 * hb:HID * ch + 512 * (hb + 1)],
                                start=False, stop=(ch == 7))
                        ins.then_inc(sGP, 1)
                        tensor.wait_ge(sGP, 6 * g + hb + 1)
                        tensor.wait_ge(sHM, 6 * g + hb + 1)
                        for k in range(4):
                            th = (6 * g + hb) * 4 + k
                            if th >= 2:
                                tensor.wait_ge(sTHE, th - 1)
                            ins = tensor.transpose(PT[th % 2][:, :],
                                                   hsb[hb % 2][:, 128 * k:128 * (k + 1)],
                                                   ident[:, :])
                            ins.then_inc(sTH, 1)
                        tensor.wait_ge(sTHE, 4 * (6 * g + hb + 1))
                        if hb == 0 and g >= 1:
                            tensor.wait_ge(sOUT, g)
                        for nb in range(2):
                            for k in range(4):
                                j = 4 * hb + k
                                ins = tensor.matmul(
                                    PF[:, 1024 + 512 * nb:1024 + 512 * (nb + 1)],
                                    hT[g % 2][:, 128 * j:128 * (j + 1)],
                                    wdT[:, D * j + 512 * nb:D * j + 512 * (nb + 1)],
                                    start=(hb == 0 and k == 0),
                                    stop=(hb == 5 and k == 3))
                            if hb == 5:
                                ins.then_inc(sDN, 1)
                    bar_last = ins
                bar.hit(wait=(sDN, 2 * NT))

            @block.gpsimd
            def _(gpsimd):
                bar = Bar(gpsimd, sBAR)
                bar.hit()
                for g in range(NT):
                    gpsimd.wait_ge(sOUT, g + 1)
                    gpsimd.dma_start(out=dst[128 * g:128 * (g + 1), :],
                                     in_=outt[g % 2][:, :]).then_inc(sST, 16)
                gpsimd.wait_ge(sST, 16 * NT)
                bar.hit()

    c.boundary()


def st_l1_attn(c, src, dst):
    nc = c.nc
    w = c.dram
    sem = c.sem
    names = ["sLD", "sW", "sCV", "sTW", "sTWE", "sBC", "sSTD", "sVAR", "sSTD2",
             "sLN", "sTE", "sTR", "sSCN", "sYP", "sYE", "sSTDy", "sVARy",
             "sSTD2y", "sLNy", "sOUT", "sST", "sX2", "sBAR", "sSV", "sSS", "sSG"]
    (sLD, sW, sCV, sTW, sTWE, sBC, sSTD, sVAR, sSTD2, sLN, sTE, sTR, sSCN,
     sYP, sYE, sSTDy, sVARy, sSTD2y, sLNy, sOUT, sST, sX2, sBAR, sSV, sSS,
     sSG) = (sem(x) for x in names)

    with ExitStack() as ctx:
        sb = lambda name, shape, dt: ctx.enter_context(nc.sbuf_tensor(name, shape, dt))
        xt = [sb(f"l1a_xt{i}", [128, D], F32) for i in range(2)]
        sq = sb("l1a_sq", [128, D], F32)
        stat = sb("l1a_st", [128, 8], F32)
        eps = sb("l1a_eps", [128, 1], F32)
        lnb = [sb(f"l1a_lnb{i}", [128, D], BF16) for i in range(2)]
        ZT = sb("l1a_ZT", [128, 8 * S], BF16)
        ST = sb("l1a_STT", [128, 8 * S], BF16)
        a_t = sb("l1a_at", [128, S], F32)
        asc = sb("l1a_asc", [128, 2], F32)
        dsc = sb("l1a_dsc", [1, 2], F32)
        WT = sb("l1a_WT", [128, 8 * D], BF16)
        wstg = sb("l1a_wstg", [128, D], F32)
        wbst = sb("l1a_wbst", [128, D], BF16)
        brow = sb("l1a_brow", [1, D], F32)
        grow = sb("l1a_grow", [1, D], F32)
        berow = sb("l1a_berow", [1, D], F32)
        gam_bc = sb("l1a_gam", [128, D], F32)
        bet_bc = sb("l1a_bet", [128, D], F32)
        Yt = [sb(f"l1a_Y{i}", [128, D], F32) for i in range(2)]
        Ynb = [sb(f"l1a_Yn{i}", [128, D], BF16) for i in range(2)]
        outt = [sb(f"l1a_out{i}", [128, D], F32) for i in range(2)]
        PF = ctx.enter_context(nc.psum_tensor("l1a_pf", [128, 2048], F32))
        PT = [ctx.enter_context(nc.psum_tensor(f"l1a_pt{i}", [128, 128], BF16))
              for i in range(2)]
        ident, onesf = c.ident, c.onesf

        with nc.Block() as block:
            @block.sync
            def _(sync):
                bar = Bar(sync, sBAR)
                for i in range(8):
                    if i >= 1:
                        sync.wait_ge(sCV, i)
                    sync.dma_start(out=wstg[:, :],
                                   in_=w["L1_attn_W"][128 * i:128 * (i + 1), :]
                                   ).then_inc(sW, 16)
                sync.dma_start(out=brow[0:1, :],
                               in_=w["L1_attn_b"][None, :]).then_inc(sW, 16)
                sync.dma_start(out=grow[0:1, :],
                               in_=w["L1_attn_gamma"][None, :]).then_inc(sW, 16)
                sync.dma_start(out=berow[0:1, :],
                               in_=w["L1_attn_beta"][None, :]).then_inc(sW, 16)
                dk = w["L1_attn_decay"]
                if dk.ndim == 1:
                    dk = dk.unsqueeze(0)
                sync.dma_start(out=dsc[0:1, 0:1],
                               in_=dk[0:1, 0:1]).then_inc(sW, 16)
                sync.wait_ge(sCV, 8)
                sync.wait_ge(sTWE, 64)
                sync.wait_ge(sBC, 5)
                bar.hit()
                for s in range(NSEQ):
                    for t in range(TT):
                        g = TT * s + t
                        if g >= 2:
                            sync.wait_ge(sLN, g - 1)
                        sync.dma_start(out=xt[g % 2][:, :],
                                       in_=src[128 * g:128 * (g + 1), :]
                                       ).then_inc(sLD, 16)
                    sync.wait_ge(sLD, 16 * TT * (s + 1))
                    bar.hit()
                    bar.hit()
                    for qt in range(TT):
                        g = TT * s + qt
                        if g >= 2:
                            sync.wait_ge(sOUT, g - 1)
                        sync.dma_start(out=xt[qt % 2][:, :],
                                       in_=src[128 * g:128 * (g + 1), :]
                                       ).then_inc(sX2, 16)
                    sync.wait_ge(sX2, 16 * TT * (s + 1))
                    bar.hit()

            @block.vector
            def _(vector):
                bar = Bar(vector, sBAR)
                sv = Ser(vector, sSV)
                sv.do(vector.memset(eps[:, :], 1.0e-5))
                for i in range(8):
                    vector.wait_ge(sW, 16 * (i + 1))
                    if i >= 1:
                        vector.wait_ge(sTW, 8 * i)
                    ins = vector.tensor_copy(wbst[:, :], wstg[:, :])
                    sv.done(ins, sCV)
                    for k in range(8):
                        tr = 8 * i + k
                        vector.wait_ge(sTW, tr + 1)
                        ins = vector.tensor_copy(
                            WT[:, D * k + 128 * i:D * k + 128 * (i + 1)],
                            PT[tr % 2][:, :])
                        sv.done(ins, sTWE)
                vector.wait_ge(sBC, 2)
                sv.do(vector.tensor_copy(gam_bc[:, :], PF[:, 0:1024]))
                ins = vector.tensor_copy(bet_bc[:, :], PF[:, 1024:2048])
                sv.done(ins, sBC)
                vector.wait_ge(sBC, 4)
                sv.do(vector.tensor_copy(asc[:, 0:1], PF[:, 0:1]))
                sv.do(vector.memset(a_t[:, :], 1.0))
                ins = vector.tensor_scalar_mul(a_t[:, :], a_t[:, :], asc[:, 0:1])
                sv.done(ins, sBC)
                bar.hit()
                for s in range(NSEQ):
                    for t in range(TT):
                        g = TT * s + t
                        vector.wait_ge(sSTD, g + 1)
                        ln_v1(vector, sv, stat, eps)
                        vector.sem_inc(sVAR, 1)
                        vector.wait_ge(sSTD2, g + 1)
                        if g >= 2:
                            vector.wait_ge(sTE, 8 * (g - 1))
                        ln_v2(vector, sv, stat, xt[g % 2], lnb[g % 2])
                        vector.sem_inc(sLN, 1)
                        for k in range(8):
                            tr = 8 * g + k
                            vector.wait_ge(sTE, tr + 1)
                            ins = vector.tensor_copy(
                                ZT[:, 2048 * k + 128 * t:2048 * k + 128 * (t + 1)],
                                PT[tr % 2][:, :])
                            sv.done(ins, sTR)
                    bar.hit()
                    for ch in range(8):
                        ins = vector.tensor_tensor_scan(
                            ST[:, 2048 * ch:2048 * (ch + 1)],
                            a_t[:, :], ZT[:, 2048 * ch:2048 * (ch + 1)],
                            0.0, ALU.mult, ALU.add)
                        sv.done(ins, sSCN)
                    bar.hit()
                    for qt in range(TT):
                        g = TT * s + qt
                        vector.wait_ge(sSTDy, g + 1)
                        ln_v1(vector, sv, stat, eps)
                        vector.sem_inc(sVARy, 1)
                        vector.wait_ge(sSTD2y, g + 1)
                        ln_v2(vector, sv, stat, Yt[qt % 2], Ynb[qt % 2])
                        vector.sem_inc(sLNy, 1)
                        vector.wait_ge(sX2, 16 * TT * s + 16 * (qt + 1))
                        if g >= 2:
                            vector.wait_ge(sST, 16 * (g - 1))
                        ins = vector.tensor_tensor(outt[qt % 2][:, :],
                                                   Ynb[qt % 2][:, :], gam_bc[:, :],
                                                   ALU.mult)
                        sv.do(ins)
                        sv.do(vector.tensor_add(outt[qt % 2][:, :],
                                                outt[qt % 2][:, :], bet_bc[:, :]))
                        ins = vector.tensor_add(outt[qt % 2][:, :],
                                                outt[qt % 2][:, :], xt[qt % 2][:, :])
                        sv.done(ins, sOUT)
                    bar.hit()

            @block.scalar
            def _(scalar):
                bar = Bar(scalar, sBAR)
                ss = Ser(scalar, sSS)
                scalar.wait_ge(sW, 16 * 12)
                ins = scalar.activation(dsc[0:1, 1:2], dsc[0:1, 0:1], AF.Sigmoid)
                ss.done(ins, sBC)
                bar.hit()
                for s in range(NSEQ):
                    for t in range(TT):
                        g = TT * s + t
                        scalar.wait_ge(sLD, 16 * (g + 1))
                        ln_s1(scalar, ss, xt[g % 2], sq, stat)
                        scalar.sem_inc(sSTD, 1)
                        scalar.wait_ge(sVAR, g + 1)
                        ln_s2(scalar, ss, stat)
                        scalar.sem_inc(sSTD2, 1)
                    bar.hit()
                    bar.hit()
                    for qt in range(TT):
                        g = TT * s + qt
                        if g >= 2:
                            scalar.wait_ge(sLNy, g - 1)
                        scalar.wait_ge(sYP, 2 * g + 1)
                        ins = scalar.activation(Yt[qt % 2][:, 0:512], PF[:, 0:512],
                                                AF.Copy)
                        ss.done(ins, sYE)
                        scalar.wait_ge(sYP, 2 * g + 2)
                        ins = scalar.activation(Yt[qt % 2][:, 512:1024],
                                                PF[:, 512:1024], AF.Copy)
                        ss.done(ins, sYE)
                        ln_s1(scalar, ss, Yt[qt % 2], sq, stat)
                        scalar.sem_inc(sSTDy, 1)
                        scalar.wait_ge(sVARy, g + 1)
                        ln_s2(scalar, ss, stat)
                        scalar.sem_inc(sSTD2y, 1)
                    bar.hit()

            @block.tensor
            def _(tensor):
                bar = Bar(tensor, sBAR)
                for i in range(8):
                    tensor.wait_ge(sCV, i + 1)
                    for k in range(8):
                        tr = 8 * i + k
                        if tr >= 2:
                            tensor.wait_ge(sTWE, tr - 1)
                        ins = tensor.transpose(PT[tr % 2][:, :],
                                               wbst[:, 128 * k:128 * (k + 1)],
                                               ident[:, :])
                        ins.then_inc(sTW, 1)
                tensor.wait_ge(sW, 16 * 12)
                tensor.wait_ge(sBC, 1)
                tensor.matmul(PF[:, 0:512], onesf[0:1, :], grow[0:1, 0:512],
                              start=True, stop=True)
                tensor.matmul(PF[:, 512:1024], onesf[0:1, :], grow[0:1, 512:1024],
                              start=True, stop=True)
                tensor.matmul(PF[:, 1024:1536], onesf[0:1, :], berow[0:1, 0:512],
                              start=True, stop=True)
                ins = tensor.matmul(PF[:, 1536:2048], onesf[0:1, :],
                                    berow[0:1, 512:1024], start=True, stop=True)
                ins.then_inc(sBC, 1)
                tensor.wait_ge(sBC, 3)
                ins = tensor.matmul(PF[:, 0:1], onesf[0:1, :],
                                    dsc[0:1, 1:2], start=True, stop=True)
                ins.then_inc(sBC, 1)
                bar.hit(wait=(sBC, 4))
                for s in range(NSEQ):
                    for t in range(TT):
                        g = TT * s + t
                        tensor.wait_ge(sLN, g + 1)
                        for k in range(8):
                            tr = 8 * g + k
                            if tr >= 2:
                                tensor.wait_ge(sTR, tr - 1)
                            ins = tensor.transpose(PT[tr % 2][:, :],
                                                   lnb[g % 2][:, 128 * k:128 * (k + 1)],
                                                   ident[:, :])
                            ins.then_inc(sTE, 1)
                    bar.hit(wait=(sTR, 8 * TT * (s + 1)))
                    bar.hit()
                    for qt in range(TT):
                        g = TT * s + qt
                        tensor.wait_ge(sSCN, 8 * (s + 1))
                        for nb in range(2):
                            if g >= 1:
                                tensor.wait_ge(sYE, 2 * (g - 1) + nb + 1)
                            for ch in range(8):
                                tensor.matmul(
                                    PF[:, 512 * nb:512 * (nb + 1)],
                                    ST[:, 2048 * ch + 128 * qt:2048 * ch + 128 * (qt + 1)],
                                    WT[:, D * ch + 512 * nb:D * ch + 512 * (nb + 1)],
                                    start=(ch == 0), stop=False)
                            ins = tensor.matmul(
                                PF[:, 512 * nb:512 * (nb + 1)],
                                onesf[0:1, :], brow[0:1, 512 * nb:512 * (nb + 1)],
                                start=False, stop=True, skip_group_check=True)
                            ins.then_inc(sYP, 1)
                    bar.hit(wait=(sYP, 2 * TT * (s + 1)))

            @block.gpsimd
            def _(gpsimd):
                bar = Bar(gpsimd, sBAR)
                bar.hit()
                nst = 0
                for s in range(NSEQ):
                    bar.hit()
                    bar.hit()
                    for qt in range(TT):
                        g = TT * s + qt
                        gpsimd.wait_ge(sOUT, g + 1)
                        gpsimd.dma_start(out=dst[128 * g:128 * (g + 1), :],
                                         in_=outt[qt % 2][:, :]).then_inc(sST, 16)
                        nst += 1
                    gpsimd.wait_ge(sST, 16 * nst)
                    bar.hit()

    c.boundary()


def st_l1_ffn(c, src, dst):
    nc = c.nc
    w = c.dram
    sem = c.sem
    names = ["sLD", "sW", "sCV", "sSTD", "sVAR", "sSTD2", "sLN", "sTE", "sTR",
             "sRT", "sRB", "sTP2", "sHP", "sRL", "sTH", "sTHE", "sEO", "sAC",
             "sOUT", "sST", "sBAR", "sSV", "sSS", "sSG"]
    (sLD, sW, sCV, sSTD, sVAR, sSTD2, sLN, sTE, sTR, sRT, sRB, sTP2, sHP, sRL,
     sTH, sTHE, sEO, sAC, sOUT, sST, sBAR, sSV, sSS, sSG) = (
        sem(x) for x in names)

    GT = 8
    NG = NT // GT

    with ExitStack() as ctx:
        sb = lambda name, shape, dt: ctx.enter_context(nc.sbuf_tensor(name, shape, dt))
        xt = [sb(f"l1f_xt{i}", [128, D], F32) for i in range(2)]
        sq = sb("l1f_sq", [128, D], F32)
        stat = sb("l1f_st", [128, 8], F32)
        eps = sb("l1f_eps", [128, 1], F32)
        lnb = [sb(f"l1f_lnb{i}", [128, D], BF16) for i in range(2)]
        lnTg = sb("l1f_lnTg", [128, 8 * GT * 128], BF16)
        acc = sb("l1f_acc", [128, GT * D], F32)
        comb = sb("l1f_comb", [128, GT * 8], F32)
        rsc = sb("l1f_rsc", [128, 8], F32)
        tk = sb("l1f_tk", [128, 8], F32)
        tkm = sb("l1f_tkm", [128, 2], F32)
        w1sb = [sb(f"l1f_w1{i}", [128, 8 * D], BF16) for i in range(2)]
        w2sb = [sb(f"l1f_w2{i}", [128, 8 * D], BF16) for i in range(2)]
        b1row = [sb(f"l1f_b1{i}", [1, D], F32) for i in range(2)]
        b2row = [sb(f"l1f_b2{i}", [1, D], F32) for i in range(2)]
        b1all = sb("l1f_b1a", [8, D], F32)
        b2all = sb("l1f_b2a", [8, D], F32)
        gwsb = sb("l1f_gw", [128, 8 * 8], BF16)
        gwf = sb("l1f_gwf", [128, 8 * 8], F32)
        gbrow = sb("l1f_gb", [1, 8], F32)
        wstg = sb("l1f_wstg", [128, D], F32)
        hsb = [sb(f"l1f_h{i}", [128, D], BF16) for i in range(2)]
        hT = [sb(f"l1f_hT{i}", [128, 8 * 128], BF16) for i in range(2)]
        PF = ctx.enter_context(nc.psum_tensor("l1f_pf", [128, 2560], F32))
        PT = [ctx.enter_context(nc.psum_tensor(f"l1f_pt{i}", [128, 128], BF16))
              for i in range(2)]
        ident, onesf = c.ident, c.onesf

        def widx(grp, e):
            return grp * 8 + e

        with nc.Block() as block:
            @block.sync
            def _(sync):
                bar = Bar(sync, sBAR)
                for chd in range(8):
                    sync.dma_start(out=gwf[:, 8 * chd:8 * (chd + 1)],
                                   in_=w["L1_ffn_gate_w"][128 * chd:128 * (chd + 1), :]
                                   ).then_inc(sW, 16)
                sync.dma_start(out=gbrow[0:1, :],
                               in_=w["L1_ffn_gate_b"][None, :]).then_inc(sW, 16)
                sync.dma_start(out=b1all[:, :],
                               in_=w["L1_ffn_b1"][:, :]).then_inc(sW, 16)
                sync.dma_start(out=b2all[:, :],
                               in_=w["L1_ffn_b2"][:, :]).then_inc(sW, 16)
                sync.wait_ge(sW, 16 * 11)
                bar.hit()
                nw = [0]
                for grp in range(NG):
                    for tt in range(GT):
                        g = GT * grp + tt
                        if g >= 2:
                            sync.wait_ge(sLN, g - 1)
                        sync.dma_start(out=xt[g % 2][:, :],
                                       in_=src[128 * g:128 * (g + 1), :]
                                       ).then_inc(sLD, 16)
                    sync.wait_ge(sLD, 16 * GT * (grp + 1))
                    bar.hit()
                    for e in range(8):
                        for p in range(16):
                            if nw[0] >= 1:
                                sync.wait_ge(sCV, nw[0])
                            nm = "L1_ffn_w1" if p < 8 else "L1_ffn_w2"
                            r = p % 8
                            sync.dma_start(
                                out=wstg[:, :],
                                in_=w[nm][e, 128 * r:128 * (r + 1), :]
                                ).then_inc(sW, 16)
                            nw[0] += 1
                    sync.wait_ge(sCV, 16 * 8 * (grp + 1))
                    bar.hit()

            @block.vector
            def _(vector):
                bar = Bar(vector, sBAR)
                sv = Ser(vector, sSV)
                sv.do(vector.memset(eps[:, :], 1.0e-5))
                vector.wait_ge(sW, 16 * 8)
                sv.do(vector.tensor_copy(gwsb[:, :], gwf[:, :]))
                bar.hit()
                ncv = [0]
                for grp in range(NG):
                    for tt in range(GT):
                        g = GT * grp + tt
                        vector.wait_ge(sSTD, g + 1)
                        ln_v1(vector, sv, stat, eps)
                        vector.sem_inc(sVAR, 1)
                        if grp >= 1:
                            vector.wait_ge(sST, 16 * (GT * (grp - 1) + tt + 1))
                        sv.do(vector.tensor_copy(acc[:, D * tt:D * (tt + 1)],
                                                 xt[g % 2][:, :]))
                        vector.wait_ge(sSTD2, g + 1)
                        if g >= 2:
                            vector.wait_ge(sTE, 8 * (g - 1))
                        ln_v2(vector, sv, stat, xt[g % 2], lnb[g % 2])
                        vector.sem_inc(sLN, 1)
                        for k in range(8):
                            tr = 8 * g + k
                            vector.wait_ge(sTE, tr + 1)
                            ins = vector.tensor_copy(
                                lnTg[:, 1024 * k + 128 * tt:1024 * k + 128 * (tt + 1)],
                                PT[tr % 2][:, :])
                            sv.done(ins, sTR)
                        vector.wait_ge(sRT, g + 1)
                        sv.do(vector.tensor_copy(rsc[:, :], PF[:, 2048:2056]))
                        sv.do(vector.reduce_max(tkm[:, 0:1], rsc[:, :], axis=AX.X))
                        sv.do(vector.tensor_single_scalar(tk[:, :], rsc[:, :],
                                                          tkm[:, 0:1], ALU.is_equal))
                        sv.do(vector.scalar_tensor_tensor(tk[:, :], tk[:, :], -1e30,
                                                          rsc[:, :], ALU.mult, ALU.add))
                        sv.do(vector.reduce_max(tkm[:, 1:2], tk[:, :], axis=AX.X))
                        sv.do(vector.tensor_single_scalar(tk[:, :], rsc[:, :],
                                                          tkm[:, 1:2], ALU.is_ge))
                        ins = vector.tensor_mul(comb[:, 8 * tt:8 * (tt + 1)],
                                                tk[:, :], rsc[:, :])
                        sv.done(ins, sTP2)
                    bar.hit()
                    def conv_expert(e):
                        for p in range(16):
                            i2 = ncv[0]
                            vector.wait_ge(sW, 16 * 11 + 16 * (i2 + 1))
                            tgt = w1sb[e % 2] if p < 8 else w2sb[e % 2]
                            ins = vector.tensor_copy(
                                tgt[:, D * (p % 8):D * (p % 8 + 1)], wstg[:, :])
                            sv.done(ins, sCV)
                            ncv[0] += 1
                    conv_expert(0)
                    conv_expert(1)
                    for e in range(8):
                        i = widx(grp, e)
                        for tt in range(GT):
                            ie = i * GT + tt
                            vector.wait_ge(sEO, 2 * ie + 2)
                            for nb in range(2):
                                ins = vector.scalar_tensor_tensor(
                                    acc[:, D * tt + 512 * nb:D * tt + 512 * (nb + 1)],
                                    PF[:, 1024 + 512 * nb:1024 + 512 * (nb + 1)],
                                    comb[:, 8 * tt + e:8 * tt + e + 1],
                                    acc[:, D * tt + 512 * nb:D * tt + 512 * (nb + 1)],
                                    ALU.mult, ALU.add)
                                if nb == 1:
                                    sv.done(ins, sAC)
                                else:
                                    sv.do(ins)
                        if e + 2 < 8:
                            conv_expert(e + 2)
                    bar.hit()

            @block.scalar
            def _(scalar):
                bar = Bar(scalar, sBAR)
                ss = Ser(scalar, sSS)
                bar.hit()
                for grp in range(NG):
                    for tt in range(GT):
                        g = GT * grp + tt
                        scalar.wait_ge(sLD, 16 * (g + 1))
                        ln_s1(scalar, ss, xt[g % 2], sq, stat)
                        scalar.sem_inc(sSTD, 1)
                        scalar.wait_ge(sVAR, g + 1)
                        ln_s2(scalar, ss, stat)
                        scalar.sem_inc(sSTD2, 1)
                    bar.hit()
                    for e in range(8):
                        i = widx(grp, e)
                        for tt in range(GT):
                            ie = i * GT + tt
                            scalar.wait_ge(sHP, 2 * ie + 2)
                            if ie >= 2:
                                scalar.wait_ge(sTH, 8 * (ie - 1))
                            ins = scalar.activation(hsb[ie % 2][:, :],
                                                    PF[:, 0:1024], AF.Relu)
                            ss.done(ins, sRL)
                            if ie >= 2:
                                scalar.wait_ge(sEO, 2 * (ie - 1))
                            for k in range(8):
                                th = 8 * ie + k
                                scalar.wait_ge(sTH, th + 1)
                                ins = scalar.activation(
                                    hT[ie % 2][:, 128 * k:128 * (k + 1)],
                                    PT[th % 2][:, :], AF.Copy)
                                ss.done(ins, sTHE)
                    bar.hit()

            @block.tensor
            def _(tensor):
                bar = Bar(tensor, sBAR)
                bar.hit()
                for grp in range(NG):
                    for tt in range(GT):
                        g = GT * grp + tt
                        tensor.wait_ge(sLN, g + 1)
                        for k in range(8):
                            tr = 8 * g + k
                            if tr >= 2:
                                tensor.wait_ge(sTR, tr - 1)
                            ins = tensor.transpose(PT[tr % 2][:, :],
                                                   lnb[g % 2][:, 128 * k:128 * (k + 1)],
                                                   ident[:, :])
                            ins.then_inc(sTE, 1)
                        tensor.wait_ge(sTR, 8 * (g + 1))
                        if g >= 1:
                            tensor.wait_ge(sTP2, g)
                        for chd in range(8):
                            tensor.matmul(
                                PF[:, 2048:2056],
                                lnTg[:, 1024 * chd + 128 * tt:1024 * chd + 128 * (tt + 1)],
                                gwsb[:, 8 * chd:8 * (chd + 1)],
                                start=(chd == 0), stop=False)
                        ins = tensor.matmul(PF[:, 2048:2056], onesf[0:1, :],
                                            gbrow[0:1, :], start=False, stop=True,
                                            skip_group_check=True)
                        ins.then_inc(sRT, 1)
                    bar.hit(wait=(sRT, GT * (grp + 1)))
                    for e in range(8):
                        i = widx(grp, e)
                        tensor.wait_ge(sCV, 16 * (i + 1))
                        tensor.wait_ge(sRB, 32 * (i + 1))
                        for tt in range(GT):
                            ie = i * GT + tt
                            if ie >= 1:
                                tensor.wait_ge(sRL, ie)
                            for nb in range(2):
                                for chd in range(8):
                                    tensor.matmul(
                                        PF[:, 512 * nb:512 * (nb + 1)],
                                        lnTg[:, 1024 * chd + 128 * tt:
                                             1024 * chd + 128 * (tt + 1)],
                                        w1sb[e % 2][:, D * chd + 512 * nb:
                                                    D * chd + 512 * (nb + 1)],
                                        start=(chd == 0), stop=False)
                                ins = tensor.matmul(
                                    PF[:, 512 * nb:512 * (nb + 1)], onesf[0:1, :],
                                    b1row[e % 2][0:1, 512 * nb:512 * (nb + 1)],
                                    start=False, stop=True, skip_group_check=True)
                                ins.then_inc(sHP, 1)
                            tensor.wait_ge(sRL, ie + 1)
                            for k in range(8):
                                th = 8 * ie + k
                                if th >= 2:
                                    tensor.wait_ge(sTHE, th - 1)
                                ins = tensor.transpose(
                                    PT[th % 2][:, :],
                                    hsb[ie % 2][:, 128 * k:128 * (k + 1)],
                                    ident[:, :])
                                ins.then_inc(sTH, 1)
                            tensor.wait_ge(sTHE, 8 * (ie + 1))
                            if ie >= 1:
                                tensor.wait_ge(sAC, ie)
                            for nb in range(2):
                                for chd in range(8):
                                    tensor.matmul(
                                        PF[:, 1024 + 512 * nb:1024 + 512 * (nb + 1)],
                                        hT[ie % 2][:, 128 * chd:128 * (chd + 1)],
                                        w2sb[e % 2][:, D * chd + 512 * nb:
                                                    D * chd + 512 * (nb + 1)],
                                        start=(chd == 0), stop=False)
                                ins = tensor.matmul(
                                    PF[:, 1024 + 512 * nb:1024 + 512 * (nb + 1)],
                                    onesf[0:1, :],
                                    b2row[e % 2][0:1, 512 * nb:512 * (nb + 1)],
                                    start=False, stop=True, skip_group_check=True)
                                ins.then_inc(sEO, 1)
                    bar.hit(wait=(sEO, 2 * GT * 8 * (grp + 1)))

            @block.gpsimd
            def _(gpsimd):
                bar = Bar(gpsimd, sBAR)
                sg = Ser(gpsimd, sSG)
                bar.hit()
                for grp in range(NG):
                    bar.hit()
                    for e in range(8):
                        i = widx(grp, e)
                        if i >= 2:
                            gpsimd.wait_ge(sHP, 2 * GT * (i - 1))
                            gpsimd.wait_ge(sEO, 2 * GT * (i - 1))
                        gpsimd.dma_start(out=b1row[e % 2][0:1, :],
                                         in_=b1all[e:e + 1, :]).then_inc(sRB, 16)
                        gpsimd.dma_start(out=b2row[e % 2][0:1, :],
                                         in_=b2all[e:e + 1, :]).then_inc(sRB, 16)
                    gpsimd.wait_ge(sAC, GT * 8 * (grp + 1))
                    for tt in range(GT):
                        g = GT * grp + tt
                        gpsimd.dma_start(out=dst[128 * g:128 * (g + 1), :],
                                         in_=acc[:, D * tt:D * (tt + 1)]
                                         ).then_inc(sST, 16)
                    gpsimd.wait_ge(sST, 16 * GT * (grp + 1))
                    bar.hit()

    c.boundary()


def st_l2_attn(c, src, dst):
    nc = c.nc
    w = c.dram
    sem = c.sem
    names = ["sLD", "sW", "sCV", "sGM", "sBC", "sSTD", "sVAR", "sSTD2", "sLN",
             "sTE", "sTR", "sQP", "sQS", "sQT", "sQTE", "sVP", "sVE", "sSC",
             "sSIM", "sERF", "sATT", "sMSK", "sMSK2", "sTA", "sTAE", "sAV",
             "sOUT", "sST", "sX2", "sBAR", "sSV", "sSS", "sSG"]
    (sLD, sW, sCV, sGM, sBC, sSTD, sVAR, sSTD2, sLN, sTE, sTR, sQP, sQS, sQT,
     sQTE, sVP, sVE, sSC, sSIM, sERF, sATT, sMSK, sMSK2, sTA, sTAE, sAV, sOUT,
     sST, sX2, sBAR, sSV, sSS, sSG) = (sem(x) for x in names)

    SC_E = 0.7978845608028654
    B_E = -0.5641895835477563

    with ExitStack() as ctx:
        sb = lambda name, shape, dt: ctx.enter_context(nc.sbuf_tensor(name, shape, dt))
        xt = [sb(f"l2a_xt{i}", [128, D], F32) for i in range(2)]
        sq = sb("l2a_sq", [128, D], F32)
        stat = sb("l2a_st", [128, 8], F32)
        eps = sb("l2a_eps", [128, 1], F32)
        bec = sb("l2a_bec", [128, 1], F32)
        lnb = [sb(f"l2a_lnb{i}", [128, D], BF16) for i in range(2)]
        lnT = [sb(f"l2a_lnT{i}", [128, 8 * 128], BF16) for i in range(2)]
        qks = sb("l2a_qks", [128, 128], F32)
        qbf = sb("l2a_qbf", [128, 128], BF16)
        kbf = sb("l2a_kbf", [128, 128], BF16)
        qT = sb("l2a_qT", [128, S], BF16)
        kT = sb("l2a_kT", [128, S], BF16)
        Vt = sb("l2a_V", [128, 16 * D], BF16)
        gmat = sb("l2a_gmat", [128, 4224], F32)
        ohsb = sb("l2a_oh", [32, 4096], F32)
        rel = sb("l2a_rel", [32, 1], F32)
        wqk = sb("l2a_wqk", [128, 8 * 128], BF16)
        wv = sb("l2a_wv", [128, 8 * D], BF16)
        wstg = sb("l2a_wstg", [128, D], F32)
        bqkr = sb("l2a_bqk", [1, 128], F32)
        bvr = sb("l2a_bv", [1, D], F32)
        gab = sb("l2a_gab", [2, 128], F32)
        beb = sb("l2a_beb", [2, 128], F32)
        garow = [sb(f"l2a_gar{i}", [1, 128], F32) for i in range(2)]
        berow = [sb(f"l2a_ber{i}", [1, 128], F32) for i in range(2)]
        gbc = [sb(f"l2a_gbc{i}", [128, 128], F32) for i in range(2)]
        bbc = [sb(f"l2a_bbc{i}", [128, 128], F32) for i in range(2)]
        sim = sb("l2a_sim", [128, S], F32)
        erf = sb("l2a_erf", [128, S], F32)
        att = [sb(f"l2a_att{i}", [128, S], BF16) for i in range(2)]
        attT = [sb(f"l2a_attT{i}", [128, 128], BF16) for i in range(2)]
        outt = [sb(f"l2a_out{i}", [128, D], F32) for i in range(2)]
        PF = ctx.enter_context(nc.psum_tensor("l2a_pf", [128, 3072], F32))
        PT = [ctx.enter_context(nc.psum_tensor(f"l2a_pt{i}", [128, 128], BF16))
              for i in range(2)]
        ident, onesf, trilb = c.ident, c.onesf, c.tril

        with nc.Block() as block:
            @block.sync
            def _(sync):
                bar = Bar(sync, sBAR)
                for i in range(8):
                    if i >= 1:
                        sync.wait_ge(sCV, i)
                    sync.dma_start(out=wstg[:, :],
                                   in_=w["L2_attn_Wv"][128 * i:128 * (i + 1), :]
                                   ).then_inc(sW, 16)
                for i in range(8):
                    sync.wait_ge(sCV, 8 + i)
                    sync.dma_start(out=wstg[:, 0:128],
                                   in_=w["L2_attn_Wqk"][128 * i:128 * (i + 1), :]
                                   ).then_inc(sW, 16)
                sync.dma_start(out=bqkr[0:1, :],
                               in_=w["L2_attn_bqk"][None, :]).then_inc(sW, 16)
                sync.dma_start(out=bvr[0:1, :],
                               in_=w["L2_attn_bv"][None, :]).then_inc(sW, 16)
                sync.dma_start(out=gab[:, :],
                               in_=w["L2_attn_gamma"][:, :]).then_inc(sW, 16)
                sync.dma_start(out=beb[:, :],
                               in_=w["L2_attn_beta"][:, :]).then_inc(sW, 16)
                sync.dma_start(out=rel[:, :],
                               in_=w["L2_attn_rel_emb"][:, :]).then_inc(sW, 16)
                sync.dma_start(out=ohsb[:, :],
                               in_=c.dram["cohrev"][:, :]).then_inc(sW, 16)
                sync.wait_ge(sCV, 16)
                bar.hit()
                for s in range(NSEQ):
                    for t in range(TT):
                        g = TT * s + t
                        if g >= 2:
                            sync.wait_ge(sLN, g - 1)
                        sync.dma_start(out=xt[g % 2][:, :],
                                       in_=src[128 * g:128 * (g + 1), :]
                                       ).then_inc(sLD, 16)
                    sync.wait_ge(sLD, 16 * TT * (s + 1))
                    bar.hit()
                    for qt in range(TT):
                        g = TT * s + qt
                        if g >= 2:
                            sync.wait_ge(sOUT, g - 1)
                        sync.dma_start(out=xt[qt % 2][:, :],
                                       in_=src[128 * g:128 * (g + 1), :]
                                       ).then_inc(sX2, 16)
                    sync.wait_ge(sX2, 16 * TT * (s + 1))
                    bar.hit()

            @block.vector
            def _(vector):
                bar = Bar(vector, sBAR)
                sv = Ser(vector, sSV)
                sv.do(vector.memset(eps[:, :], 1.0e-5))
                sv.do(vector.memset(bec[:, :], B_E))
                for i in range(8):
                    vector.wait_ge(sW, 16 * (i + 1))
                    ins = vector.tensor_copy(wv[:, D * i:D * (i + 1)], wstg[:, :])
                    sv.done(ins, sCV)
                for i in range(8):
                    vector.wait_ge(sW, 16 * (9 + i))
                    ins = vector.tensor_copy(wqk[:, 128 * i:128 * (i + 1)],
                                             wstg[:, 0:128])
                    sv.done(ins, sCV)
                vector.wait_ge(sBC, 66)
                sv.do(vector.tensor_copy(gbc[0][:, :], PF[:, 0:128]))
                sv.do(vector.tensor_copy(gbc[1][:, :], PF[:, 128:256]))
                sv.do(vector.tensor_copy(bbc[0][:, :], PF[:, 256:384]))
                ins = vector.tensor_copy(bbc[1][:, :], PF[:, 384:512])
                sv.done(ins, sBC)
                vector.wait_ge(sGM, 8)
                ins = vector.memset(gmat[0:1, 4096:4224], 0.0)
                sv.done(ins, sGM)
                bar.hit()
                for s in range(NSEQ):
                    for t in range(TT):
                        g = TT * s + t
                        vector.wait_ge(sSTD, g + 1)
                        ln_v1(vector, sv, stat, eps)
                        vector.sem_inc(sVAR, 1)
                        vector.wait_ge(sSTD2, g + 1)
                        if g >= 2:
                            vector.wait_ge(sTE, 8 * (g - 1))
                        ln_v2(vector, sv, stat, xt[g % 2], lnb[g % 2])
                        vector.sem_inc(sLN, 1)
                        for k in range(8):
                            tr = 8 * g + k
                            vector.wait_ge(sTE, tr + 1)
                            ins = vector.tensor_copy(
                                lnT[g % 2][:, 128 * k:128 * (k + 1)],
                                PT[tr % 2][:, :])
                            sv.done(ins, sTR)
                        vector.wait_ge(sQS, g + 1)
                        if g >= 1:
                            vector.wait_ge(sQT, 2 * g)
                        sv.do(vector.tensor_tensor(qbf[:, :], qks[:, :],
                                                   gbc[0][:, :], ALU.mult))
                        sv.do(vector.tensor_add(qbf[:, :], qbf[:, :], bbc[0][:, :]))
                        sv.do(vector.tensor_tensor(kbf[:, :], qks[:, :],
                                                   gbc[1][:, :], ALU.mult))
                        ins = vector.tensor_add(kbf[:, :], kbf[:, :], bbc[1][:, :])
                        sv.done(ins, sATT)
                        for j in range(2):
                            tq = 2 * g + j
                            vector.wait_ge(sQT, tq + 1)
                            dstT = qT if j == 0 else kT
                            ins = vector.tensor_copy(
                                dstT[:, 128 * t:128 * (t + 1)], PT[tq % 2][:, :])
                            sv.done(ins, sQTE)
                    bar.hit()
                    for qt in range(TT):
                        g = TT * s + qt
                        L = 128 * (qt + 1)
                        c0 = 2047 - 128 * qt
                        vector.wait_ge(sSC, g + 1)
                        if g >= 1:
                            vector.wait_ge(sERF, g)
                        ins = vector.scalar_tensor_tensor(
                            sim[:, 0:L], PF[:, 0:L], 1.0 / 2048.0,
                            gmat[:, c0:c0 + L], ALU.mult, ALU.add)
                        sv.done(ins, sSIM)
                        vector.wait_ge(sERF, g + 1)
                        if g >= 2:
                            vector.wait_ge(sTA, tau2_l2(g - 1))
                        ins = vector.tensor_scalar(att[qt % 2][:, 0:L], erf[:, 0:L],
                                                   0.5, 0.5, ALU.mult, ALU.add)
                        sv.done(ins, sMSK)
                        t2 = tau2_l2(g) - (qt + 1)
                        for kt in range(qt + 1):
                            vector.wait_ge(sTA, t2 + 1)
                            ins = vector.tensor_copy(attT[t2 % 2][:, :],
                                                     PT[t2 % 2][:, :])
                            sv.done(ins, sTAE)
                            t2 += 1
                        vector.wait_ge(sAV, 2 * (g + 1))
                        vector.wait_ge(sX2, 16 * TT * s + 16 * (qt + 1))
                        if g >= 2:
                            vector.wait_ge(sST, 16 * (g - 1))
                        for nb in range(2):
                            ins = vector.scalar_tensor_tensor(
                                outt[qt % 2][:, 512 * nb:512 * (nb + 1)],
                                PF[:, 2048 + 512 * nb:2048 + 512 * (nb + 1)],
                                1.0, xt[qt % 2][:, 512 * nb:512 * (nb + 1)],
                                ALU.mult, ALU.add)
                            if nb == 1:
                                sv.done(ins, sOUT)
                            else:
                                sv.do(ins)
                    bar.hit()

            @block.scalar
            def _(scalar):
                bar = Bar(scalar, sBAR)
                ss = Ser(scalar, sSS)
                for cpc in range(8):
                    scalar.wait_ge(sGM if False else sQP, cpc + 1)
                    ins = scalar.activation(gmat[0:1, 512 * cpc:512 * (cpc + 1)],
                                            PF[:, 512:1024][0:1, 0:512],
                                            AF.Copy, scale=float(np.sqrt(128.0)))
                    ss.done(ins, sGM)
                bar.hit()
                for s in range(NSEQ):
                    for t in range(TT):
                        g = TT * s + t
                        scalar.wait_ge(sLD, 16 * (g + 1))
                        ln_s1(scalar, ss, xt[g % 2], sq, stat)
                        scalar.sem_inc(sSTD, 1)
                        scalar.wait_ge(sVAR, g + 1)
                        ln_s2(scalar, ss, stat)
                        scalar.sem_inc(sSTD2, 1)
                        if g >= 1:
                            scalar.wait_ge(sATT, g)
                        scalar.wait_ge(sQP, 8 + g + 1)
                        ins = scalar.activation(qks[:, :], PF[:, 0:128], AF.Silu)
                        ss.done(ins, sQS)
                        for nb in range(2):
                            scalar.wait_ge(sVP, 2 * g + nb + 1)
                            ins = scalar.activation(
                                Vt[:, D * t + 512 * nb:D * t + 512 * (nb + 1)],
                                PF[:, 2048 + 512 * nb:2048 + 512 * (nb + 1)],
                                AF.Silu)
                            ss.done(ins, sVE)
                    bar.hit()
                    for qt in range(TT):
                        g = TT * s + qt
                        L = 128 * (qt + 1)
                        scalar.wait_ge(sSIM, g + 1)
                        if g >= 1:
                            scalar.wait_ge(sMSK, g)
                        ins = scalar.activation(erf[:, 0:L], sim[:, 0:L], AF.Erf,
                                                scale=SC_E, bias=bec[:, 0:1])
                        ss.done(ins, sERF)
                    bar.hit()

            @block.tensor
            def _(tensor):
                bar = Bar(tensor, sBAR)
                tensor.wait_ge(sW, 16 * 22)
                for cpc in range(8):
                    if cpc >= 1:
                        tensor.wait_ge(sGM, cpc)
                    ins = tensor.matmul(PF[0:1, 512:1024][0:1, 0:512],
                                        rel[:, 0:1],
                                        ohsb[:, 512 * cpc:512 * (cpc + 1)],
                                        start=True, stop=True)
                    ins.then_inc(sQP, 1)
                tensor.wait_ge(sBC, 64)
                tensor.matmul(PF[:, 0:128], onesf[0:1, :], garow[0][0:1, :],
                              start=True, stop=True)
                tensor.matmul(PF[:, 128:256], onesf[0:1, :], garow[1][0:1, :],
                              start=True, stop=True)
                tensor.matmul(PF[:, 256:384], onesf[0:1, :], berow[0][0:1, :],
                              start=True, stop=True)
                ins = tensor.matmul(PF[:, 384:512], onesf[0:1, :], berow[1][0:1, :],
                                    start=True, stop=True)
                ins.then_inc(sBC, 2)
                tensor.wait_ge(sGM, 8)
                bar.hit(wait=(sBC, 66))
                for s in range(NSEQ):
                    for t in range(TT):
                        g = TT * s + t
                        tensor.wait_ge(sLN, g + 1)
                        if g >= 1:
                            tensor.wait_ge(sQTE, 2 * g)
                        for k in range(8):
                            tr = 8 * g + k
                            if tr >= 2:
                                tensor.wait_ge(sTR, tr - 1)
                            ins = tensor.transpose(PT[tr % 2][:, :],
                                                   lnb[g % 2][:, 128 * k:128 * (k + 1)],
                                                   ident[:, :])
                            ins.then_inc(sTE, 1)
                        tensor.wait_ge(sTR, 8 * (g + 1))
                        if g >= 1:
                            tensor.wait_ge(sQS, g)
                        for ch in range(8):
                            tensor.matmul(PF[:, 0:128],
                                          lnT[g % 2][:, 128 * ch:128 * (ch + 1)],
                                          wqk[:, 128 * ch:128 * (ch + 1)],
                                          start=(ch == 0), stop=False)
                        ins = tensor.matmul(PF[:, 0:128], onesf[0:1, :],
                                            bqkr[0:1, :], start=False, stop=True,
                                            skip_group_check=True)
                        ins.then_inc(sQP, 1)
                        for nb in range(2):
                            if g >= 1:
                                tensor.wait_ge(sVE, 2 * (g - 1) + nb + 1)
                            for ch in range(8):
                                tensor.matmul(
                                    PF[:, 2048 + 512 * nb:2048 + 512 * (nb + 1)],
                                    lnT[g % 2][:, 128 * ch:128 * (ch + 1)],
                                    wv[:, D * ch + 512 * nb:D * ch + 512 * (nb + 1)],
                                    start=(ch == 0), stop=False)
                            ins = tensor.matmul(
                                PF[:, 2048 + 512 * nb:2048 + 512 * (nb + 1)],
                                onesf[0:1, :], bvr[0:1, 512 * nb:512 * (nb + 1)],
                                start=False, stop=True, skip_group_check=True)
                            ins.then_inc(sVP, 1)
                        tensor.wait_ge(sATT, g + 1)
                        for j in range(2):
                            tq = 2 * g + j
                            if tq >= 2:
                                tensor.wait_ge(sQTE, tq - 1)
                            ins = tensor.transpose(PT[tq % 2][:, :],
                                                   (qbf if j == 0 else kbf)[:, :],
                                                   ident[:, :])
                            ins.then_inc(sQT, 1)
                    bar.hit(wait=(sQTE, 2 * TT * (s + 1)))
                    for qt in range(TT):
                        g = TT * s + qt
                        L = 128 * (qt + 1)
                        if g >= 1:
                            tensor.wait_ge(sSIM, g)
                        nkb = (L + 511) // 512
                        for kb in range(nkb):
                            nn = min(512, L - 512 * kb)
                            ins = tensor.matmul(
                                PF[:, 512 * kb:512 * kb + nn],
                                qT[:, 128 * qt:128 * (qt + 1)],
                                kT[:, 512 * kb:512 * kb + nn],
                                start=True, stop=True)
                        ins.then_inc(sSC, 1)
                        tensor.wait_ge(sMSK, g + 1)
                        t2 = tau2_l2(g) - (qt + 1)
                        if g >= 1:
                            tensor.wait_ge(sOUT, g)
                        for kt in range(qt + 1):
                            if kt == qt:
                                tensor.wait_ge(sMSK2, g + 1)
                            if t2 >= 2:
                                tensor.wait_ge(sTAE, t2 - 1)
                            ins = tensor.transpose(
                                PT[t2 % 2][:, :],
                                att[qt % 2][:, 128 * kt:128 * (kt + 1)],
                                ident[:, :])
                            ins.then_inc(sTA, 1)
                            tensor.wait_ge(sTAE, t2 + 1)
                            for nb in range(2):
                                ins = tensor.matmul(
                                    PF[:, 2048 + 512 * nb:2048 + 512 * (nb + 1)],
                                    attT[t2 % 2][:, :],
                                    Vt[:, D * kt + 512 * nb:D * kt + 512 * (nb + 1)],
                                    start=(kt == 0), stop=(kt == qt))
                            t2 += 1
                        ins.then_inc(sAV, 2)
                    bar.hit(wait=(sAV, 2 * TT * (s + 1)))

            @block.gpsimd
            def _(gpsimd):
                bar = Bar(gpsimd, sBAR)
                sg = Ser(gpsimd, sSG)
                gpsimd.wait_ge(sW, 16 * 22)
                gpsimd.dma_start(out=garow[0][0:1, :], in_=gab[0:1, :]).then_inc(sBC, 16)
                gpsimd.dma_start(out=garow[1][0:1, :], in_=gab[1:2, :]).then_inc(sBC, 16)
                gpsimd.dma_start(out=berow[0][0:1, :], in_=beb[0:1, :]).then_inc(sBC, 16)
                gpsimd.dma_start(out=berow[1][0:1, :], in_=beb[1:2, :]).then_inc(sBC, 16)
                gpsimd.wait_ge(sGM, 9)
                for p in range(1, 128):
                    gpsimd.dma_start(out=gmat[p:p + 1, p:4224],
                                     in_=gmat[0:1, 0:4224 - p]).then_inc(sGM, 16)
                gpsimd.wait_ge(sGM, 9 + 127 * 16)
                bar.hit()
                nst = 0
                for s in range(NSEQ):
                    bar.hit()
                    for qt in range(TT):
                        g = TT * s + qt
                        L = 128 * (qt + 1)
                        gpsimd.wait_ge(sMSK, g + 1)
                        ins = gpsimd.tensor_mul(att[qt % 2][:, 128 * qt:128 * (qt + 1)],
                                                att[qt % 2][:, 128 * qt:128 * (qt + 1)],
                                                trilb[:, :])
                        sg.done(ins, sMSK2)
                        gpsimd.wait_ge(sOUT, g + 1)
                        gpsimd.dma_start(out=dst[128 * g:128 * (g + 1), :],
                                         in_=outt[qt % 2][:, :]).then_inc(sST, 16)
                        nst += 1
                    gpsimd.wait_ge(sST, 16 * nst)
                    bar.hit()

    c.boundary()


def tau2_l2(g):
    n = 0
    for gg in range(g + 1):
        n += (gg % TT) + 1
    return n


def st_l2_ffn(c, src, dst):
    nc = c.nc
    w = c.dram
    sem = c.sem
    names = ["sLD", "sW", "sCV", "sSTD", "sVAR", "sSTD2", "sLN", "sTE", "sTR",
             "sXF", "sRT", "sRE", "sTP2", "sSUB", "sCV8", "sBR", "sBR2", "sH1",
             "sH1E", "sT1", "sT1E", "sGP", "sGS", "sH2", "sT2", "sT2E", "sDN",
             "sAC", "sRM1", "sRM2", "sRM3", "sEPI", "sLN2", "sTE2", "sTR2",
             "sOUT", "sST", "sX2", "sBAR", "sSV", "sSS", "sSG", "sBR3"]
    (sLD, sW, sCV, sSTD, sVAR, sSTD2, sLN, sTE, sTR, sXF, sRT, sRE, sTP2,
     sSUB, sCV8, sBR, sBR2, sH1, sH1E, sT1, sT1E, sGP, sGS, sH2, sT2, sT2E,
     sDN, sAC, sRM1, sRM2, sRM3, sEPI, sLN2, sTE2, sTR2, sOUT, sST, sX2,
     sBAR, sSV, sSS, sSG, sBR3) = (sem(x) for x in names)

    GT = 4
    NG = NT // GT
    NE = 8

    with ExitStack() as ctx:
        sb = lambda name, shape, dt: ctx.enter_context(nc.sbuf_tensor(name, shape, dt))
        xt = [sb(f"l2f_xt{i}", [128, D], F32) for i in range(2)]
        stat = sb("l2f_st", [128, 8], F32)
        eps = sb("l2f_eps", [128, 1], F32)
        eps6 = sb("l2f_eps6", [128, 1], F32)
        neg6 = sb("l2f_neg6", [128, 6], F32)
        lnb = [sb(f"l2f_lnb{i}", [128, D], BF16) for i in range(2)]
        XF = sb("l2f_XF", [128, GT * D], F32)
        XT = sb("l2f_XT", [128, 8 * GT * 128], BF16)
        h1g = sb("l2f_h1g", [128, GT * D], BF16)
        h1T = sb("l2f_h1T", [128, 8 * GT * 128], BF16)
        h2s = [sb(f"l2f_h2s{i}", [128, D], BF16) for i in range(2)]
        h2T = h1T
        gs = sb("l2f_gs", [128, D], BF16)
        wsub = [sb(f"l2f_ws{i}", [128, 8 * D], BF16) for i in range(2)]
        wstg = sb("l2f_wstg", [128, D], F32)
        shWup = sb("l2f_shup", [128, 8 * 2048], BF16)
        shWg = sb("l2f_shg", [128, 8 * D], BF16)
        shWdn = sb("l2f_shdn", [128, 8 * D], BF16)
        shbup = sb("l2f_shbup", [1, 2048], BF16)
        shbg = sb("l2f_shbg", [1, D], BF16)
        shbdn = sb("l2f_shbdn", [1, D], BF16)
        brup = [sb(f"l2f_brup{i}", [1, D], BF16) for i in range(2)]
        brg = [sb(f"l2f_brg{i}", [1, D], BF16) for i in range(2)]
        brdn = [sb(f"l2f_brdn{i}", [1, D], BF16) for i in range(2)]
        brstg = sb("l2f_brstg", [1, D], F32)
        rwb = sb("l2f_rwb", [128, 96], BF16)
        rwf = sb("l2f_rwf", [128, 96], F32)
        rbrow = [sb(f"l2f_rbr{i}", [1, 6], F32) for i in range(2)]
        gbrow = [sb(f"l2f_gbr{i}", [1, 6], F32) for i in range(2)]
        nwstg = sb("l2f_nwstg", [1, D], F32)
        gb_bc = [sb(f"l2f_gbbc{i}", [128, 6], F32) for i in range(2)]
        nw_bc = [sb(f"l2f_nwbc{i}", [128, D], BF16) for i in range(2)]
        rsc = sb("l2f_rsc", [128, 6], F32)
        bsd = sb("l2f_bsd", [128, 6], F32)
        tk = sb("l2f_tk", [128, 6], F32)
        tkm = sb("l2f_tkm", [128, 4], F32)
        CMB = sb("l2f_cmb", [128, GT * 6], F32)
        PF = ctx.enter_context(nc.psum_tensor("l2f_pf", [128, 2560], F32))
        PT = [ctx.enter_context(nc.psum_tensor(f"l2f_pt{i}", [128, 128], BF16))
              for i in range(2)]
        ident, onesf, onesb = c.ident, c.onesf, c.onesb
        EXPW = ["L2_ffn_exp_Wup", "L2_ffn_exp_Wg", "L2_ffn_exp_Wdown"]

        def xts(ch_, tt):
            return slice(512 * ch_ + 128 * tt, 512 * ch_ + 128 * (tt + 1))

        with nc.Block() as block:
            @block.sync
            def _(sync):
                bar = Bar(sync, sBAR)
                sync.dma_start(out=rwf[:, 0:48].rearrange("p (c e) -> p c e", c=8),
                               in_=w["L2_ffn_router_W"][0].rearrange(
                                   "(c p) e -> p c e", p=128)).then_inc(sW, 16)
                sync.dma_start(out=rwf[:, 48:96].rearrange("p (c e) -> p c e", c=8),
                               in_=w["L2_ffn_router_W"][1].rearrange(
                                   "(c p) e -> p c e", p=128)).then_inc(sW, 16)
                np_ = [0]

                def ldw(dram_slice, row=False):
                    if np_[0] >= 1:
                        sync.wait_ge(sCV, np_[0])
                    out = wstg[0:1, :dram_slice.shape[-1]] if row else wstg[:, :]
                    sync.dma_start(out=out, in_=dram_slice).then_inc(sW, 16)
                    np_[0] += 1
                for i in range(8):
                    ldw(w["L2_ffn_share_Wup"][128 * i:128 * (i + 1), 0:1024])
                for i in range(8):
                    ldw(w["L2_ffn_share_Wup"][128 * i:128 * (i + 1), 1024:2048])
                for i in range(8):
                    ldw(w["L2_ffn_share_Wg"][128 * i:128 * (i + 1), :])
                for i in range(8):
                    ldw(w["L2_ffn_share_Wdown"][128 * i:128 * (i + 1), :])
                ldw(w["L2_ffn_share_bup"][None, 0:1024], row=True)
                ldw(w["L2_ffn_share_bup"][None, 1024:2048], row=True)
                ldw(w["L2_ffn_share_bg"][None, :], row=True)
                ldw(w["L2_ffn_share_bdown"][None, :], row=True)
                sync.wait_ge(sCV, 36)
                bar.hit()
                nsub = [0]
                for grp in range(NG):
                    for tt in range(GT):
                        g = GT * grp + tt
                        if g >= 2:
                            sync.wait_ge(sXF, g - 1)
                        sync.dma_start(out=xt[g % 2][:, :],
                                       in_=src[128 * g:128 * (g + 1), :]
                                       ).then_inc(sLD, 16)
                    sync.wait_ge(sLD, 16 * GT * (grp + 1))
                    bar.hit()
                    for chk in range(2):
                        for e in range(6):
                            for smi in range(3):
                                m = nsub[0]
                                for p in range(8):
                                    if m >= 2 and p == 0:
                                        sync.wait_ge(sSUB, m - 1)
                                    if 8 * m + p >= 1:
                                        sync.wait_ge(sCV8, 8 * m + p)
                                    sync.dma_start(
                                        out=wstg[:, :],
                                        in_=w[EXPW[smi]][e, 128 * p:128 * (p + 1), :]
                                        ).then_inc(sRE, 16)
                                nsub[0] += 1
                        bar.hit()
                        if chk == 1:
                            for tt in range(GT):
                                g = GT * grp + tt
                                if g >= 2:
                                    sync.wait_ge(sOUT, g - 1)
                                sync.dma_start(out=xt[tt % 2][:, :],
                                               in_=src[128 * g:128 * (g + 1), :]
                                               ).then_inc(sX2, 16)
                            sync.wait_ge(sX2, 16 * GT * (grp + 1))
                        bar.hit()

            @block.vector
            def _(vector):
                bar = Bar(vector, sBAR)
                sv = Ser(vector, sSV)
                sv.do(vector.memset(eps[:, :], 1.0e-5))
                sv.do(vector.memset(eps6[:, :], 1.0e-6))
                sv.do(vector.memset(neg6[:, :], NEG))
                ncv = [0]

                def cvt(tgt, row=False):
                    vector.wait_ge(sW, 16 * (ncv[0] + 3))
                    src_ = wstg[0:1, :tgt.shape[-1]] if row else wstg[:, :]
                    ins = vector.tensor_copy(tgt, src_)
                    sv.done(ins, sCV)
                    ncv[0] += 1
                for i in range(8):
                    cvt(shWup[:, 2048 * i:2048 * i + 1024])
                for i in range(8):
                    cvt(shWup[:, 2048 * i + 1024:2048 * (i + 1)])
                for i in range(8):
                    cvt(shWg[:, D * i:D * (i + 1)])
                for i in range(8):
                    cvt(shWdn[:, D * i:D * (i + 1)])
                cvt(shbup[0:1, 0:1024], row=True)
                cvt(shbup[0:1, 1024:2048], row=True)
                cvt(shbg[0:1, :], row=True)
                cvt(shbdn[0:1, :], row=True)
                vector.wait_ge(sW, 16 * 2)
                sv.do(vector.tensor_copy(rwb[:, :], rwf[:, :]))
                vector.wait_ge(sBR, 98)
                sv.do(vector.tensor_copy(gb_bc[0][:, :], PF[:, 2048:2054]))
                sv.do(vector.tensor_copy(gb_bc[1][:, :], PF[:, 2054:2060]))
                sv.do(vector.tensor_copy(nw_bc[0][:, :], PF[:, 0:1024]))
                sv.do(vector.tensor_copy(nw_bc[1][:, :], PF[:, 1024:2048]))
                bar.hit()
                for grp in range(NG):
                    for tt in range(GT):
                        g = GT * grp + tt
                        vector.wait_ge(sSTD, g + 1)
                        ln_v1(vector, sv, stat, eps)
                        vector.sem_inc(sVAR, 1)
                        vector.wait_ge(sSTD2, g + 1)
                        if g >= 2:
                            vector.wait_ge(sTE, 8 * (g - 1))
                        ln_v2(vector, sv, stat, xt[g % 2], lnb[g % 2])
                        vector.sem_inc(sLN, 1)
                        ins = vector.tensor_scalar(XF[:, D * tt:D * (tt + 1)],
                                                   xt[g % 2][:, :], stat[:, 5:6],
                                                   stat[:, 7:8], ALU.mult,
                                                   ALU.subtract)
                        sv.done(ins, sXF)
                        for k in range(8):
                            tr = 8 * g + k
                            vector.wait_ge(sTE, tr + 1)
                            ins = vector.tensor_copy(XT[:, xts(k, tt)],
                                                     PT[tr % 2][:, :])
                            sv.done(ins, sTR)
                    bar.hit()
                    for chk in range(2):
                        cgi = 2 * grp + chk
                        for tt in range(GT):
                            rt = cgi * GT + tt
                            vector.wait_ge(sRT, rt + 1)
                            sv.do(vector.tensor_copy(rsc[:, :], PF[:, 2048:2054]))
                            sv.do(vector.tensor_add(bsd[:, :], rsc[:, :],
                                                    gb_bc[chk][:, :]))
                            sv.do(vector.reduce_max(tkm[:, 0:1], bsd[:, :],
                                                    axis=AX.X))
                            sv.do(vector.tensor_single_scalar(
                                tk[:, :], bsd[:, :], tkm[:, 0:1], ALU.is_equal))
                            sv.do(vector.scalar_tensor_tensor(
                                tk[:, :], tk[:, :], -1e30, bsd[:, :],
                                ALU.mult, ALU.add))
                            sv.do(vector.reduce_max(tkm[:, 1:2], tk[:, :],
                                                    axis=AX.X))
                            sv.do(vector.tensor_single_scalar(
                                tk[:, :], bsd[:, :], tkm[:, 1:2], ALU.is_ge))
                            sv.do(vector.tensor_scalar_add(bsd[:, :], rsc[:, :],
                                                           1.0e4))
                            sv.do(vector.tensor_mul(tk[:, :], tk[:, :], bsd[:, :]))
                            sv.do(vector.tensor_scalar_add(tk[:, :], tk[:, :],
                                                           -1.0e4))
                            sv.do(vector.reduce_max(tkm[:, 2:3], tk[:, :],
                                                    axis=AX.X))
                            ins = vector.tensor_scalar_mul(tkm[:, 3:4],
                                                           tkm[:, 2:3], -1.0)
                            sv.done(ins, sTP2)
                            vector.wait_ge(sRM1, rt + 1)
                            sv.do(vector.reciprocal(tkm[:, 2:3], tkm[:, 0:1]))
                            ins = vector.tensor_scalar_mul(
                                CMB[:, 6 * tt:6 * (tt + 1)], bsd[:, :],
                                tkm[:, 2:3])
                            sv.done(ins, sRM2)
                        for e in range(NE):
                            ieb = cgi * NE + e
                            for tt in range(GT):
                                q = ieb * GT + tt
                                for k in range(8):
                                    t1 = 8 * q + k
                                    vector.wait_ge(sT1, t1 + 1)
                                    ins = vector.tensor_copy(
                                        h1T[:, xts(k, tt)], PT[t1 % 2][:, :])
                                    sv.done(ins, sT1E)
                            for tt in range(GT):
                                q = ieb * GT + tt
                                vector.wait_ge(sGS, q + 1)
                                if q >= 2:
                                    vector.wait_ge(sT2, 8 * (q - 1))
                                ins = vector.tensor_mul(
                                    h2s[q % 2][:, :], gs[:, :],
                                    h1g[:, D * tt:D * (tt + 1)])
                                sv.done(ins, sH2)
                                for k in range(8):
                                    t2_ = 8 * q + k
                                    vector.wait_ge(sT2, t2_ + 1)
                                    ins = vector.tensor_copy(
                                        h2T[:, xts(k, tt)], PT[t2_ % 2][:, :])
                                    sv.done(ins, sT2E)
                            for tt in range(GT):
                                q = ieb * GT + tt
                                vector.wait_ge(sDN, q + 1)
                                vector.wait_ge(sRM2, cgi * GT + tt + 1)
                                for nb in range(2):
                                    xsl = XF[:, D * tt + 512 * nb:
                                             D * tt + 512 * (nb + 1)]
                                    psl = PF[:, 1024 + 512 * nb:
                                             1024 + 512 * (nb + 1)]
                                    if e < 6:
                                        ins = vector.scalar_tensor_tensor(
                                            xsl, psl,
                                            CMB[:, 6 * tt + e:6 * tt + e + 1],
                                            xsl, ALU.mult, ALU.add)
                                    else:
                                        ins = vector.scalar_tensor_tensor(
                                            xsl, psl, 1.0, xsl, ALU.mult, ALU.add)
                                    if nb == 1:
                                        sv.done(ins, sAC)
                                    else:
                                        sv.do(ins)
                        bar.hit()
                        for tt in range(GT):
                            g = GT * grp + tt
                            et = cgi * GT + tt
                            vector.wait_ge(sRM3, et + 1)
                            sv.do(vector.reciprocal(stat[:, 5:6], stat[:, 4:5]))
                            ins = vector.tensor_scalar_mul(
                                XF[:, D * tt:D * (tt + 1)],
                                XF[:, D * tt:D * (tt + 1)], stat[:, 5:6])
                            sv.do(ins)
                            ins = vector.tensor_mul(XF[:, D * tt:D * (tt + 1)],
                                                    XF[:, D * tt:D * (tt + 1)],
                                                    nw_bc[chk][:, :])
                            sv.done(ins, sEPI)
                            if chk == 0:
                                wcnt = grp * GT + tt
                                if wcnt >= 2:
                                    vector.wait_ge(sTE2, 8 * (wcnt - 1))
                                ins = vector.tensor_copy(lnb[tt % 2][:, :],
                                                         XF[:, D * tt:D * (tt + 1)])
                                sv.done(ins, sLN2)
                                for k in range(8):
                                    tr2 = 8 * wcnt + k
                                    vector.wait_ge(sTE2, tr2 + 1)
                                    ins = vector.tensor_copy(XT[:, xts(k, tt)],
                                                             PT[tr2 % 2][:, :])
                                    sv.done(ins, sTR2)
                            else:
                                vector.wait_ge(sX2, 16 * GT * grp + 16 * (tt + 1))
                                ins = vector.tensor_add(XF[:, D * tt:D * (tt + 1)],
                                                        XF[:, D * tt:D * (tt + 1)],
                                                        xt[tt % 2][:, :])
                                sv.done(ins, sOUT)
                        bar.hit()

            @block.scalar
            def _(scalar):
                bar = Bar(scalar, sBAR)
                ss = Ser(scalar, sSS)
                bar.hit()
                for grp in range(NG):
                    for tt in range(GT):
                        g = GT * grp + tt
                        scalar.wait_ge(sLD, 16 * (g + 1))
                        if g >= 2:
                            scalar.wait_ge(sXF, g - 1)
                        ln_s1(scalar, ss, xt[g % 2], XF[:, D * tt:D * (tt + 1)], stat)
                        scalar.sem_inc(sSTD, 1)
                        scalar.wait_ge(sVAR, g + 1)
                        ln_s2(scalar, ss, stat)
                        scalar.sem_inc(sSTD2, 1)
                    bar.hit()
                    for chk in range(2):
                        cgi = 2 * grp + chk
                        for tt in range(GT):
                            rt = cgi * GT + tt
                            scalar.wait_ge(sTP2, rt + 1)
                            ins = scalar.activation(bsd[:, :], tk[:, :], AF.Exp,
                                                    bias=tkm[:, 3:4],
                                                    accum_out=tkm[:, 0:1])
                            ss.done(ins, sRM1)
                        for e in range(NE):
                            ieb = cgi * NE + e
                            for tt in range(GT):
                                q = ieb * GT + tt
                                scalar.wait_ge(sH1, q + 1)
                                scalar.wait_ge(sH2, GT * ieb)
                                ins = scalar.activation(
                                    h1g[:, D * tt:D * (tt + 1)],
                                    PF[:, 0:1024], AF.Copy)
                                ss.done(ins, sH1E)
                            for tt in range(GT):
                                q = ieb * GT + tt
                                scalar.wait_ge(sGP, q + 1)
                                scalar.wait_ge(sH2, q)
                                ins = scalar.activation(gs[:, :], PF[:, 0:1024],
                                                        AF.Silu)
                                ss.done(ins, sGS)
                        bar.hit()
                        for tt in range(GT):
                            et = cgi * GT + tt
                            ss.do(scalar.activation(h1g[:, 0:D],
                                                    XF[:, D * tt:D * (tt + 1)],
                                                    AF.Square,
                                                    accum_out=stat[:, 2:3]))
                            a4 = scalar.activation(stat[:, 4:5], stat[:, 2:3],
                                                   AF.Sqrt, scale=1.0 / D,
                                                   bias=eps6[:, 0:1])
                            ss.do(a4)
                            scalar.sem_inc(sRM3, 1)
                        bar.hit()

            @block.tensor
            def _(tensor):
                bar = Bar(tensor, sBAR)
                tensor.wait_ge(sBR, 80)
                tensor.matmul(PF[:, 2048:2054], onesf[0:1, :], gbrow[0][0:1, :],
                              start=True, stop=True)
                tensor.matmul(PF[:, 2054:2060], onesf[0:1, :], gbrow[1][0:1, :],
                              start=True, stop=True)
                tensor.matmul(PF[:, 0:512], onesf[0:1, :], nwstg[0:1, 0:512],
                              start=True, stop=True)
                ins = tensor.matmul(PF[:, 512:1024], onesf[0:1, :],
                                    nwstg[0:1, 512:1024], start=True, stop=True)
                ins.then_inc(sBR, 1)
                tensor.wait_ge(sBR, 97)
                tensor.matmul(PF[:, 1024:1536], onesf[0:1, :],
                              nwstg[0:1, 0:512], start=True, stop=True)
                ins = tensor.matmul(PF[:, 1536:2048], onesf[0:1, :],
                                    nwstg[0:1, 512:1024], start=True, stop=True)
                ins.then_inc(sBR, 1)
                bar.hit(wait=(sBR, 98))
                for grp in range(NG):
                    for tt in range(GT):
                        g = GT * grp + tt
                        tensor.wait_ge(sLN, g + 1)
                        for k in range(8):
                            tr = 8 * g + k
                            if tr >= 2:
                                tensor.wait_ge(sTR, tr - 1)
                            ins = tensor.transpose(PT[tr % 2][:, :],
                                                   lnb[g % 2][:, 128 * k:128 * (k + 1)],
                                                   ident[:, :])
                            ins.then_inc(sTE, 1)
                    bar.hit(wait=(sTR, 8 * GT * (grp + 1)))
                    for chk in range(2):
                        cgi = 2 * grp + chk
                        for tt in range(GT):
                            rt = cgi * GT + tt
                            if rt >= 1:
                                tensor.wait_ge(sTP2, rt)
                            for ch8 in range(8):
                                tensor.matmul(
                                    PF[:, 2048:2054], XT[:, xts(ch8, tt)],
                                    rwb[:, 48 * chk + 6 * ch8:48 * chk + 6 * (ch8 + 1)],
                                    start=(ch8 == 0), stop=False)
                            ins = tensor.matmul(PF[:, 2048:2054], onesf[0:1, :],
                                                rbrow[chk][0:1, :], start=False,
                                                stop=True, skip_group_check=True)
                            ins.then_inc(sRT, 1)
                        for e in range(NE):
                            ieb = cgi * NE + e
                            shared = e >= 6
                            ns = e - 6
                            if not shared:
                                m1 = cgi * 18 + 3 * e
                                tensor.wait_ge(sCV8, 8 * (m1 + 1))
                                tensor.wait_ge(sBR3, 3 * (cgi * 6 + e + 1))
                            for tt in range(GT):
                                q = ieb * GT + tt
                                if tt == 0:
                                    tensor.wait_ge(sGS, GT * ieb)
                                else:
                                    tensor.wait_ge(sH1E, q)
                                for nb in range(2):
                                    for ch8 in range(8):
                                        if shared:
                                            rhs = shWup[:, 2048 * ch8 + 1024 * ns +
                                                        512 * nb:2048 * ch8 +
                                                        1024 * ns + 512 * (nb + 1)]
                                        else:
                                            rhs = wsub[m1 % 2][:, D * ch8 + 512 * nb:
                                                               D * ch8 + 512 * (nb + 1)]
                                        tensor.matmul(
                                            PF[:, 512 * nb:512 * (nb + 1)],
                                            XT[:, xts(ch8, tt)], rhs,
                                            start=(ch8 == 0), stop=False)
                                    brow = (shbup[0:1, 1024 * ns + 512 * nb:
                                                  1024 * ns + 512 * (nb + 1)]
                                            if shared else
                                            brup[e % 2][0:1, 512 * nb:512 * (nb + 1)])
                                    ins = tensor.matmul(
                                        PF[:, 512 * nb:512 * (nb + 1)],
                                        onesb[0:1, :], brow, start=False,
                                        stop=True, skip_group_check=True)
                                ins.then_inc(sH1, 1)
                                if not shared and tt == GT - 1:
                                    tensor.wait_ge(sH1, q + 1)
                                    tensor.sem_inc(sSUB, 1)
                                tensor.wait_ge(sH1E, q + 1)
                                for k in range(8):
                                    t1 = 8 * q + k
                                    if t1 >= 2:
                                        tensor.wait_ge(sT1E, t1 - 1)
                                    ins = tensor.transpose(
                                        PT[t1 % 2][:, :],
                                        h1g[:, D * tt + 128 * k:D * tt + 128 * (k + 1)],
                                        ident[:, :])
                                    ins.then_inc(sT1, 1)
                            if not shared:
                                m2 = m1 + 1
                                tensor.wait_ge(sCV8, 8 * (m2 + 1))
                            for tt in range(GT):
                                q = ieb * GT + tt
                                tensor.wait_ge(sT1E, 8 * (q + 1))
                                if tt == 0:
                                    tensor.wait_ge(sH1E, GT * (ieb + 1))
                                else:
                                    tensor.wait_ge(sGS, q)
                                for nb in range(2):
                                    for ch8 in range(8):
                                        rhs = (shWg[:, D * ch8 + 512 * nb:
                                                    D * ch8 + 512 * (nb + 1)]
                                               if shared else
                                               wsub[m2 % 2][:, D * ch8 + 512 * nb:
                                                            D * ch8 + 512 * (nb + 1)])
                                        tensor.matmul(
                                            PF[:, 512 * nb:512 * (nb + 1)],
                                            h1T[:, xts(ch8, tt)], rhs,
                                            start=(ch8 == 0), stop=False)
                                    brow = (shbg[0:1, 512 * nb:512 * (nb + 1)]
                                            if shared else
                                            brg[e % 2][0:1, 512 * nb:512 * (nb + 1)])
                                    ins = tensor.matmul(
                                        PF[:, 512 * nb:512 * (nb + 1)],
                                        onesb[0:1, :], brow, start=False,
                                        stop=True, skip_group_check=True)
                                ins.then_inc(sGP, 1)
                                if not shared and tt == GT - 1:
                                    tensor.wait_ge(sGP, q + 1)
                                    tensor.sem_inc(sSUB, 1)
                                tensor.wait_ge(sH2, q + 1)
                                for k in range(8):
                                    t2_ = 8 * q + k
                                    if t2_ >= 2:
                                        tensor.wait_ge(sT2E, t2_ - 1)
                                    ins = tensor.transpose(
                                        PT[t2_ % 2][:, :],
                                        h2s[q % 2][:, 128 * k:128 * (k + 1)],
                                        ident[:, :])
                                    ins.then_inc(sT2, 1)
                            if not shared:
                                m3 = m1 + 2
                                tensor.wait_ge(sCV8, 8 * (m3 + 1))
                            for tt in range(GT):
                                q = ieb * GT + tt
                                tensor.wait_ge(sT2E, 8 * (q + 1))
                                tensor.wait_ge(sAC, q)
                                for nb in range(2):
                                    for ch8 in range(8):
                                        rhs = (shWdn[:, D * ch8 + 512 * nb:
                                                     D * ch8 + 512 * (nb + 1)]
                                               if shared else
                                               wsub[m3 % 2][:, D * ch8 + 512 * nb:
                                                            D * ch8 + 512 * (nb + 1)])
                                        tensor.matmul(
                                            PF[:, 1024 + 512 * nb:
                                               1024 + 512 * (nb + 1)],
                                            h2T[:, xts(ch8, tt)], rhs,
                                            start=(ch8 == 0), stop=False)
                                    brow = (shbdn[0:1, 512 * nb:512 * (nb + 1)]
                                            if shared else
                                            brdn[e % 2][0:1, 512 * nb:512 * (nb + 1)])
                                    ins = tensor.matmul(
                                        PF[:, 1024 + 512 * nb:1024 + 512 * (nb + 1)],
                                        onesb[0:1, :], brow, start=False,
                                        stop=True, skip_group_check=True)
                                ins.then_inc(sDN, 1)
                                if not shared and tt == GT - 1:
                                    tensor.wait_ge(sDN, q + 1)
                                    tensor.sem_inc(sSUB, 1)
                        bar.hit(wait=(sDN, GT * NE * (cgi + 1)))
                        if chk == 0:
                            for tt in range(GT):
                                wcnt = grp * GT + tt
                                tensor.wait_ge(sLN2, wcnt + 1)
                                for k in range(8):
                                    tr2 = 8 * wcnt + k
                                    if tr2 >= 2:
                                        tensor.wait_ge(sTR2, tr2 - 1)
                                    ins = tensor.transpose(
                                        PT[tr2 % 2][:, :],
                                        lnb[tt % 2][:, 128 * k:128 * (k + 1)],
                                        ident[:, :])
                                    ins.then_inc(sTE2, 1)
                            bar.hit(wait=(sTR2, 8 * GT * (grp + 1)))
                        else:
                            bar.hit()

            @block.gpsimd
            def _(gpsimd):
                bar = Bar(gpsimd, sBAR)
                sg = Ser(gpsimd, sSG)
                gpsimd.dma_start(out=rbrow[0][0:1, :],
                                 in_=w["L2_ffn_router_b"][0:1, :]).then_inc(sBR, 16)
                gpsimd.dma_start(out=rbrow[1][0:1, :],
                                 in_=w["L2_ffn_router_b"][1:2, :]).then_inc(sBR, 16)
                gpsimd.dma_start(out=gbrow[0][0:1, :],
                                 in_=w["L2_ffn_gate_bias"][0:1, :]).then_inc(sBR, 16)
                gpsimd.dma_start(out=gbrow[1][0:1, :],
                                 in_=w["L2_ffn_gate_bias"][1:2, :]).then_inc(sBR, 16)
                gpsimd.dma_start(out=nwstg[0:1, :],
                                 in_=w["L2_ffn_norm_w"][0:1, :]).then_inc(sBR, 16)
                gpsimd.wait_ge(sBR, 81)
                gpsimd.dma_start(out=nwstg[0:1, :],
                                 in_=w["L2_ffn_norm_w"][1:2, :]).then_inc(sBR, 16)
                bar.hit()
                mg = [0]
                for grp in range(NG):
                    bar.hit()
                    for chk in range(2):
                        cgi = 2 * grp + chk
                        for e in range(6):
                            iexp = cgi * 6 + e
                            if iexp >= 2:
                                pe = iexp - 2
                                ieb_prev = (pe // 6) * NE + (pe % 6)
                                gpsimd.wait_ge(sDN, GT * (ieb_prev + 1))
                            for ri, (tgt, nm2) in enumerate(
                                    [(brup[e % 2], "L2_ffn_exp_bup"),
                                     (brg[e % 2], "L2_ffn_exp_bg"),
                                     (brdn[e % 2], "L2_ffn_exp_bdown")]):
                                rr = 3 * iexp + ri
                                gpsimd.dma_start(out=brstg[0:1, :],
                                                 in_=w[nm2][e:e + 1, :]
                                                 ).then_inc(sBR2, 16)
                                gpsimd.wait_ge(sBR2, 16 * (rr + 1))
                                ins = gpsimd.tensor_copy(tgt[0:1, :],
                                                         brstg[0:1, :])
                                sg.done(ins, sBR3)
                            for smi in range(3):
                                m = mg[0]
                                if m >= 2:
                                    gpsimd.wait_ge(sSUB, m - 1)
                                for p in range(8):
                                    gpsimd.wait_ge(sRE, 16 * (8 * m + p + 1))
                                    ins = gpsimd.tensor_copy(
                                        wsub[m % 2][:, D * p:D * (p + 1)],
                                        wstg[:, :])
                                    sg.done(ins, sCV8)
                                mg[0] += 1
                        bar.hit()
                        if chk == 1:
                            for tt in range(GT):
                                g = GT * grp + tt
                                gpsimd.wait_ge(sOUT, g + 1)
                                gpsimd.dma_start(out=dst[128 * g:128 * (g + 1), :],
                                                 in_=XF[:, D * tt:D * (tt + 1)]
                                                 ).then_inc(sST, 16)
                            gpsimd.wait_ge(sST, 16 * GT * (grp + 1))
                        bar.hit()

    c.boundary()


def st_l3_attn(c, src, dst):
    nc = c.nc
    w = c.dram
    sem = c.sem
    names = ["sLD", "sW", "sCV", "sWB", "sCB", "sTW", "sTWE", "sWQ", "sCQ",
             "sPE", "sPEE", "sSTD", "sVAR", "sSTD2", "sLN", "sTR", "sTE",
             "sVP", "sVE", "sQK", "sQE2", "sRP", "sRPE", "sSC", "sMX", "sEX",
             "sRC", "sSCL", "sTA", "sTAE", "sAV", "sAE", "sU", "sOUT", "sST",
             "sX2", "sBAR", "sSV", "sSS", "sSG", "sQD"]
    (sLD, sW, sCV, sWB, sCB, sTW, sTWE, sWQ, sCQ, sPE, sPEE, sSTD, sVAR,
     sSTD2, sLN, sTR, sTE, sVP, sVE, sQK, sQE2, sRP, sRPE, sSC, sMX, sEX,
     sRC, sSCL, sTA, sTAE, sAV, sAE, sU, sOUT, sST, sX2, sBAR, sSV, sSS,
     sSG, sQD) = (sem(x) for x in names)

    ESC = 1.0 / math.sqrt(96.0)
    itemsB = []
    tau_after = []
    tau = 0
    for s_ in range(NSEQ):
        for qt in range(TT):
            for h in range(16):
                H = (s_ * TT + qt) * 16 + h
                tau += qt + 1
                itemsB.append((s_, qt, h, H, (qt + 1) * 128, s_ * TT + qt, H // 2))
                tau_after.append(tau)

    def tau_start(H):
        return tau_after[H] - (itemsB[H][1] + 1)

    def bitems(s_, gb):
        lo = s_ * 256 + gb * 64
        return itemsB[lo:lo + 64]

    with ExitStack() as ctx:
        sb = lambda name, shape, dt: ctx.enter_context(nc.sbuf_tensor(name, shape, dt))
        xt = [sb(f"l3a_xt{i}", [128, D], F32) for i in range(2)]
        outt0 = sb("l3a_out0", [128, D], F32)
        outt = [outt0, outt0]
        sq = sb("l3a_sq", [128, D], F32)
        stat = sb("l3a_st", [128, 8], F32)
        eps = sb("l3a_eps", [128, 1], F32)
        lnb0 = sb("l3a_lnb0", [128, D], BF16)
        lnb = [lnb0, lnb0]
        lnT = sb("l3a_lnT", [128, 8 * 512], BF16)
        Vt = sb("l3a_V", [128, 16 * D], BF16)
        QTg = sb("l3a_QT", [128, 8 * 512], BF16)
        KT = sb("l3a_KT", [128, 8 * S], BF16)
        qRTg = sb("l3a_qRT", [128, 8 * 512], BF16)
        kRT4 = sb("l3a_kRT4", [128, S], BF16)
        ropet = sb("l3a_ropet", [128, S], BF16)
        att0 = sb("l3a_att", [128, S], BF16)
        att = [att0, att0]
        attT = [sb(f"l3a_attT{i}", [128, 128], BF16) for i in range(2)]
        avh = sb("l3a_avh", [128, 8 * 128], BF16)
        sm = sb("l3a_sm", [128, 8], F32)
        QEw = sb("l3a_QEw", [128, 8 * D], BF16)
        KEw = sb("l3a_KEw", [128, 8 * D], BF16)
        VEw = sb("l3a_VEw", [128, 8 * D], BF16)
        QREw = sb("l3a_QREw", [128, 8 * 512], BF16)
        wkr4 = sb("l3a_wkr4", [128, 8 * 128], BF16)
        wow = sb("l3a_wo", [128, 8 * D], BF16)
        wuk = sb("l3a_wuk", [128, 2 * D], BF16)
        wuv = sb("l3a_wuv", [128, 2 * D], BF16)
        wtp = Vt[:, 0:32 * 128]
        wblk = sq[:, 0:128]
        wbb = KT[:, 0:128]
        wqs = [KT[:, 1024:2048], KT[:, 2048:3072]]
        wrs = [KT[:, 3072:3584], KT[:, 3584:4096]]
        wstg = xt[0]
        wstg2 = outt[1][:, 0:512]
        PF = ctx.enter_context(nc.psum_tensor("l3a_pf", [128, 3072], F32))
        PT = [ctx.enter_context(nc.psum_tensor(f"l3a_pt{i}", [128, 128], BF16))
              for i in range(2)]
        ident = c.ident

        with nc.Block() as block:
            @block.sync
            def _(sync):
                bar = Bar(sync, sBAR)
                nw = [0]
                for nm, cols, n in [("L3_attn_W_UK", 1024, 2),
                                    ("L3_attn_W_UV", 1024, 2),
                                    ("W_KR4", 128, 8),
                                    ("L3_attn_W_O", 1024, 8)]:
                    for i in range(n):
                        if nw[0] >= 1:
                            sync.wait_ge(sCV, nw[0])
                        sync.dma_start(out=wstg[:, 0:cols],
                                       in_=w[nm][128 * i:128 * (i + 1), :]
                                       ).then_inc(sW, 16)
                        nw[0] += 1
                for hh in range(2):
                    sync.wait_ge(sCV, nw[0])
                    sync.dma_start(out=wstg[:, :],
                                   in_=c.dram["cropet"][:, 1024 * hh:1024 * (hh + 1)]
                                   ).then_inc(sW, 16)
                    nw[0] += 1
                nb_ = [0]
                for d in range(8):
                    for dc in range(2):
                        if nb_[0] >= 1:
                            sync.wait_ge(sCB, nb_[0])
                        sync.dma_start(out=wblk[:, :],
                                       in_=w["L3_attn_W_DKV"][128 * d:128 * (d + 1),
                                                              128 * dc:128 * (dc + 1)]
                                       ).then_inc(sWB, 16)
                        nb_[0] += 1
                nq = [0]
                for d in range(8):
                    for dc in range(32):
                        if nb_[0] >= 1:
                            sync.wait_ge(sCB, nb_[0])
                        sync.dma_start(out=wblk[:, :],
                                       in_=w["L3_attn_W_DQ"][128 * d:128 * (d + 1),
                                                             128 * dc:128 * (dc + 1)]
                                       ).then_inc(sWB, 16)
                        nb_[0] += 1
                    for dc in range(32):
                        if nq[0] >= 2:
                            sync.wait_ge(sCQ, nq[0] - 1)
                        sync.dma_start(out=wstg[:, :],
                                       in_=w["L3_attn_W_UQ"][128 * dc:128 * (dc + 1), :]
                                       ).then_inc(sWQ, 16)
                        sync.dma_start(out=wstg2[:, :],
                                       in_=w["L3_attn_W_QR"][128 * dc:128 * (dc + 1), :]
                                       ).then_inc(sWQ, 16)
                        nq[0] += 2
                bar.hit()
                for s_ in range(NSEQ):
                    for gb in range(4):
                        for t in range(4 * gb, 4 * gb + 4):
                            g = TT * s_ + t
                            if g >= 2:
                                sync.wait_ge(sLN, g - 1)
                            sync.dma_start(out=xt[g % 2][:, :],
                                           in_=src[128 * g:128 * (g + 1), :]
                                           ).then_inc(sLD, 16)
                        sync.wait_ge(sLD, 16 * (TT * s_ + 4 * gb + 4))
                        bar.hit()
                        for qt in range(4 * gb, 4 * gb + 4):
                            g = TT * s_ + qt
                            if g >= 2:
                                sync.wait_ge(sOUT, g - 1)
                            sync.dma_start(out=xt[qt % 2][:, :],
                                           in_=src[128 * g:128 * (g + 1), :]
                                           ).then_inc(sX2, 16)
                        sync.wait_ge(sX2, 16 * (TT * s_ + 4 * gb + 4))
                        bar.hit()

            @block.vector
            def _(vector):
                bar = Bar(vector, sBAR)
                sv = Ser(vector, sSV)
                sv.do(vector.memset(eps[:, :], 1.0e-5))
                ncv = [0]
                for tgt, cols, n in [(wuk, 1024, 2), (wuv, 1024, 2),
                                     (wkr4, 128, 8), (wow, 1024, 8)]:
                    for i in range(n):
                        vector.wait_ge(sW, 16 * (ncv[0] + 1))
                        ins = vector.tensor_copy(tgt[:, cols * i:cols * (i + 1)],
                                                 wstg[:, 0:cols])
                        sv.done(ins, sCV)
                        ncv[0] += 1
                for hh in range(2):
                    vector.wait_ge(sW, 16 * (ncv[0] + 1))
                    ins = vector.tensor_copy(ropet[:, 1024 * hh:1024 * (hh + 1)],
                                             wstg[:, :])
                    sv.done(ins, sCV)
                    ncv[0] += 1
                nb_ = [0]
                ntw = [0]
                npe = [0]

                def blk():
                    vector.wait_ge(sWB, 16 * (nb_[0] + 1))
                    if nb_[0] >= 1:
                        vector.wait_ge(sTW, nb_[0])
                    ins = vector.tensor_copy(wbb[:, :], wblk[:, :])
                    sv.done(ins, sCB)
                    nb_[0] += 1
                    tr = ntw[0]
                    vector.wait_ge(sTW, tr + 1)
                    ins = vector.tensor_copy(wtp[:, 128 * (tr % 32):
                                                 128 * (tr % 32) + 128]
                                             if False else
                                             wtp[:, 128 * ((tr) % 32):
                                                 128 * ((tr) % 32 + 1)],
                                             PT[tr % 2][:, :])
                    sv.done(ins, sTWE)
                    ntw[0] += 1
                for d in range(8):
                    for dc in range(2):
                        blk()
                    for tgt in (KEw, VEw):
                        for ch in range(2):
                            vector.wait_ge(sPE, npe[0] + 1)
                            ins = vector.tensor_copy(
                                tgt[:, D * d + 512 * ch:D * d + 512 * (ch + 1)],
                                PF[:, 512 * (npe[0] % 4):512 * (npe[0] % 4) + 512])
                            sv.done(ins, sPEE)
                            npe[0] += 1
                nq = [0]
                for d in range(8):
                    vector.wait_ge(sPE, 32 + 3 * d)
                    for dc in range(32):
                        blk()
                    for dc in range(32):
                        gd = 32 * d + dc
                        if gd >= 2:
                            vector.wait_ge(sQD, gd - 1)
                        vector.wait_ge(sWQ, 16 * (nq[0] + 1))
                        ins = vector.tensor_copy(wqs[dc % 2][:, :], wstg[:, :])
                        sv.done(ins, sCQ)
                        nq[0] += 1
                        vector.wait_ge(sWQ, 16 * (nq[0] + 1))
                        ins = vector.tensor_copy(wrs[dc % 2][:, :], wstg2[:, :])
                        sv.done(ins, sCQ)
                        nq[0] += 1
                    for ch in range(2):
                        vector.wait_ge(sPE, npe[0] + 1)
                        ins = vector.tensor_copy(
                            QEw[:, D * d + 512 * ch:D * d + 512 * (ch + 1)],
                            PF[:, 512 * (npe[0] % 4):512 * (npe[0] % 4) + 512])
                        sv.done(ins, sPEE)
                        npe[0] += 1
                    vector.wait_ge(sPE, npe[0] + 1)
                    ins = vector.tensor_copy(
                        QREw[:, 512 * d:512 * (d + 1)],
                        PF[:, 512 * (npe[0] % 4):512 * (npe[0] % 4) + 512])
                    sv.done(ins, sPEE)
                    npe[0] += 1
                bar.hit()
                for s_ in range(NSEQ):
                    for gb in range(4):
                        for t in range(4 * gb, 4 * gb + 4):
                            g = TT * s_ + t
                            vector.wait_ge(sSTD, g + 1)
                            ln_v1(vector, sv, stat, eps)
                            vector.sem_inc(sVAR, 1)
                            vector.wait_ge(sSTD2, g + 1)
                            if g >= 1:
                                vector.wait_ge(sTE, 8 * g)
                            ln_v2(vector, sv, stat, xt[g % 2], lnb[g % 2])
                            vector.sem_inc(sLN, 1)
                            for k in range(8):
                                tr = 8 * g + k
                                vector.wait_ge(sTR, tr + 1)
                                ins = vector.tensor_copy(
                                    lnT[:, 512 * k + 128 * (t % 4):
                                        512 * k + 128 * (t % 4 + 1)],
                                    PT[tr % 2][:, :])
                                sv.done(ins, sTE)
                        gqi = s_ * 4 + gb
                        for hp in range(8):
                            rp = gqi * 9 + hp
                            vector.wait_ge(sRP, rp + 1)
                            ins = vector.tensor_tensor(
                                qRTg[:, 512 * hp:512 * (hp + 1)],
                                PF[:, 2048 + 512 * (hp % 2):
                                   2048 + 512 * (hp % 2) + 512],
                                ropet[:, 512 * gb:512 * (gb + 1)], ALU.mult)
                            sv.done(ins, sRPE)
                        rp = gqi * 9 + 8
                        vector.wait_ge(sRP, rp + 1)
                        ins = vector.tensor_tensor(
                            kRT4[:, 512 * gb:512 * (gb + 1)],
                            PF[:, 2048:2560],
                            ropet[:, 512 * gb:512 * (gb + 1)], ALU.mult)
                        sv.done(ins, sRPE)
                        bar.hit()
                        for (s2, qt, h, H, L, qtg, pr) in bitems(s_, gb):
                            vector.wait_ge(sSC, H + 1)
                            sv.do(vector.tensor_tensor(
                                PF[:, 128 * qt:128 * (qt + 1)],
                                PF[:, 128 * qt:128 * (qt + 1)],
                                c.mdiag[:, :], ALU.add))
                            sv.do(vector.reduce_max(sm[:, 0:1], PF[:, 0:L],
                                                    axis=AX.X))
                            ins = vector.tensor_scalar_mul(sm[:, 1:2],
                                                           sm[:, 0:1], -ESC)
                            sv.done(ins, sMX)
                            vector.wait_ge(sEX, H + 1)
                            if H >= 1:
                                vector.wait_ge(sSCL, H)
                            ins = vector.reciprocal(sm[:, 3:4], sm[:, 2:3])
                            sv.done(ins, sRC)
                            t2 = tau_start(H)
                            for kt in range(qt + 1):
                                vector.wait_ge(sTA, t2 + 1)
                                ins = vector.tensor_copy(attT[t2 % 2][:, :],
                                                         PT[t2 % 2][:, :])
                                sv.done(ins, sTAE)
                                t2 += 1
                            if h == 15:
                                vector.wait_ge(sU, 2 * (qtg + 1))
                                vector.wait_ge(sX2, 16 * (qt + 1) + 16 * TT * s_)
                                if qtg >= 1:
                                    vector.wait_ge(sST, 16 * qtg)
                                for nb2 in range(2):
                                    ins = vector.scalar_tensor_tensor(
                                        outt[qt % 2][:, 512 * nb2:512 * (nb2 + 1)],
                                        PF[:, 2048 + 512 * nb2:
                                           2048 + 512 * (nb2 + 1)],
                                        1.0,
                                        xt[qt % 2][:, 512 * nb2:512 * (nb2 + 1)],
                                        ALU.mult, ALU.add)
                                    sv.do(ins)
                                vector.sem_inc(sOUT, 1)
                        bar.hit()

            @block.scalar
            def _(scalar):
                bar = Bar(scalar, sBAR)
                ss = Ser(scalar, sSS)
                bar.hit()
                for s_ in range(NSEQ):
                    for gb in range(4):
                        for t in range(4 * gb, 4 * gb + 4):
                            g = TT * s_ + t
                            scalar.wait_ge(sLD, 16 * (g + 1))
                            ln_s1(scalar, ss, xt[g % 2], sq, stat)
                            scalar.sem_inc(sSTD, 1)
                            scalar.wait_ge(sVAR, g + 1)
                            ln_s2(scalar, ss, stat)
                            scalar.sem_inc(sSTD2, 1)
                            for nb2 in range(2):
                                scalar.wait_ge(sVP, 2 * g + nb2 + 1)
                                ins = scalar.activation(
                                    Vt[:, D * t + 512 * nb2:D * t + 512 * (nb2 + 1)],
                                    PF[:, 2560:3072], AF.Copy)
                                ss.done(ins, sVE)
                        gqi = s_ * 4 + gb
                        for wi in range(2):
                            for hc in range(8):
                                qv = gqi * 16 + wi * 8 + hc
                                scalar.wait_ge(sQK, qv + 1)
                                if wi == 0:
                                    tgt = QTg[:, 512 * hc:512 * (hc + 1)]
                                else:
                                    tgt = KT[:, 2048 * hc + 512 * gb:
                                             2048 * hc + 512 * (gb + 1)]
                                ins = scalar.activation(
                                    tgt, PF[:, 512 * (qv % 4):512 * (qv % 4) + 512],
                                    AF.Copy)
                                ss.done(ins, sQE2)
                        bar.hit()
                        for (s2, qt, h, H, L, qtg, pr) in bitems(s_, gb):
                            scalar.wait_ge(sMX, H + 1)
                            if H >= 1:
                                scalar.wait_ge(sTA, tau_after[H - 1])
                            ins = scalar.activation(att[h % 2][:, 0:L],
                                                    PF[:, 0:L], AF.Exp,
                                                    bias=sm[:, 1:2], scale=ESC,
                                                    accum_out=sm[:, 2:3])
                            ss.done(ins, sEX)
                            if h % 2 == 1:
                                hc = h // 2
                                scalar.wait_ge(sAV, 2 * (pr + 1))
                                if qtg >= 1:
                                    scalar.wait_ge(sU, 2 * qtg)
                                ins = scalar.activation(
                                    avh[:, 128 * hc:128 * (hc + 1)],
                                    PF[:, 2048 + 512 * (pr % 2):
                                       2048 + 512 * (pr % 2) + 128],
                                    AF.Copy)
                                ss.done(ins, sAE)
                        bar.hit()

            @block.tensor
            def _(tensor):
                bar = Bar(tensor, sBAR)
                nb_ = [0]
                ntw = [0]
                npe = [0]

                def tblk():
                    tr = ntw[0]
                    tensor.wait_ge(sCB, nb_[0] + 1)
                    nb_[0] += 1
                    if tr >= 2:
                        tensor.wait_ge(sTWE, tr - 1)
                    ins = tensor.transpose(PT[tr % 2][:, :], wbb[:, :],
                                           ident[:, :])
                    ins.then_inc(sTW, 1)
                    ntw[0] += 1

                for d in range(8):
                    for dc in range(2):
                        tblk()
                    tensor.wait_ge(sTWE, ntw[0])
                    tensor.wait_ge(sCV, 4)
                    for ti, wres in [(0, wuk), (1, wuv)]:
                        for ch in range(2):
                            if npe[0] >= 4:
                                tensor.wait_ge(sPEE, npe[0] - 3)
                            for dc in range(2):
                                ins = tensor.matmul(
                                    PF[:, 512 * (npe[0] % 4):512 * (npe[0] % 4) + 512],
                                    wtp[:, 128 * ((ntw[0] - 2 + dc) % 32):
                                        128 * ((ntw[0] - 2 + dc) % 32 + 1)],
                                    wres[:, D * dc + 512 * ch:D * dc + 512 * (ch + 1)],
                                    start=(dc == 0), stop=(dc == 1))
                            ins.then_inc(sPE, 1)
                            npe[0] += 1
                nq = [0]
                for d in range(8):
                    for dc in range(32):
                        tblk()
                    tensor.wait_ge(sTWE, ntw[0])
                    if npe[0] >= 4:
                        pass
                    for dc in range(32):
                        tensor.wait_ge(sCQ, nq[0] + 2)
                        base = (ntw[0] - 32 + dc) % 32
                        if dc == 0:
                            tensor.wait_ge(sPEE, npe[0])
                        for ch in range(2):
                            tensor.matmul(
                                PF[:, 512 * ((npe[0] + ch) % 4):
                                   512 * ((npe[0] + ch) % 4) + 512],
                                wtp[:, 128 * base:128 * (base + 1)],
                                wqs[dc % 2][:, 512 * ch:512 * (ch + 1)],
                                start=(dc == 0), stop=(dc == 31))
                        ins = tensor.matmul(
                            PF[:, 512 * ((npe[0] + 2) % 4):
                               512 * ((npe[0] + 2) % 4) + 512],
                            wtp[:, 128 * base:128 * (base + 1)],
                            wrs[dc % 2][:, :],
                            start=(dc == 0), stop=(dc == 31))
                        ins.then_inc(sQD, 1)
                        nq[0] += 2
                        if dc == 31:
                            tensor.wait_ge(sQD, 32 * (d + 1))
                            tensor.sem_inc(sPE, 3)
                            npe[0] += 3
                bar.hit(wait=(sPEE, npe[0]))
                for s_ in range(NSEQ):
                    for gb in range(4):
                        gqi = s_ * 4 + gb
                        for t in range(4 * gb, 4 * gb + 4):
                            g = TT * s_ + t
                            tensor.wait_ge(sLN, g + 1)
                            for k in range(8):
                                tr = 8 * g + k
                                if tr >= 2:
                                    tensor.wait_ge(sTE, tr - 1)
                                ins = tensor.transpose(
                                    PT[tr % 2][:, :],
                                    lnb[g % 2][:, 128 * k:128 * (k + 1)],
                                    ident[:, :])
                                ins.then_inc(sTR, 1)
                            tensor.wait_ge(sTE, 8 * (g + 1))
                            for nb2 in range(2):
                                if 2 * g + nb2 >= 1:
                                    tensor.wait_ge(sVE, 2 * g + nb2)
                                for ch in range(8):
                                    ins = tensor.matmul(
                                        PF[:, 2560:3072],
                                        lnT[:, 512 * ch + 128 * (t % 4):
                                            512 * ch + 128 * (t % 4 + 1)],
                                        VEw[:, D * ch + 512 * nb2:
                                            D * ch + 512 * (nb2 + 1)],
                                        start=(ch == 0), stop=(ch == 7))
                                ins.then_inc(sVP, 1)
                        for wi in range(2):
                            wsb = QEw if wi == 0 else KEw
                            for hc in range(8):
                                qv = gqi * 16 + wi * 8 + hc
                                if qv >= 4:
                                    tensor.wait_ge(sQE2, qv - 3)
                                for ch in range(8):
                                    ins = tensor.matmul(
                                        PF[:, 512 * (qv % 4):512 * (qv % 4) + 512],
                                        wsb[:, D * ch + 128 * hc:
                                            D * ch + 128 * (hc + 1)],
                                        lnT[:, 512 * ch:512 * (ch + 1)],
                                        start=(ch == 0), stop=(ch == 7))
                                ins.then_inc(sQK, 1)
                        tensor.wait_ge(sVE, 2 * (TT * s_ + 4 * gb + 4))
                        for hp in range(8):
                            rp = gqi * 9 + hp
                            if rp >= 2:
                                tensor.wait_ge(sRPE, rp - 1)
                            psl = PF[:, 2048 + 512 * (hp % 2):
                                     2048 + 512 * (hp % 2) + 512]
                            for sub2 in range(2):
                                h_ = 2 * hp + sub2
                                rows = slice(64 * sub2, 64 * sub2 + 32)
                                for ch in range(8):
                                    ins = tensor.matmul(
                                        psl[rows, :],
                                        QREw[:, 512 * ch + 32 * h_:
                                             512 * ch + 32 * (h_ + 1)],
                                        lnT[:, 512 * ch:512 * (ch + 1)],
                                        start=(ch == 0), stop=(ch == 7))
                            ins.then_inc(sRP, 1)
                        rp = gqi * 9 + 8
                        tensor.wait_ge(sRPE, rp - 1)
                        for ch in range(8):
                            ins = tensor.matmul(
                                PF[:, 2048:2560],
                                wkr4[:, 128 * ch:128 * (ch + 1)],
                                lnT[:, 512 * ch:512 * (ch + 1)],
                                start=(ch == 0), stop=(ch == 7))
                        ins.then_inc(sRP, 1)
                        bar.hit(wait=(sRPE, 9 * (gqi + 1)))
                        for (s2, qt, h, H, L, qtg, pr) in bitems(s_, gb):
                            hc, hb = h // 2, (h % 2) * 64
                            tensor.wait_ge(sEX, H)
                            nkb = (L + 511) // 512
                            for kb in range(nkb):
                                nn = min(512, L - 512 * kb)
                                tensor.matmul(
                                    PF[:, 512 * kb:512 * kb + nn],
                                    QTg[hb:hb + 64,
                                        512 * hc + 128 * (qt % 4):
                                        512 * hc + 128 * (qt % 4 + 1)],
                                    KT[hb:hb + 64, 2048 * hc + 512 * kb:
                                       2048 * hc + 512 * kb + nn],
                                    start=True, stop=False)
                                rb = 64 * (h % 2)
                                ins = tensor.matmul(
                                    PF[:, 512 * kb:512 * kb + nn],
                                    qRTg[rb:rb + 32,
                                         512 * (h // 2) + 128 * (qt % 4):
                                         512 * (h // 2) + 128 * (qt % 4 + 1)],
                                    kRT4[rb:rb + 32, 512 * kb:512 * kb + nn],
                                    start=False, stop=True)
                            ins.then_inc(sSC, 1)
                            tensor.wait_ge(sSCL, H + 1)
                            if h % 2 == 0 and pr >= 2:
                                tensor.wait_ge(sAE, pr - 1)
                            if h == 0 and qtg >= 1:
                                tensor.wait_ge(sOUT, qtg)
                            t2 = tau_start(H)
                            for kt in range(qt + 1):
                                if t2 >= 2:
                                    tensor.wait_ge(sTAE, t2 - 1)
                                ins = tensor.transpose(
                                    PT[t2 % 2][:, :],
                                    att[h % 2][:, 128 * kt:128 * (kt + 1)],
                                    ident[:, :])
                                ins.then_inc(sTA, 1)
                                tensor.wait_ge(sTAE, t2 + 1)
                                ins = tensor.matmul(
                                    PF[hb:hb + 64,
                                       2048 + 512 * (pr % 2):
                                       2048 + 512 * (pr % 2) + 128],
                                    Vt[:, D * kt + 64 * h:D * kt + 64 * (h + 1)],
                                    attT[t2 % 2][:, :],
                                    start=(kt == 0), stop=(kt == qt))
                                t2 += 1
                            ins.then_inc(sAV, 1)
                            if h == 15:
                                tensor.wait_ge(sAE, 8 * (qtg + 1))
                                for nb2 in range(2):
                                    for hc2 in range(8):
                                        ins = tensor.matmul(
                                            PF[:, 2048 + 512 * nb2:
                                               2048 + 512 * (nb2 + 1)],
                                            avh[:, 128 * hc2:128 * (hc2 + 1)],
                                            wow[:, D * hc2 + 512 * nb2:
                                                D * hc2 + 512 * (nb2 + 1)],
                                            start=(hc2 == 0), stop=(hc2 == 7))
                                    ins.then_inc(sU, 1)
                        bar.hit(wait=(sU, 2 * (TT * s_ + 4 * gb + 4)))

            @block.gpsimd
            def _(gpsimd):
                bar = Bar(gpsimd, sBAR)
                sg = Ser(gpsimd, sSG)
                bar.hit()
                nst = 0
                for s_ in range(NSEQ):
                    for gb in range(4):
                        bar.hit()
                        for (s2, qt, h, H, L, qtg, pr) in bitems(s_, gb):
                            gpsimd.wait_ge(sRC, H + 1)
                            ins = gpsimd.tensor_scalar_mul(att[h % 2][:, 0:L],
                                                           att[h % 2][:, 0:L],
                                                           sm[:, 3:4])
                            sg.done(ins, sSCL)
                            if h == 15:
                                gpsimd.wait_ge(sOUT, qtg + 1)
                                g = TT * s_ + qt
                                gpsimd.dma_start(
                                    out=dst[128 * g:128 * (g + 1), :],
                                    in_=outt[qt % 2][:, :]).then_inc(sST, 16)
                                nst += 1
                        gpsimd.wait_ge(sST, 16 * nst)
                        bar.hit()

    c.boundary()


def st_l3_ffn(c, src, dst):
    nc = c.nc
    w = c.dram
    sem = c.sem
    names = ["sLD", "sW", "sCV", "sSTD", "sVAR", "sSTD2", "sLN", "sTE", "sTR",
             "sRT", "sTP2", "sGP", "sSI", "sUP", "sHM", "sTH", "sTHE", "sEO",
             "sAC", "sST", "sBAR", "sSV", "sSS", "sSG", "sZF", "sZT", "sZE"]
    (sLD, sW, sCV, sSTD, sVAR, sSTD2, sLN, sTE, sTR, sRT, sTP2, sGP, sSI, sUP,
     sHM, sTH, sTHE, sEO, sAC, sST, sBAR, sSV, sSS, sSG, sZF, sZT, sZE) = (
        sem(x) for x in names)

    GT = 16
    NG = NT // GT
    NU = 16
    IH = 1408
    NIC = 11

    with ExitStack() as ctx:
        sb = lambda name, shape, dt: ctx.enter_context(nc.sbuf_tensor(name, shape, dt))
        xt = [sb(f"l3f_xt{i}", [128, D], F32) for i in range(2)]
        sq = sb("l3f_sq", [128, D], F32)
        stat = sb("l3f_st", [128, 8], F32)
        eps = sb("l3f_eps", [128, 1], F32)
        lnb = [sb(f"l3f_lnb{i}", [128, D], BF16) for i in range(2)]
        lnTg = sb("l3f_lnTg", [128, 8 * GT * 128], BF16)
        acc = sb("l3f_acc", [128, GT * D], F32)
        comb = sb("l3f_comb", [128, GT * 8], F32)
        rsc = sb("l3f_rsc", [128, 8], F32)
        tkm = sb("l3f_tkm", [128, 1], F32)
        rwf = sb("l3f_rwf", [128, 64], F32)
        zf = sb("l3f_zf", [128, D], F32)
        zTf = sb("l3f_zTf", [128, D], F32)
        wstg = sb("l3f_wstg", [128, IH], F32)
        wg = sb("l3f_wg", [128, 8 * IH], BF16)
        wu = sb("l3f_wu", [128, 8 * IH], BF16)
        wd = sb("l3f_wd", [128, NIC * D], BF16)
        hg = sb("l3f_hg", [128, IH], F32)
        hsb = sb("l3f_hsb", [128, IH], BF16)
        hT = sb("l3f_hT", [128, NIC * 128], BF16)
        PF = ctx.enter_context(nc.psum_tensor("l3f_pf", [128, 2560], F32))
        PT = [ctx.enter_context(nc.psum_tensor(f"l3f_pt{i}", [128, 128], BF16))
              for i in range(2)]
        PTF = ctx.enter_context(nc.psum_tensor("l3f_ptf", [128, 128], F32))
        ident, identf = c.ident, c.identf
        GSUB = [(0, 512), (512, 512), (1024, 384)]
        DSUB = [(1536, 512), (2048, 512)]

        def chunk_src(e, half, j):
            if j < 8:
                return w["L3_ffn_gate_wT"][e, 128 * j:128 * (j + 1),
                                           IH * half:IH * (half + 1)], IH
            if j < 16:
                r = j - 8
                return w["L3_ffn_up_wT"][e, 128 * r:128 * (r + 1),
                                         IH * half:IH * (half + 1)], IH
            ic = j - 16
            r0 = IH * half + 128 * ic
            return w["L3_ffn_down_wT"][e, r0:r0 + 128, :], D

        with nc.Block() as block:
            @block.sync
            def _(sync):
                bar = Bar(sync, sBAR)
                for chd in range(8):
                    sync.dma_start(out=rwf[:, 8 * chd:8 * (chd + 1)],
                                   in_=w["L3_ffn_router_wT"][128 * chd:128 * (chd + 1), :]
                                   ).then_inc(sW, 16)
                sync.wait_ge(sW, 16 * 8)
                bar.hit()
                gl = [0]
                for grp in range(NG):
                    for tt in range(GT):
                        g = GT * grp + tt
                        if g >= 2:
                            sync.wait_ge(sLN, g - 1)
                        sync.dma_start(out=xt[g % 2][:, :],
                                       in_=src[128 * g:128 * (g + 1), :]
                                       ).then_inc(sLD, 16)
                    sync.wait_ge(sLD, 16 * GT * (grp + 1))
                    bar.hit()
                    for u in range(NU):
                        e, half = u // 2, u % 2
                        for j in range(27):
                            if gl[0] >= 1:
                                sync.wait_ge(sCV, gl[0])
                            srcap, width = chunk_src(e, half, j)
                            sync.dma_start(out=wstg[:, 0:width],
                                           in_=srcap).then_inc(sW, 16)
                            gl[0] += 1
                    sync.wait_ge(sCV, 27 * NU * (grp + 1))
                    bar.hit()

            @block.vector
            def _(vector):
                bar = Bar(vector, sBAR)
                sv = Ser(vector, sSV)
                sv.do(vector.memset(eps[:, :], 1.0e-5))
                bar.hit()
                gl = [0]
                for grp in range(NG):
                    for tt in range(GT):
                        g = GT * grp + tt
                        vector.wait_ge(sSTD, g + 1)
                        ln_v1(vector, sv, stat, eps)
                        vector.sem_inc(sVAR, 1)
                        if grp >= 1:
                            vector.wait_ge(sST, 16 * (GT * (grp - 1) + tt + 1))
                        sv.do(vector.tensor_copy(acc[:, D * tt:D * (tt + 1)],
                                                 xt[g % 2][:, :]))
                        vector.wait_ge(sSTD2, g + 1)
                        if g >= 2:
                            vector.wait_ge(sTE, 8 * (g - 1))
                        ln_v2(vector, sv, stat, xt[g % 2], lnb[g % 2])
                        if g >= 1:
                            vector.wait_ge(sZT, 8 * g)
                        ins = vector.tensor_scalar(zf[:, :], xt[g % 2][:, :],
                                                   stat[:, 5:6], stat[:, 7:8],
                                                   ALU.mult, ALU.subtract)
                        sv.done(ins, sZF)
                        vector.sem_inc(sLN, 1)
                        for k in range(8):
                            tr = 8 * g + k
                            vector.wait_ge(sTE, tr + 1)
                            ins = vector.tensor_copy(
                                lnTg[:, GT * 128 * k + 128 * tt:
                                     GT * 128 * k + 128 * (tt + 1)],
                                PT[tr % 2][:, :])
                            sv.done(ins, sTR)
                        vector.wait_ge(sRT, g + 1)
                        sv.do(vector.tensor_copy(rsc[:, :], PF[:, 1408:1416]))
                        sv.do(vector.reduce_max(tkm[:, 0:1], rsc[:, :], axis=AX.X))
                        ins = vector.tensor_single_scalar(
                            comb[:, 8 * tt:8 * (tt + 1)], rsc[:, :],
                            tkm[:, 0:1], ALU.is_equal)
                        sv.done(ins, sTP2)
                    bar.hit()
                    for u in range(NU):
                        e = u // 2
                        for j in range(27):
                            vector.wait_ge(sW, 16 * (8 + gl[0] + 1))
                            width = IH if j < 16 else D
                            if j < 8:
                                tgt = wg[:, IH * j:IH * (j + 1)]
                            elif j < 16:
                                tgt = wu[:, IH * (j - 8):IH * (j - 7)]
                            else:
                                tgt = wd[:, D * (j - 16):D * (j - 15)]
                            ins = vector.tensor_copy(tgt, wstg[:, 0:width])
                            sv.done(ins, sCV)
                            gl[0] += 1
                        for tt in range(GT):
                            it = GT * (NU * grp + u) + tt
                            vector.wait_ge(sUP, 3 * (it + 1))
                            ins = vector.tensor_mul(hsb[:, :], PF[:, 0:IH],
                                                    hg[:, :])
                            sv.done(ins, sHM)
                            vector.wait_ge(sEO, 2 * (it + 1))
                            for nb in range(2):
                                ins = vector.scalar_tensor_tensor(
                                    acc[:, D * tt + 512 * nb:D * tt + 512 * (nb + 1)],
                                    PF[:, DSUB[nb][0]:DSUB[nb][0] + 512],
                                    comb[:, 8 * tt + e:8 * tt + e + 1],
                                    acc[:, D * tt + 512 * nb:D * tt + 512 * (nb + 1)],
                                    ALU.mult, ALU.add)
                                if nb == 1:
                                    sv.done(ins, sAC)
                                else:
                                    sv.do(ins)
                    bar.hit()

            @block.scalar
            def _(scalar):
                bar = Bar(scalar, sBAR)
                ss = Ser(scalar, sSS)
                bar.hit()
                for grp in range(NG):
                    for tt in range(GT):
                        g = GT * grp + tt
                        scalar.wait_ge(sLD, 16 * (g + 1))
                        ln_s1(scalar, ss, xt[g % 2], sq, stat)
                        scalar.sem_inc(sSTD, 1)
                        scalar.wait_ge(sVAR, g + 1)
                        ln_s2(scalar, ss, stat)
                        scalar.sem_inc(sSTD2, 1)
                        for k in range(8):
                            zt = 8 * g + k
                            scalar.wait_ge(sZT, zt + 1)
                            ins = scalar.activation(
                                zTf[:, 128 * k:128 * (k + 1)],
                                PTF[:, :], AF.Copy)
                            ss.done(ins, sZE)
                    bar.hit()
                    for u in range(NU):
                        for tt in range(GT):
                            it = GT * (NU * grp + u) + tt
                            scalar.wait_ge(sGP, 3 * (it + 1))
                            ins = scalar.activation(hg[:, :], PF[:, 0:IH],
                                                    AF.Silu)
                            ss.done(ins, sSI)
                            for k in range(NIC):
                                th = NIC * it + k
                                scalar.wait_ge(sTH, th + 1)
                                ins = scalar.activation(
                                    hT[:, 128 * k:128 * (k + 1)],
                                    PT[th % 2][:, :], AF.Copy)
                                ss.done(ins, sTHE)
                    bar.hit()

            @block.tensor
            def _(tensor):
                bar = Bar(tensor, sBAR)
                bar.hit()
                for grp in range(NG):
                    for tt in range(GT):
                        g = GT * grp + tt
                        tensor.wait_ge(sLN, g + 1)
                        for k in range(8):
                            tr = 8 * g + k
                            if tr >= 2:
                                tensor.wait_ge(sTR, tr - 1)
                            ins = tensor.transpose(PT[tr % 2][:, :],
                                                   lnb[g % 2][:, 128 * k:128 * (k + 1)],
                                                   ident[:, :])
                            ins.then_inc(sTE, 1)
                        tensor.wait_ge(sZF, g + 1)
                        for k in range(8):
                            zt = 8 * g + k
                            if zt >= 1:
                                tensor.wait_ge(sZE, zt)
                            ins = tensor.transpose(PTF[:, :],
                                                   zf[:, 128 * k:128 * (k + 1)],
                                                   identf[:, :])
                            ins.then_inc(sZT, 1)
                        tensor.wait_ge(sZE, 8 * (g + 1))
                        if g >= 1:
                            tensor.wait_ge(sTP2, g)
                        for chd in range(8):
                            ins = tensor.matmul(
                                PF[:, 1408:1416],
                                zTf[:, 128 * chd:128 * (chd + 1)],
                                rwf[:, 8 * chd:8 * (chd + 1)],
                                start=(chd == 0), stop=(chd == 7))
                        ins.then_inc(sRT, 1)
                    bar.hit(wait=(sRT, GT * (grp + 1)))
                    for u in range(NU):
                        gu = NU * grp + u
                        tensor.wait_ge(sCV, 27 * (gu + 1))
                        for tt in range(GT):
                            it = GT * gu + tt
                            if it >= 1:
                                tensor.wait_ge(sHM, it)
                            for nb in range(3):
                                off, wid = GSUB[nb]
                                for chd in range(8):
                                    ins = tensor.matmul(
                                        PF[:, off:off + wid],
                                        lnTg[:, GT * 128 * chd + 128 * tt:
                                             GT * 128 * chd + 128 * (tt + 1)],
                                        wg[:, IH * chd + off:IH * chd + off + wid],
                                        start=(chd == 0), stop=(chd == 7))
                                ins.then_inc(sGP, 1)
                            tensor.wait_ge(sSI, it + 1)
                            for nb in range(3):
                                off, wid = GSUB[nb]
                                for chd in range(8):
                                    ins = tensor.matmul(
                                        PF[:, off:off + wid],
                                        lnTg[:, GT * 128 * chd + 128 * tt:
                                             GT * 128 * chd + 128 * (tt + 1)],
                                        wu[:, IH * chd + off:IH * chd + off + wid],
                                        start=(chd == 0), stop=(chd == 7))
                                ins.then_inc(sUP, 1)
                            tensor.wait_ge(sHM, it + 1)
                            for k in range(NIC):
                                th = NIC * it + k
                                if th >= 2:
                                    tensor.wait_ge(sTHE, th - 1)
                                ins = tensor.transpose(
                                    PT[th % 2][:, :],
                                    hsb[:, 128 * k:128 * (k + 1)],
                                    ident[:, :])
                                ins.then_inc(sTH, 1)
                            tensor.wait_ge(sTHE, NIC * (it + 1))
                            if it >= 1:
                                tensor.wait_ge(sAC, it)
                            for nb in range(2):
                                off, wid = DSUB[nb]
                                for ic in range(NIC):
                                    ins = tensor.matmul(
                                        PF[:, off:off + wid],
                                        hT[:, 128 * ic:128 * (ic + 1)],
                                        wd[:, D * ic + 512 * nb:
                                           D * ic + 512 * (nb + 1)],
                                        start=(ic == 0), stop=(ic == NIC - 1))
                                ins.then_inc(sEO, 1)
                    bar.hit(wait=(sEO, 2 * GT * NU * (grp + 1)))

            @block.gpsimd
            def _(gpsimd):
                bar = Bar(gpsimd, sBAR)
                bar.hit()
                for grp in range(NG):
                    bar.hit()
                    gpsimd.wait_ge(sAC, GT * NU * (grp + 1))
                    for tt in range(GT):
                        g = GT * grp + tt
                        gpsimd.dma_start(out=dst[128 * g:128 * (g + 1), :],
                                         in_=acc[:, D * tt:D * (tt + 1)]
                                         ).then_inc(sST, 16)
                    gpsimd.wait_ge(sST, 16 * GT * (grp + 1))
                    bar.hit()

    c.boundary()


_COPY_N = [0]


def st_copy(c, src, dst):
    nc = c.nc
    sem = c.sem
    sLD, sCP, sST = sem("sLD"), sem("sCP"), sem("sST")
    _COPY_N[0] += 1
    u = _COPY_N[0]
    with ExitStack() as ctx:
        t = [ctx.enter_context(nc.sbuf_tensor(f"cp{u}_t{i}", [128, D], F32))
             for i in range(2)]
        with nc.Block() as block:
            @block.sync
            def _(sync):
                for g in range(NT):
                    if g >= 2:
                        sync.wait_ge(sST, 16 * (g - 1))
                    sync.dma_start(out=t[g % 2][:, :],
                                   in_=src[128 * g:128 * (g + 1), :]).then_inc(sLD, 16)
                sync.wait_ge(sLD, 16 * NT)

            @block.gpsimd
            def _(gpsimd):
                for g in range(NT):
                    gpsimd.wait_ge(sLD, 16 * (g + 1))
                    gpsimd.dma_start(out=dst[128 * g:128 * (g + 1), :],
                                     in_=t[g % 2][:, :]).then_inc(sST, 16)
                gpsimd.wait_ge(sST, 16 * NT)
    c.boundary()


STAGE_FNS = {}


def build_program(stage_names, in_specs):
    nc = bass.Bass("TRN2", target_bir_lowering=False)
    es = ExitStack()
    c = Ctx(nc, es)
    for nm, (shape, dt) in in_specs.items():
        c.dram[nm] = nc.dram_tensor(nm, list(shape), dt, kind="ExternalInput").ap()
    xa = nc.dram_tensor("xa", [NT * 128, D], F32, kind="Internal").ap()
    xb = nc.dram_tensor("xb", [NT * 128, D], F32, kind="Internal").ap()
    y = nc.dram_tensor("y", [NT * 128, D], F32, kind="ExternalOutput").ap()
    c.xa, c.xb, c.y = xa, xb, y

    with es:
        c.ident = es.enter_context(nc.sbuf_tensor("g_ident", [128, 128], BF16))
        c.identf = es.enter_context(nc.sbuf_tensor("g_identf", [128, 128], F32))
        c.onesf = es.enter_context(nc.sbuf_tensor("g_onesf", [1, 128], F32))
        c.onesb = es.enter_context(nc.sbuf_tensor("g_onesb", [1, 128], BF16))
        c.mdiag = es.enter_context(nc.sbuf_tensor("g_mdiag", [128, 128], F32))
        c.tril = es.enter_context(nc.sbuf_tensor("g_tril", [128, 128], BF16))

        preamble(nc)
        s0 = c.sem("sLD")
        nc.sync.dma_start(out=c.ident[:, :], in_=c.dram["cid"][:, :]).then_inc(s0, 16)
        nc.sync.dma_start(out=c.onesf[:, :], in_=c.dram["conesf"][:, :]).then_inc(s0, 16)
        nc.sync.dma_start(out=c.onesb[:, :], in_=c.dram["conesb"][:, :]).then_inc(s0, 16)
        nc.sync.dma_start(out=c.mdiag[:, :], in_=c.dram["cmdiag"][:, :]).then_inc(s0, 16)
        nc.sync.dma_start(out=c.tril[:, :], in_=c.dram["ctril"][:, :]).then_inc(s0, 16)
        nc.sync.dma_start(out=c.identf[:, :], in_=c.dram["cidf"][:, :]).then_inc(s0, 16)
        nc.sync.wait_ge(s0, 96)
        c.boundary()

        bufs = [c.dram["xin"], xa, xb]
        cur = 0
        for i, nm in enumerate(stage_names):
            srcb = bufs[cur]
            dstb = y if i == len(stage_names) - 1 else (xa if cur != 1 else xb)
            STAGE_FNS[nm](c, srcb, dstb)
            cur = 1 if dstb is xa else (2 if dstb is xb else 0)
    return nc


STAGE_FNS["l0_attn"] = st_l0_attn
STAGE_FNS["l0_ffn"] = st_l0_ffn
STAGE_FNS["copy"] = st_copy
STAGE_FNS["l1_attn"] = st_l1_attn
STAGE_FNS["l1_ffn"] = st_l1_ffn
STAGE_FNS["l2_attn"] = st_l2_attn
STAGE_FNS["l2_ffn"] = st_l2_ffn
STAGE_FNS["l3_attn"] = st_l3_attn
STAGE_FNS["l3_ffn"] = st_l3_ffn

ALL_STAGES = ["l0_attn", "l0_ffn", "l1_attn", "l1_ffn",
              "l2_attn", "l2_ffn", "l3_attn", "l3_ffn"]

WEIGHT_NAMES = [
    "L0_attn_w_q", "L0_attn_w_k", "L0_attn_w_v", "L0_attn_w_u", "L0_attn_b_u",
]


def stage_weight_names(stage_names):
    pref = {"l0_attn": ["L0_attn_w_q", "L0_attn_w_k", "L0_attn_w_v",
                        "L0_attn_w_u", "L0_attn_b_u"],
            "l0_ffn": ["L0_ffn_w_gate", "L0_ffn_w_up", "L0_ffn_w_down"],
            "copy": [],
            "l1_attn": ["L1_attn_decay", "L1_attn_W", "L1_attn_b",
                        "L1_attn_gamma", "L1_attn_beta"],
            "l1_ffn": ["L1_ffn_gate_w", "L1_ffn_gate_b", "L1_ffn_w1",
                       "L1_ffn_b1", "L1_ffn_w2", "L1_ffn_b2"],
            "l2_attn": ["L2_attn_Wqk", "L2_attn_bqk", "L2_attn_gamma",
                        "L2_attn_beta", "L2_attn_Wv", "L2_attn_bv",
                        "L2_attn_rel_emb"],
            "l2_ffn": ["L2_ffn_router_W", "L2_ffn_router_b", "L2_ffn_gate_bias",
                       "L2_ffn_exp_Wup", "L2_ffn_exp_bup", "L2_ffn_exp_Wg",
                       "L2_ffn_exp_bg", "L2_ffn_exp_Wdown", "L2_ffn_exp_bdown",
                       "L2_ffn_share_Wup", "L2_ffn_share_bup", "L2_ffn_share_Wg",
                       "L2_ffn_share_bg", "L2_ffn_share_Wdown",
                       "L2_ffn_share_bdown", "L2_ffn_norm_w"],
            "l3_attn": ["L3_attn_W_DQ", "L3_attn_W_UQ", "L3_attn_W_DKV",
                        "L3_attn_W_UK", "L3_attn_W_UV", "L3_attn_W_QR",
                        "W_KR4", "L3_attn_W_O"],
            "l3_ffn": ["L3_ffn_router_wT", "L3_ffn_gate_wT", "L3_ffn_up_wT",
                       "L3_ffn_down_wT"]}
    out = []
    for s in stage_names:
        out += pref[s]
    return out


def _run_once(stage_names, x, inputs, trace=False):
    consts = HOST_CONSTS
    wn = stage_weight_names(stage_names)
    in_specs = {"xin": ((NT * 128, D), F32)}
    host_w = {}
    for nm in wn:
        if nm == "W_KR4":
            arr = np.ascontiguousarray(
                np.tile(np.asarray(inputs["L3_attn_W_KR"]), (1, 4)))
        elif nm == "L3_ffn_router_wT":
            arr = np.ascontiguousarray(np.asarray(inputs["L3_ffn_router_w"]).T)
        elif nm in ("L3_ffn_gate_wT", "L3_ffn_up_wT", "L3_ffn_down_wT"):
            arr = np.ascontiguousarray(
                np.asarray(inputs[nm[:-1]]).transpose(0, 2, 1))
        else:
            arr = np.ascontiguousarray(np.asarray(inputs[nm]))
        if arr.ndim == 0:
            arr = arr.reshape(1, 1)
        host_w[nm] = arr
        dt = I32 if arr.dtype == np.int32 else F32
        in_specs[nm] = (arr.shape, dt)
    for nm, arr in consts.items():
        in_specs[nm] = (arr.shape, BF16 if arr.dtype == BF else F32)

    nc = build_program(stage_names, in_specs)

    in_maps = []
    for core in range(8):
        m = {"xin": x[core]}
        m.update(host_w)
        m.update(consts)
        in_maps.append(m)
    res = run_bass_kernel_spmd(nc, in_maps, core_ids=list(range(8)), trace=trace)
    return [r["y"] for r in res.results]


def run_chain(stage_names, inputs, trace=False, split=False):
    x = np.ascontiguousarray(np.asarray(inputs["L0_attn_x"], dtype=np.float32))
    shards = [x[4 * core:4 * core + 4].reshape(NT * 128, D) for core in range(8)]
    if split:
        for nm in stage_names:
            shards = _run_once([nm], shards, inputs)
    else:
        shards = _run_once(stage_names, shards, inputs, trace=trace)
    return np.concatenate([s.reshape(4, S, D) for s in shards], axis=0)


def kernel(**inputs):
    return run_chain(ALL_STAGES, inputs)


# version 03 (final)
# speedup vs baseline: 1.3629x; 1.3629x over previous
import numpy as np
import ml_dtypes
import math
from contextlib import ExitStack

import concourse.bass as bass
import concourse.mybir as mybir
from concourse.bass_utils import run_bass_kernel_spmd

F32 = mybir.dt.float32
BF16 = mybir.dt.bfloat16
I32 = mybir.dt.int32
AF = mybir.ActivationFunctionType
ALU = mybir.AluOpType
AX = mybir.AxisListType

SEMR = range(155, 256)
D = 1024
S = 2048
NSEQ = 4
TT = 16
NT = NSEQ * TT
NEG = -1.0e30

BF = ml_dtypes.bfloat16


def host_consts():
    c = {}
    c["cid"] = np.eye(128, dtype=BF)
    c["cidf"] = np.eye(128, dtype=np.float32)
    md = np.zeros((128, 128), dtype=np.float32)
    md[np.triu_indices(128, 1)] = NEG
    c["cmdiag"] = md
    c["cmdiagb"] = md.astype(BF)
    c["ctril"] = np.tril(np.ones((128, 128), dtype=np.float32)).astype(BF)
    c["conesf"] = np.ones((1, 128), dtype=np.float32)
    c["conesb"] = np.ones((1, 128), dtype=BF)
    DHR = 32
    t = np.arange(S, dtype=np.float32)
    inv = 1.0 / 10000.0 ** (np.arange(0, DHR, 2, dtype=np.float32) / DHR)
    freqs = t[:, None] * inv[None, :]
    emb = np.concatenate([freqs, freqs], -1)
    rope = (np.cos(emb) + np.sin(emb)).astype(np.float32)
    c["cropet"] = np.ascontiguousarray(rope.T[np.arange(128) % 32])
    NB, MAXD = 32, 128
    n = np.arange(S)
    max_exact = NB // 2
    nf = np.maximum(n, 1).astype(np.float32)
    vil = max_exact + (np.log(nf / max_exact) / math.log(MAXD / max_exact)
                       ).astype(np.float32)
    vil = max_exact + (np.log(nf / max_exact) / math.log(MAXD / max_exact)
                       * (NB - max_exact)).astype(np.int32)
    vil = np.minimum(vil, NB - 1)
    buck = np.where(n < max_exact, n, vil)
    oh = np.zeros((NB, 2 * S), dtype=np.float32)
    for t_ in range(S):
        oh[buck[2047 - t_], t_] = 1.0
    c["cohrev"] = oh
    return c


HOST_CONSTS = host_consts()


class Ser:

    def __init__(self, eng, sem):
        self.eng, self.sem, self.n = eng, sem, 0

    def do(self, ins):
        self.n += 1
        ins.then_inc(self.sem, 1)
        self.eng.wait_ge(self.sem, self.n)
        return ins

    def done(self, ins, milestone, k=1):
        self.do(ins)
        self.eng.sem_inc(milestone, k)
        return ins


class Bar:

    def __init__(self, eng, sem):
        self.eng, self.sem, self.p = eng, sem, 0

    def hit(self, wait=None):
        self.p += 1
        if wait is not None:
            self.eng.wait_ge(wait[0], wait[1])
        self.eng.sem_inc(self.sem, 1)
        self.eng.wait_ge(self.sem, 5 * self.p)


class Ctx:

    def __init__(self, nc, es):
        self.nc = nc
        self.es = es
        self.dram = {}
        self.sems = {}

    def sem(self, name):
        if name not in self.sems:
            self.sems[name] = self.es.enter_context(self.nc.semaphore(name))
        return self.sems[name]

    def boundary(self):
        nc = self.nc
        nc.all_engine_barrier()
        nc.gpsimd.dma_reset(SEMR)
        nc.gpsimd.sem_clear(SEMR)
        nc.all_engine_barrier()


def preamble(nc):
    nc.gpsimd.dma_reset(SEMR)
    for e in (nc.gpsimd, nc.vector, nc.scalar, nc.tensor, nc.sync):
        e.sem_clear(SEMR)
    nc.all_engine_barrier()


def ln_s1(scalar, ss, xt, junk, stat):
    ss.do(scalar.activation(junk[:, :], xt[:, :], AF.Square, accum_out=stat[:, 2:3]))
    ss.do(scalar.activation(junk[:, :], xt[:, :], AF.Copy, accum_out=stat[:, 0:1]))
    a3 = scalar.activation(stat[:, 1:2], stat[:, 0:1], AF.Copy, scale=1.0 / D)
    ss.do(a3)
    return a3


def ln_s2(scalar, ss, stat):
    a4 = scalar.activation(stat[:, 4:5], stat[:, 3:4], AF.Sqrt)
    ss.do(a4)
    return a4


def ln_v1(vector, sv, stat, eps):
    sv.do(vector.tensor_mul(stat[:, 6:7], stat[:, 1:2], stat[:, 1:2]))
    sv.do(vector.scalar_tensor_tensor(stat[:, 3:4], stat[:, 2:3], 1.0 / D,
                                      stat[:, 6:7], ALU.mult, ALU.subtract))
    ins = vector.tensor_scalar_add(stat[:, 3:4], stat[:, 3:4], eps[:, 0:1])
    sv.do(ins)
    return ins


def ln_v2(vector, sv, stat, x, out_bf):
    sv.do(vector.reciprocal(stat[:, 5:6], stat[:, 4:5]))
    sv.do(vector.tensor_mul(stat[:, 7:8], stat[:, 1:2], stat[:, 5:6]))
    v5 = vector.tensor_scalar(out_bf[:, :], x[:, :], stat[:, 5:6], stat[:, 7:8],
                              ALU.mult, ALU.subtract)
    sv.do(v5)
    return v5


def st_l0_attn(c, src, dst):
    nc = c.nc
    w = c.dram
    sem = c.sem
    names = ["sLD", "sW", "sCV", "sSTD", "sVAR", "sSTD2", "sLN", "sTR", "sTE", "sVP",
             "sVE", "sQK", "sQE", "sSC", "sMX", "sEX", "sRC", "sSCL", "sTA",
             "sTAE", "sAV", "sAE", "sU", "sOUT", "sST", "sX2", "sBAR",
             "sSV", "sSS", "sSG", "sDG"]
    (sLD, sW, sCV, sSTD, sVAR, sSTD2, sLN, sTR, sTE, sVP, sVE, sQK, sQE, sSC, sMX,
     sEX, sRC, sSCL, sTA, sTAE, sAV, sAE, sU, sOUT, sST, sX2, sBAR, sSV, sSS,
     sSG, sDG) = (sem(x) for x in names)

    itemsB = []
    tau_after = []
    tau = 0
    for s in range(NSEQ):
        for qt in range(TT):
            for h in range(16):
                H = (s * TT + qt) * 16 + h
                tau += qt + 1
                itemsB.append((s, qt, h, H, (qt + 1) * 128, s * TT + qt, H // 2))
                tau_after.append(tau)

    def tau_start(H):
        return tau_after[H] - (itemsB[H][1] + 1)

    with ExitStack() as ctx:
        sb = lambda name, shape, dt: ctx.enter_context(nc.sbuf_tensor(name, shape, dt))
        xt = [sb(f"l0a_xt{i}", [128, D], F32) for i in range(2)]
        outt = [sb(f"l0a_out{i}", [128, D], F32) for i in range(2)]
        xc = outt[1]
        sq = outt[0]
        stat = sb("l0a_st", [128, 8], F32)
        eps = sb("l0a_eps", [128, 1], F32)
        lnb = [sb(f"l0a_lnb{i}", [128, D], BF16) for i in range(2)]
        lnT = sb("l0a_lnT", [128, 8 * 512], BF16)
        Vt = sb("l0a_V", [128, 16 * D], BF16)
        QT = sb("l0a_QT", [128, 8 * S], BF16)
        KT = sb("l0a_KT", [128, 8 * S], BF16)
        att = [sb(f"l0a_att{i}", [128, S], BF16) for i in range(2)]
        attT = [sb(f"l0a_attT{i}", [128, 128], BF16) for i in range(2)]
        dgm = [sb(f"l0a_dgm{i}", [128, 128], BF16) for i in range(2)]
        avh = sb("l0a_avh", [128, 8 * 128], BF16)
        sm = sb("l0a_sm", [128, 8], F32)
        wq = sb("l0a_wq", [128, 8 * D], BF16)
        wk = sb("l0a_wk", [128, 8 * D], BF16)
        wv = sb("l0a_wv", [128, 8 * D], BF16)
        wu = sb("l0a_wu", [128, 8 * D], BF16)
        wstg = [sb("l0a_wstg0", [128, D], F32)]
        buf = xt[0]
        bu_bc = sb("l0a_bubc", [128, D], BF16)
        PF = ctx.enter_context(nc.psum_tensor("l0a_pf", [128, 3072], F32))
        PT = [ctx.enter_context(nc.psum_tensor(f"l0a_pt{i}", [128, 128], BF16))
              for i in range(2)]
        ident, onesf, mdiag = c.ident, c.onesf, c.mdiag

        wlist = [(wq, "L0_attn_w_q"), (wk, "L0_attn_w_k"),
                 (wv, "L0_attn_w_v"), (wu, "L0_attn_w_u")]

        with nc.Block() as block:
            @block.sync
            def _(sync):
                bar = Bar(sync, sBAR)
                nw = 0
                for wsb, nm in wlist:
                    for ch in range(8):
                        if nw >= 1:
                            sync.wait_ge(sCV, nw)
                        sync.dma_start(out=wstg[0][:, :],
                                       in_=w[nm][128 * ch:128 * (ch + 1), :]
                                       ).then_inc(sW, 16)
                        nw += 1
                sync.dma_start(out=buf[0:1, :],
                               in_=w["L0_attn_b_u"][None, :]).then_inc(sW, 16)
                sync.wait_ge(sCV, 32)
                sync.wait_ge(sW, 16 * 33)
                bar.hit()
                for s in range(NSEQ):
                    for t in range(TT):
                        g = TT * s + t
                        if g >= 2:
                            sync.wait_ge(sLN, g - 1)
                        sync.dma_start(out=xt[g % 2][:, :],
                                       in_=src[128 * g:128 * (g + 1), :]
                                       ).then_inc(sLD, 16)
                    sync.wait_ge(sLD, 16 * TT * (s + 1))
                    bar.hit()
                    for qt in range(TT):
                        g = TT * s + qt
                        qtg = g
                        if qtg >= 2:
                            sync.wait_ge(sOUT, qtg - 1)
                        sync.dma_start(out=xt[qt % 2][:, :],
                                       in_=src[128 * g:128 * (g + 1), :]
                                       ).then_inc(sX2, 16)
                    sync.wait_ge(sX2, 16 * TT * (s + 1))
                    bar.hit()

            @block.vector
            def _(vector):
                bar = Bar(vector, sBAR)
                sv = Ser(vector, sSV)
                sv.do(vector.memset(eps[:, :], 1.0e-5))
                nw = 0
                for wsb, nm in wlist:
                    for ch in range(8):
                        vector.wait_ge(sW, 16 * (nw + 1))
                        ins = vector.tensor_copy(wsb[:, D * ch:D * (ch + 1)],
                                                 wstg[0][:, :])
                        sv.done(ins, sCV)
                        nw += 1
                vector.wait_ge(sQK, 2)
                sv.do(vector.tensor_copy(bu_bc[:, 0:512], PF[:, 2048:2560]))
                sv.do(vector.tensor_copy(bu_bc[:, 512:1024], PF[:, 2560:3072]))
                bar.hit()
                for s in range(NSEQ):
                    for t in range(TT):
                        g = TT * s + t
                        vector.wait_ge(sSTD, g + 1)
                        ln_v1(vector, sv, stat, eps)
                        vector.sem_inc(sVAR, 1)
                        vector.wait_ge(sSTD2, g + 1)
                        if g >= 2:
                            vector.wait_ge(sTR, 8 * (g - 1))
                        ln_v2(vector, sv, stat, xt[g % 2], lnb[g % 2])
                        vector.sem_inc(sLN, 1)
                        for k in range(8):
                            tr = 8 * g + k
                            vector.wait_ge(sTR, tr + 1)
                            ins = vector.tensor_copy(
                                lnT[:, 512 * k + 128 * (t % 4):
                                    512 * k + 128 * (t % 4 + 1)],
                                PT[tr % 2][:, :])
                            sv.done(ins, sTE)
                    bar.hit()
                    for (s_, qt, h, H, L, qtg, pr) in itemsB[s * 256:(s + 1) * 256]:
                        vector.wait_ge(sSC, H + 1)
                        sv.do(vector.reduce_max(sm[:, 0:1], PF[:, 0:L], axis=AX.X))
                        ins = vector.tensor_scalar_mul(sm[:, 1:2], sm[:, 0:1],
                                                       -1.0)
                        sv.done(ins, sMX)
                        vector.wait_ge(sEX, H + 1)
                        sv.do(vector.reciprocal(sm[:, 3:4], sm[:, 2:3]))
                        ins = vector.tensor_scalar_mul(att[h % 2][:, 0:L],
                                                       att[h % 2][:, 0:L],
                                                       sm[:, 3:4])
                        sv.done(ins, sDG)
                        t2 = tau_start(H)
                        for kt in range(qt + 1):
                            vector.wait_ge(sTA, t2 + 1)
                            ins = vector.tensor_copy(attT[t2 % 2][:, :],
                                                     PT[t2 % 2][:, :])
                            sv.done(ins, sTAE)
                            t2 += 1
                        if h == 15:
                            vector.wait_ge(sU, 2 * (qtg + 1))
                            vector.wait_ge(sX2, 16 * (qt + 1) + 16 * TT * s)
                            if qtg >= 2:
                                vector.wait_ge(sST, 16 * (qtg - 1))
                            for nb in range(2):
                                ins = vector.scalar_tensor_tensor(
                                    outt[qt % 2][:, 512 * nb:512 * (nb + 1)],
                                    PF[:, 2048 + 512 * nb:2048 + 512 * (nb + 1)],
                                    1.0, xt[qt % 2][:, 512 * nb:512 * (nb + 1)],
                                    ALU.mult, ALU.add)
                                sv.do(ins)
                            ins = vector.tensor_add(outt[qt % 2][:, :],
                                                    outt[qt % 2][:, :], bu_bc[:, :])
                            sv.done(ins, sOUT)
                    bar.hit()

            @block.scalar
            def _(scalar):
                bar = Bar(scalar, sBAR)
                ss = Ser(scalar, sSS)
                bar.hit()
                for s in range(NSEQ):
                    for t in range(TT):
                        g = TT * s + t
                        scalar.wait_ge(sLD, 16 * (g + 1))
                        ln_s1(scalar, ss, xt[g % 2], sq, stat)
                        scalar.sem_inc(sSTD, 1)
                        scalar.wait_ge(sVAR, g + 1)
                        ln_s2(scalar, ss, stat)
                        scalar.sem_inc(sSTD2, 1)
                        for nb in range(2):
                            scalar.wait_ge(sVP, 2 * g + nb + 1)
                            ins = scalar.activation(
                                Vt[:, D * t + 512 * nb:D * t + 512 * (nb + 1)],
                                PF[:, 2048 + 512 * nb:2048 + 512 * (nb + 1)],
                                AF.Copy)
                            ss.done(ins, sVE)
                        if t % 4 == 3:
                            gb = t // 4
                            for wi in range(2):
                                dstT = QT if wi == 0 else KT
                                for hc in range(8):
                                    q = (s * 4 + gb) * 16 + wi * 8 + hc
                                    scalar.wait_ge(sQK, q + 1 + 2)
                                    ins = scalar.activation(
                                        dstT[:, 2048 * hc + 512 * gb:
                                             2048 * hc + 512 * (gb + 1)],
                                        PF[:, 512 * (q % 2):512 * (q % 2 + 1)],
                                        AF.Copy, scale=0.125)
                                    ss.done(ins, sQE)
                    bar.hit()
                    for (s_, qt, h, H, L, qtg, pr) in itemsB[s * 256:(s + 1) * 256]:
                        scalar.wait_ge(sMX, H + 1)
                        if H >= 2:
                            scalar.wait_ge(sTA, tau_after[H - 2])
                        ins = scalar.activation(att[h % 2][:, 0:L], PF[:, 0:L],
                                                AF.Exp, bias=sm[:, 1:2],
                                                scale=1.0,
                                                accum_out=sm[:, 2:3])
                        ss.done(ins, sEX)
                        if h % 2 == 1:
                            hc = h // 2
                            scalar.wait_ge(sAV, 2 * (pr + 1))
                            if qtg >= 1:
                                scalar.wait_ge(sU, 2 * qtg)
                            ins = scalar.activation(
                                avh[:, 128 * hc:128 * (hc + 1)],
                                PF[:, 2048 + 512 * (pr % 2):
                                   2048 + 512 * (pr % 2) + 128],
                                AF.Copy)
                            ss.done(ins, sAE)
                    bar.hit()

            @block.tensor
            def _(tensor):
                bar = Bar(tensor, sBAR)
                tensor.wait_ge(sW, 16 * 33)
                tensor.matmul(PF[:, 2048:2560], onesf[0:1, :], buf[0:1, 0:512],
                              start=True, stop=True).then_inc(sQK, 1)
                ins = tensor.matmul(PF[:, 2560:3072], onesf[0:1, :],
                                    buf[0:1, 512:1024], start=True, stop=True)
                ins.then_inc(sQK, 1)
                bar.hit(wait=(sQK, 2))
                for s in range(NSEQ):
                    for t in range(TT):
                        g = TT * s + t
                        tensor.wait_ge(sLN, g + 1)
                        if t % 4 == 0 and g > 0:
                            pass
                        for k in range(8):
                            tr = 8 * g + k
                            if tr >= 2:
                                tensor.wait_ge(sTE, tr - 1)
                            ins = tensor.transpose(PT[tr % 2][:, :],
                                                   lnb[g % 2][:, 128 * k:128 * (k + 1)],
                                                   ident[:, :])
                            ins.then_inc(sTR, 1)
                        tensor.wait_ge(sTE, 8 * (g + 1))
                        tensor.wait_ge(sCV, 24)
                        for nb in range(2):
                            if g >= 1:
                                tensor.wait_ge(sVE, 2 * (g - 1) + nb + 1)
                            for ch in range(8):
                                ins = tensor.matmul(
                                    PF[:, 2048 + 512 * nb:2048 + 512 * (nb + 1)],
                                    lnT[:, 512 * ch + 128 * (t % 4):
                                        512 * ch + 128 * (t % 4 + 1)],
                                    wv[:, D * ch + 512 * nb:D * ch + 512 * (nb + 1)],
                                    start=(ch == 0), stop=(ch == 7))
                            ins.then_inc(sVP, 1)
                        if t % 4 == 3:
                            gb = t // 4
                            tensor.wait_ge(sCV, 16)
                            for wi in range(2):
                                wsb = wq if wi == 0 else wk
                                for hc in range(8):
                                    q = (s * 4 + gb) * 16 + wi * 8 + hc
                                    if q >= 2:
                                        tensor.wait_ge(sQE, q - 1)
                                    for ch in range(8):
                                        ins = tensor.matmul(
                                            PF[:, 512 * (q % 2):512 * (q % 2 + 1)],
                                            wsb[:, D * ch + 128 * hc:
                                                D * ch + 128 * (hc + 1)],
                                            lnT[:, 512 * ch:512 * (ch + 1)],
                                            start=(ch == 0), stop=(ch == 7))
                                    ins.then_inc(sQK, 1)
                    bar.hit(wait=(sQK, 2 + 64 * (s + 1)))
                    for (s_, qt, h, H, L, qtg, pr) in itemsB[s * 256:(s + 1) * 256]:
                        hc, hb = h // 2, (h % 2) * 64
                        tensor.wait_ge(sEX, H)
                        nkb = (L + 511) // 512
                        for kb in range(nkb):
                            nn = min(512, L - 512 * kb)
                            tensor.matmul(
                                PF[:, 512 * kb:512 * kb + nn],
                                QT[hb:hb + 64,
                                   2048 * hc + 128 * qt:2048 * hc + 128 * (qt + 1)],
                                KT[hb:hb + 64, 2048 * hc + 512 * kb:
                                   2048 * hc + 512 * kb + nn],
                                start=True, stop=(kb != nkb - 1))
                        ins = tensor.matmul(
                            PF[:, 128 * qt:128 * (qt + 1)],
                            ident[:, :], c.mdiagb[:, :],
                            start=False, stop=True, skip_group_check=True)
                        ins.then_inc(sSC, 1)
                        tensor.wait_ge(sDG, H + 1)
                        if h % 2 == 0 and pr >= 2:
                            tensor.wait_ge(sAE, pr - 1)
                        if h == 0 and qtg >= 1:
                            tensor.wait_ge(sOUT, qtg)
                        t2 = tau_start(H)
                        for kt in range(qt + 1):
                            if t2 >= 2:
                                tensor.wait_ge(sTAE, t2 - 1)
                            ins = tensor.transpose(PT[t2 % 2][:, :],
                                                   att[h % 2][:, 128 * kt:128 * (kt + 1)],
                                                   ident[:, :])
                            ins.then_inc(sTA, 1)
                            tensor.wait_ge(sTAE, t2 + 1)
                            ins = tensor.matmul(
                                PF[hb:hb + 64,
                                   2048 + 512 * (pr % 2):2048 + 512 * (pr % 2) + 128],
                                Vt[:, D * kt + 64 * h:D * kt + 64 * (h + 1)],
                                attT[t2 % 2][:, :],
                                start=(kt == 0), stop=(kt == qt))
                            t2 += 1
                        ins.then_inc(sAV, 1)
                        if h == 15:
                            tensor.wait_ge(sAE, 8 * (qtg + 1))
                            for nb in range(2):
                                for hc2 in range(8):
                                    ins = tensor.matmul(
                                        PF[:, 2048 + 512 * nb:2048 + 512 * (nb + 1)],
                                        avh[:, 128 * hc2:128 * (hc2 + 1)],
                                        wu[:, D * hc2 + 512 * nb:
                                           D * hc2 + 512 * (nb + 1)],
                                        start=(hc2 == 0), stop=(hc2 == 7))
                                ins.then_inc(sU, 1)
                    bar.hit(wait=(sU, 32 * (s + 1)))

            @block.gpsimd
            def _(gpsimd):
                bar = Bar(gpsimd, sBAR)
                sg = Ser(gpsimd, sSG)
                bar.hit()
                nst = 0
                for s in range(NSEQ):
                    bar.hit()
                    for (s_, qt, h, H, L, qtg, pr) in itemsB[s * 256:(s + 1) * 256]:
                        if h == 15:
                            gpsimd.wait_ge(sOUT, qtg + 1)
                            g = TT * s + qt
                            gpsimd.dma_start(out=dst[128 * g:128 * (g + 1), :],
                                             in_=outt[qt % 2][:, :]).then_inc(sST, 16)
                            nst += 1
                    gpsimd.wait_ge(sST, 16 * nst)
                    bar.hit()

    c.boundary()


def st_l0_ffn(c, src, dst):
    nc = c.nc
    w = c.dram
    sem = c.sem
    names = ["sLD", "sW", "sCV", "sTW", "sTWE", "sSTD", "sVAR", "sSTD2", "sLN",
             "sTR", "sTE", "sGP", "sUP", "sSIL", "sHM", "sTH", "sTHE", "sDN",
             "sOUT", "sST", "sBAR", "sSV", "sSS", "sSG"]
    (sLD, sW, sCV, sTW, sTWE, sSTD, sVAR, sSTD2, sLN, sTR, sTE, sGP, sUP,
     sSIL, sHM, sTH, sTHE, sDN, sOUT, sST, sBAR, sSV, sSS, sSG) = (
        sem(x) for x in names)

    HID = 3072
    prep = []
    for wi, nm in enumerate(["L0_ffn_w_gate", "L0_ffn_w_up"]):
        for r in range(24):
            blocks = [(cb, HID * cb + 128 * r) for cb in range(8)]
            prep.append((nm, 128 * r, 0, wi, blocks))
    for r in range(8):
        for p in range(3):
            blocks = [(cb, D * (8 * p + cb) + 128 * r) for cb in range(8)]
            prep.append(("L0_ffn_w_down", 128 * r, 1024 * p, 2, blocks))
    NPREP = len(prep)

    with ExitStack() as ctx:
        sb = lambda name, shape, dt: ctx.enter_context(nc.sbuf_tensor(name, shape, dt))
        xt = [sb(f"l0f_xt{i}", [128, D], F32) for i in range(2)]
        outt = [sb(f"l0f_out{i}", [128, D], F32) for i in range(2)]
        sq = sb("l0f_sq", [128, D], F32)
        stat = sb("l0f_st", [128, 8], F32)
        eps = sb("l0f_eps", [128, 1], F32)
        lnb = [sb(f"l0f_lnb{i}", [128, D], BF16) for i in range(2)]
        lnT = [sb(f"l0f_lnT{i}", [128, 8 * 128], BF16) for i in range(2)]
        gs = sb("l0f_gs", [128, 512], BF16)
        hsb = [sb(f"l0f_h{i}", [128, 512], BF16) for i in range(2)]
        hT = [sb(f"l0f_hT{i}", [128, 24 * 128], BF16) for i in range(2)]
        wgT = sb("l0f_wgT", [128, 8 * HID], BF16)
        wuT = sb("l0f_wuT", [128, 8 * HID], BF16)
        wdT = sb("l0f_wdT", [128, 24 * D], BF16)
        wstg = sb("l0f_wstg", [128, D], I32)
        wbst = sb("l0f_wbst", [128, D], BF16)
        tgts = [wgT, wuT, wdT]
        PF = ctx.enter_context(nc.psum_tensor("l0f_pf", [128, 2048], F32))
        PT = [ctx.enter_context(nc.psum_tensor(f"l0f_pt{i}", [128, 128], BF16))
              for i in range(2)]
        ident = c.ident

        with nc.Block() as block:
            @block.sync
            def _(sync):
                bar = Bar(sync, sBAR)
                for i, (nm, r0, c0, ti, blocks) in enumerate(prep):
                    if i >= 1:
                        sync.wait_ge(sCV, i)
                    sync.dma_start(out=wstg[:, :],
                                   in_=w[nm][r0:r0 + 128, c0:c0 + 1024]
                                   ).then_inc(sW, 16)
                sync.wait_ge(sCV, NPREP)
                bar.hit()
                for g in range(NT):
                    if g >= 2:
                        sync.wait_ge(sOUT, g - 1)
                    sync.dma_start(out=xt[g % 2][:, :],
                                   in_=src[128 * g:128 * (g + 1), :]
                                   ).then_inc(sLD, 16)
                sync.wait_ge(sLD, 16 * NT)
                bar.hit()

            @block.vector
            def _(vector):
                bar = Bar(vector, sBAR)
                sv = Ser(vector, sSV)
                sv.do(vector.memset(eps[:, :], 1.0e-5))
                for i in range(NPREP):
                    vector.wait_ge(sW, 16 * (i + 1))
                    if i >= 1:
                        vector.wait_ge(sTW, 8 * i)
                    ins = vector.tensor_scalar_mul(wbst[:, :], wstg[:, :],
                                                   1.0 / 127.0)
                    sv.done(ins, sCV)
                    nm, r0, c0, ti, blocks = prep[i]
                    for k, (cb, toff) in enumerate(blocks):
                        tr = 8 * i + k
                        vector.wait_ge(sTW, tr + 1)
                        ins = vector.tensor_copy(tgts[ti][:, toff:toff + 128],
                                                 PT[tr % 2][:, :])
                        sv.done(ins, sTWE)
                bar.hit()
                for g in range(NT):
                    vector.wait_ge(sSTD, g + 1)
                    ln_v1(vector, sv, stat, eps)
                    vector.sem_inc(sVAR, 1)
                    vector.wait_ge(sSTD2, g + 1)
                    if g >= 2:
                        vector.wait_ge(sTR, 8 * (g - 1))
                    ln_v2(vector, sv, stat, xt[g % 2], lnb[g % 2])
                    vector.sem_inc(sLN, 1)
                    if g >= 2:
                        vector.wait_ge(sGP, 6 * (g - 1))
                    for k in range(8):
                        tr = 8 * g + k
                        vector.wait_ge(sTE, tr + 1)
                        ins = vector.tensor_copy(lnT[g % 2][:, 128 * k:128 * (k + 1)],
                                                 PT[tr % 2][:, :])
                        sv.done(ins, sTR)
                    if g >= 2:
                        vector.wait_ge(sDN, 2 * (g - 1))
                    for hb in range(6):
                        vector.wait_ge(sSIL, 6 * g + hb + 1)
                        if 6 * g + hb >= 2:
                            vector.wait_ge(sTH, 4 * (6 * g + hb - 1))
                        ins = vector.tensor_tensor(hsb[hb % 2][:, :], gs[:, :],
                                                   PF[:, 512:1024], ALU.mult)
                        sv.done(ins, sHM)
                        for k in range(4):
                            th = (6 * g + hb) * 4 + k
                            vector.wait_ge(sTH, th + 1)
                            ins = vector.tensor_copy(
                                hT[g % 2][:, 512 * hb + 128 * k:512 * hb + 128 * (k + 1)],
                                PT[th % 2][:, :])
                            sv.done(ins, sTHE)
                    vector.wait_ge(sDN, 2 * (g + 1))
                    if g >= 2:
                        vector.wait_ge(sST, 16 * (g - 1))
                    for nb in range(2):
                        ins = vector.scalar_tensor_tensor(
                            outt[g % 2][:, 512 * nb:512 * (nb + 1)],
                            PF[:, 1024 + 512 * nb:1024 + 512 * (nb + 1)],
                            1.0, xt[g % 2][:, 512 * nb:512 * (nb + 1)],
                            ALU.mult, ALU.add)
                        sv.do(ins)
                    vector.sem_inc(sOUT, 1)
                bar.hit()

            @block.scalar
            def _(scalar):
                bar = Bar(scalar, sBAR)
                ss = Ser(scalar, sSS)
                bar.hit()
                for g in range(NT):
                    scalar.wait_ge(sLD, 16 * (g + 1))
                    ln_s1(scalar, ss, xt[g % 2], sq, stat)
                    scalar.sem_inc(sSTD, 1)
                    scalar.wait_ge(sVAR, g + 1)
                    ln_s2(scalar, ss, stat)
                    scalar.sem_inc(sSTD2, 1)
                    for hb in range(6):
                        scalar.wait_ge(sGP, 6 * g + hb + 1)
                        if hb >= 1:
                            scalar.wait_ge(sHM, 6 * g + hb)
                        elif g >= 1:
                            scalar.wait_ge(sHM, 6 * g)
                        ins = scalar.activation(gs[:, :], PF[:, 0:512], AF.Silu)
                        ss.done(ins, sSIL)
                bar.hit()

            @block.tensor
            def _(tensor):
                bar = Bar(tensor, sBAR)
                for i in range(NPREP):
                    tensor.wait_ge(sCV, i + 1)
                    for k in range(8):
                        tr = 8 * i + k
                        if tr >= 2:
                            tensor.wait_ge(sTWE, tr - 1)
                        ins = tensor.transpose(PT[tr % 2][:, :],
                                               wbst[:, 128 * k:128 * (k + 1)],
                                               ident[:, :])
                        ins.then_inc(sTW, 1)
                bar.hit(wait=(sTWE, 8 * NPREP))
                for g in range(NT):
                    tensor.wait_ge(sLN, g + 1)
                    if g >= 1:
                        tensor.wait_ge(sTHE, 24 * g)
                    for k in range(8):
                        tr = 8 * g + k
                        tensor.wait_ge(sTR, tr - 1) if tr >= 2 else None
                        ins = tensor.transpose(PT[tr % 2][:, :],
                                               lnb[g % 2][:, 128 * k:128 * (k + 1)],
                                               ident[:, :])
                        ins.then_inc(sTE, 1)
                    tensor.wait_ge(sTR, 8 * (g + 1))
                    for hb in range(6):
                        if 6 * g + hb >= 1:
                            tensor.wait_ge(sSIL, 6 * g + hb)
                            tensor.wait_ge(sHM, 6 * g + hb)
                        for ch in range(8):
                            tensor.matmul(
                                PF[:, 0:512],
                                lnT[g % 2][:, 128 * ch:128 * (ch + 1)],
                                wgT[:, HID * ch + 512 * hb:HID * ch + 512 * (hb + 1)],
                                start=(ch == 0), stop=(ch == 7))
                        ins = tensor.matmul(
                            PF[:, 512:1024],
                            lnT[g % 2][:, 0:128],
                            wuT[:, 512 * hb:512 * (hb + 1)],
                            start=True, stop=False)
                        for ch in range(1, 8):
                            ins = tensor.matmul(
                                PF[:, 512:1024],
                                lnT[g % 2][:, 128 * ch:128 * (ch + 1)],
                                wuT[:, HID * ch + 512 * hb:HID * ch + 512 * (hb + 1)],
                                start=False, stop=(ch == 7))
                        ins.then_inc(sGP, 1)
                        tensor.wait_ge(sGP, 6 * g + hb + 1)
                        tensor.wait_ge(sHM, 6 * g + hb + 1)
                        for k in range(4):
                            th = (6 * g + hb) * 4 + k
                            if th >= 2:
                                tensor.wait_ge(sTHE, th - 1)
                            ins = tensor.transpose(PT[th % 2][:, :],
                                                   hsb[hb % 2][:, 128 * k:128 * (k + 1)],
                                                   ident[:, :])
                            ins.then_inc(sTH, 1)
                        tensor.wait_ge(sTHE, 4 * (6 * g + hb + 1))
                        if hb == 0 and g >= 1:
                            tensor.wait_ge(sOUT, g)
                        for nb in range(2):
                            for k in range(4):
                                j = 4 * hb + k
                                ins = tensor.matmul(
                                    PF[:, 1024 + 512 * nb:1024 + 512 * (nb + 1)],
                                    hT[g % 2][:, 128 * j:128 * (j + 1)],
                                    wdT[:, D * j + 512 * nb:D * j + 512 * (nb + 1)],
                                    start=(hb == 0 and k == 0),
                                    stop=(hb == 5 and k == 3))
                            if hb == 5:
                                ins.then_inc(sDN, 1)
                    bar_last = ins
                bar.hit(wait=(sDN, 2 * NT))

            @block.gpsimd
            def _(gpsimd):
                bar = Bar(gpsimd, sBAR)
                bar.hit()
                for g in range(NT):
                    gpsimd.wait_ge(sOUT, g + 1)
                    gpsimd.dma_start(out=dst[128 * g:128 * (g + 1), :],
                                     in_=outt[g % 2][:, :]).then_inc(sST, 16)
                gpsimd.wait_ge(sST, 16 * NT)
                bar.hit()

    c.boundary()


def st_l1_attn(c, src, dst):
    nc = c.nc
    w = c.dram
    sem = c.sem
    names = ["sLD", "sW", "sCV", "sTW", "sTWE", "sBC", "sSTD", "sVAR", "sSTD2",
             "sLN", "sTE", "sTR", "sSCN", "sYP", "sYE", "sSTDy", "sVARy",
             "sSTD2y", "sLNy", "sOUT", "sST", "sX2", "sBAR", "sSV", "sSS", "sSG"]
    (sLD, sW, sCV, sTW, sTWE, sBC, sSTD, sVAR, sSTD2, sLN, sTE, sTR, sSCN,
     sYP, sYE, sSTDy, sVARy, sSTD2y, sLNy, sOUT, sST, sX2, sBAR, sSV, sSS,
     sSG) = (sem(x) for x in names)

    with ExitStack() as ctx:
        sb = lambda name, shape, dt: ctx.enter_context(nc.sbuf_tensor(name, shape, dt))
        xt = [sb(f"l1a_xt{i}", [128, D], F32) for i in range(2)]
        sq = sb("l1a_sq", [128, D], F32)
        stat = sb("l1a_st", [128, 8], F32)
        eps = sb("l1a_eps", [128, 1], F32)
        lnb = [sb(f"l1a_lnb{i}", [128, D], BF16) for i in range(2)]
        ZT = sb("l1a_ZT", [128, 8 * S], BF16)
        ST = sb("l1a_STT", [128, 8 * S], BF16)
        a_t = sb("l1a_at", [128, S], F32)
        asc = sb("l1a_asc", [128, 2], F32)
        dsc = sb("l1a_dsc", [1, 2], F32)
        WT = sb("l1a_WT", [128, 8 * D], BF16)
        wstg = sb("l1a_wstg", [128, D], F32)
        wbst = sb("l1a_wbst", [128, D], BF16)
        brow = sb("l1a_brow", [1, D], F32)
        grow = sb("l1a_grow", [1, D], F32)
        berow = sb("l1a_berow", [1, D], F32)
        gam_bc = sb("l1a_gam", [128, D], F32)
        bet_bc = sb("l1a_bet", [128, D], F32)
        Yt = [sb(f"l1a_Y{i}", [128, D], F32) for i in range(2)]
        Ynb = [sb(f"l1a_Yn{i}", [128, D], BF16) for i in range(2)]
        outt = [sb(f"l1a_out{i}", [128, D], F32) for i in range(2)]
        PF = ctx.enter_context(nc.psum_tensor("l1a_pf", [128, 2048], F32))
        PT = [ctx.enter_context(nc.psum_tensor(f"l1a_pt{i}", [128, 128], BF16))
              for i in range(2)]
        ident, onesf = c.ident, c.onesf

        with nc.Block() as block:
            @block.sync
            def _(sync):
                bar = Bar(sync, sBAR)
                for i in range(8):
                    if i >= 1:
                        sync.wait_ge(sCV, i)
                    sync.dma_start(out=wstg[:, :],
                                   in_=w["L1_attn_W"][128 * i:128 * (i + 1), :]
                                   ).then_inc(sW, 16)
                sync.dma_start(out=brow[0:1, :],
                               in_=w["L1_attn_b"][None, :]).then_inc(sW, 16)
                sync.dma_start(out=grow[0:1, :],
                               in_=w["L1_attn_gamma"][None, :]).then_inc(sW, 16)
                sync.dma_start(out=berow[0:1, :],
                               in_=w["L1_attn_beta"][None, :]).then_inc(sW, 16)
                dk = w["L1_attn_decay"]
                if dk.ndim == 1:
                    dk = dk.unsqueeze(0)
                sync.dma_start(out=dsc[0:1, 0:1],
                               in_=dk[0:1, 0:1]).then_inc(sW, 16)
                sync.wait_ge(sCV, 8)
                sync.wait_ge(sTWE, 64)
                sync.wait_ge(sBC, 5)
                bar.hit()
                for s in range(NSEQ):
                    for t in range(TT):
                        g = TT * s + t
                        if g >= 2:
                            sync.wait_ge(sLN, g - 1)
                        sync.dma_start(out=xt[g % 2][:, :],
                                       in_=src[128 * g:128 * (g + 1), :]
                                       ).then_inc(sLD, 16)
                    sync.wait_ge(sLD, 16 * TT * (s + 1))
                    bar.hit()
                    bar.hit()
                    for qt in range(TT):
                        g = TT * s + qt
                        if g >= 2:
                            sync.wait_ge(sOUT, g - 1)
                        sync.dma_start(out=xt[qt % 2][:, :],
                                       in_=src[128 * g:128 * (g + 1), :]
                                       ).then_inc(sX2, 16)
                    sync.wait_ge(sX2, 16 * TT * (s + 1))
                    bar.hit()

            @block.vector
            def _(vector):
                bar = Bar(vector, sBAR)
                sv = Ser(vector, sSV)
                sv.do(vector.memset(eps[:, :], 1.0e-5))
                for i in range(8):
                    vector.wait_ge(sW, 16 * (i + 1))
                    if i >= 1:
                        vector.wait_ge(sTW, 8 * i)
                    ins = vector.tensor_copy(wbst[:, :], wstg[:, :])
                    sv.done(ins, sCV)
                    for k in range(8):
                        tr = 8 * i + k
                        vector.wait_ge(sTW, tr + 1)
                        ins = vector.tensor_copy(
                            WT[:, D * k + 128 * i:D * k + 128 * (i + 1)],
                            PT[tr % 2][:, :])
                        sv.done(ins, sTWE)
                vector.wait_ge(sBC, 2)
                sv.do(vector.tensor_copy(gam_bc[:, :], PF[:, 0:1024]))
                ins = vector.tensor_copy(bet_bc[:, :], PF[:, 1024:2048])
                sv.done(ins, sBC)
                vector.wait_ge(sBC, 4)
                sv.do(vector.tensor_copy(asc[:, 0:1], PF[:, 0:1]))
                sv.do(vector.memset(a_t[:, :], 1.0))
                ins = vector.tensor_scalar_mul(a_t[:, :], a_t[:, :], asc[:, 0:1])
                sv.done(ins, sBC)
                bar.hit()
                for s in range(NSEQ):
                    for t in range(TT):
                        g = TT * s + t
                        vector.wait_ge(sSTD, g + 1)
                        ln_v1(vector, sv, stat, eps)
                        vector.sem_inc(sVAR, 1)
                        vector.wait_ge(sSTD2, g + 1)
                        if g >= 2:
                            vector.wait_ge(sTE, 8 * (g - 1))
                        ln_v2(vector, sv, stat, xt[g % 2], lnb[g % 2])
                        vector.sem_inc(sLN, 1)
                        for k in range(8):
                            tr = 8 * g + k
                            vector.wait_ge(sTE, tr + 1)
                            ins = vector.tensor_copy(
                                ZT[:, 2048 * k + 128 * t:2048 * k + 128 * (t + 1)],
                                PT[tr % 2][:, :])
                            sv.done(ins, sTR)
                    bar.hit()
                    for ch in range(8):
                        ins = vector.tensor_tensor_scan(
                            ST[:, 2048 * ch:2048 * (ch + 1)],
                            a_t[:, :], ZT[:, 2048 * ch:2048 * (ch + 1)],
                            0.0, ALU.mult, ALU.add)
                        sv.done(ins, sSCN)
                    bar.hit()
                    for qt in range(TT):
                        g = TT * s + qt
                        vector.wait_ge(sSTDy, g + 1)
                        ln_v1(vector, sv, stat, eps)
                        vector.sem_inc(sVARy, 1)
                        vector.wait_ge(sSTD2y, g + 1)
                        ln_v2(vector, sv, stat, Yt[qt % 2], Ynb[qt % 2])
                        vector.sem_inc(sLNy, 1)
                        vector.wait_ge(sX2, 16 * TT * s + 16 * (qt + 1))
                        if g >= 2:
                            vector.wait_ge(sST, 16 * (g - 1))
                        ins = vector.tensor_tensor(outt[qt % 2][:, :],
                                                   Ynb[qt % 2][:, :], gam_bc[:, :],
                                                   ALU.mult)
                        sv.do(ins)
                        sv.do(vector.tensor_add(outt[qt % 2][:, :],
                                                outt[qt % 2][:, :], bet_bc[:, :]))
                        ins = vector.tensor_add(outt[qt % 2][:, :],
                                                outt[qt % 2][:, :], xt[qt % 2][:, :])
                        sv.done(ins, sOUT)
                    bar.hit()

            @block.scalar
            def _(scalar):
                bar = Bar(scalar, sBAR)
                ss = Ser(scalar, sSS)
                scalar.wait_ge(sW, 16 * 12)
                ins = scalar.activation(dsc[0:1, 1:2], dsc[0:1, 0:1], AF.Sigmoid)
                ss.done(ins, sBC)
                bar.hit()
                for s in range(NSEQ):
                    for t in range(TT):
                        g = TT * s + t
                        scalar.wait_ge(sLD, 16 * (g + 1))
                        ln_s1(scalar, ss, xt[g % 2], sq, stat)
                        scalar.sem_inc(sSTD, 1)
                        scalar.wait_ge(sVAR, g + 1)
                        ln_s2(scalar, ss, stat)
                        scalar.sem_inc(sSTD2, 1)
                    bar.hit()
                    bar.hit()
                    for qt in range(TT):
                        g = TT * s + qt
                        if g >= 2:
                            scalar.wait_ge(sLNy, g - 1)
                        scalar.wait_ge(sYP, 2 * g + 1)
                        ins = scalar.activation(Yt[qt % 2][:, 0:512], PF[:, 0:512],
                                                AF.Copy)
                        ss.done(ins, sYE)
                        scalar.wait_ge(sYP, 2 * g + 2)
                        ins = scalar.activation(Yt[qt % 2][:, 512:1024],
                                                PF[:, 512:1024], AF.Copy)
                        ss.done(ins, sYE)
                        ln_s1(scalar, ss, Yt[qt % 2], sq, stat)
                        scalar.sem_inc(sSTDy, 1)
                        scalar.wait_ge(sVARy, g + 1)
                        ln_s2(scalar, ss, stat)
                        scalar.sem_inc(sSTD2y, 1)
                    bar.hit()

            @block.tensor
            def _(tensor):
                bar = Bar(tensor, sBAR)
                for i in range(8):
                    tensor.wait_ge(sCV, i + 1)
                    for k in range(8):
                        tr = 8 * i + k
                        if tr >= 2:
                            tensor.wait_ge(sTWE, tr - 1)
                        ins = tensor.transpose(PT[tr % 2][:, :],
                                               wbst[:, 128 * k:128 * (k + 1)],
                                               ident[:, :])
                        ins.then_inc(sTW, 1)
                tensor.wait_ge(sW, 16 * 12)
                tensor.wait_ge(sBC, 1)
                tensor.matmul(PF[:, 0:512], onesf[0:1, :], grow[0:1, 0:512],
                              start=True, stop=True)
                tensor.matmul(PF[:, 512:1024], onesf[0:1, :], grow[0:1, 512:1024],
                              start=True, stop=True)
                tensor.matmul(PF[:, 1024:1536], onesf[0:1, :], berow[0:1, 0:512],
                              start=True, stop=True)
                ins = tensor.matmul(PF[:, 1536:2048], onesf[0:1, :],
                                    berow[0:1, 512:1024], start=True, stop=True)
                ins.then_inc(sBC, 1)
                tensor.wait_ge(sBC, 3)
                ins = tensor.matmul(PF[:, 0:1], onesf[0:1, :],
                                    dsc[0:1, 1:2], start=True, stop=True)
                ins.then_inc(sBC, 1)
                bar.hit(wait=(sBC, 4))
                for s in range(NSEQ):
                    for t in range(TT):
                        g = TT * s + t
                        tensor.wait_ge(sLN, g + 1)
                        for k in range(8):
                            tr = 8 * g + k
                            if tr >= 2:
                                tensor.wait_ge(sTR, tr - 1)
                            ins = tensor.transpose(PT[tr % 2][:, :],
                                                   lnb[g % 2][:, 128 * k:128 * (k + 1)],
                                                   ident[:, :])
                            ins.then_inc(sTE, 1)
                    bar.hit(wait=(sTR, 8 * TT * (s + 1)))
                    bar.hit()
                    for qt in range(TT):
                        g = TT * s + qt
                        tensor.wait_ge(sSCN, 8 * (s + 1))
                        for nb in range(2):
                            if g >= 1:
                                tensor.wait_ge(sYE, 2 * (g - 1) + nb + 1)
                            for ch in range(8):
                                tensor.matmul(
                                    PF[:, 512 * nb:512 * (nb + 1)],
                                    ST[:, 2048 * ch + 128 * qt:2048 * ch + 128 * (qt + 1)],
                                    WT[:, D * ch + 512 * nb:D * ch + 512 * (nb + 1)],
                                    start=(ch == 0), stop=False)
                            ins = tensor.matmul(
                                PF[:, 512 * nb:512 * (nb + 1)],
                                onesf[0:1, :], brow[0:1, 512 * nb:512 * (nb + 1)],
                                start=False, stop=True, skip_group_check=True)
                            ins.then_inc(sYP, 1)
                    bar.hit(wait=(sYP, 2 * TT * (s + 1)))

            @block.gpsimd
            def _(gpsimd):
                bar = Bar(gpsimd, sBAR)
                bar.hit()
                nst = 0
                for s in range(NSEQ):
                    bar.hit()
                    bar.hit()
                    for qt in range(TT):
                        g = TT * s + qt
                        gpsimd.wait_ge(sOUT, g + 1)
                        gpsimd.dma_start(out=dst[128 * g:128 * (g + 1), :],
                                         in_=outt[qt % 2][:, :]).then_inc(sST, 16)
                        nst += 1
                    gpsimd.wait_ge(sST, 16 * nst)
                    bar.hit()

    c.boundary()


def st_l1_ffn(c, src, dst):
    nc = c.nc
    w = c.dram
    sem = c.sem
    names = ["sLD", "sW", "sCV", "sSTD", "sVAR", "sSTD2", "sLN", "sTE", "sTR",
             "sRT", "sRB", "sTP2", "sHP", "sRL", "sTH", "sTHE", "sEO", "sAC",
             "sOUT", "sST", "sBAR", "sSV", "sSS", "sSG"]
    (sLD, sW, sCV, sSTD, sVAR, sSTD2, sLN, sTE, sTR, sRT, sRB, sTP2, sHP, sRL,
     sTH, sTHE, sEO, sAC, sOUT, sST, sBAR, sSV, sSS, sSG) = (
        sem(x) for x in names)

    GT = 8
    NG = NT // GT

    with ExitStack() as ctx:
        sb = lambda name, shape, dt: ctx.enter_context(nc.sbuf_tensor(name, shape, dt))
        xt = [sb(f"l1f_xt{i}", [128, D], F32) for i in range(2)]
        sq = sb("l1f_sq", [128, D], F32)
        stat = sb("l1f_st", [128, 8], F32)
        eps = sb("l1f_eps", [128, 1], F32)
        lnb = [sb(f"l1f_lnb{i}", [128, D], BF16) for i in range(2)]
        lnTg = sb("l1f_lnTg", [128, 8 * GT * 128], BF16)
        acc = sb("l1f_acc", [128, GT * D], F32)
        comb = sb("l1f_comb", [128, GT * 8], F32)
        rsc = sb("l1f_rsc", [128, 8], F32)
        tk = sb("l1f_tk", [128, 8], F32)
        tkm = sb("l1f_tkm", [128, 2], F32)
        w1sb = [sb(f"l1f_w1{i}", [128, 8 * D], BF16) for i in range(2)]
        w2sb = [sb(f"l1f_w2{i}", [128, 8 * D], BF16) for i in range(2)]
        b1row = [sb(f"l1f_b1{i}", [1, D], F32) for i in range(2)]
        b2row = [sb(f"l1f_b2{i}", [1, D], F32) for i in range(2)]
        b1all = sb("l1f_b1a", [8, D], F32)
        b2all = sb("l1f_b2a", [8, D], F32)
        gwsb = sb("l1f_gw", [128, 8 * 8], BF16)
        gwf = sb("l1f_gwf", [128, 8 * 8], F32)
        gbrow = sb("l1f_gb", [1, 8], F32)
        wstg = [sb(f"l1f_wstg{i}", [128, D], F32) for i in range(2)]
        hsb = [sb(f"l1f_h{i}", [128, D], BF16) for i in range(2)]
        hT = [sb(f"l1f_hT{i}", [128, 8 * 128], BF16) for i in range(2)]
        PF = ctx.enter_context(nc.psum_tensor("l1f_pf", [128, 2560], F32))
        PT = [ctx.enter_context(nc.psum_tensor(f"l1f_pt{i}", [128, 128], BF16))
              for i in range(2)]
        ident, onesf = c.ident, c.onesf

        def widx(grp, e):
            return grp * 8 + e

        with nc.Block() as block:
            @block.sync
            def _(sync):
                bar = Bar(sync, sBAR)
                for chd in range(8):
                    sync.dma_start(out=gwf[:, 8 * chd:8 * (chd + 1)],
                                   in_=w["L1_ffn_gate_w"][128 * chd:128 * (chd + 1), :]
                                   ).then_inc(sW, 16)
                sync.dma_start(out=gbrow[0:1, :],
                               in_=w["L1_ffn_gate_b"][None, :]).then_inc(sW, 16)
                sync.dma_start(out=b1all[:, :],
                               in_=w["L1_ffn_b1"][:, :]).then_inc(sW, 16)
                sync.dma_start(out=b2all[:, :],
                               in_=w["L1_ffn_b2"][:, :]).then_inc(sW, 16)
                sync.wait_ge(sW, 16 * 11)
                bar.hit()
                nw = [0]
                for grp in range(NG):
                    for tt in range(GT):
                        g = GT * grp + tt
                        if g >= 2:
                            sync.wait_ge(sLN, g - 1)
                        sync.dma_start(out=xt[g % 2][:, :],
                                       in_=src[128 * g:128 * (g + 1), :]
                                       ).then_inc(sLD, 16)
                    sync.wait_ge(sLD, 16 * GT * (grp + 1))
                    bar.hit()
                    for e in range(8):
                        for p in range(16):
                            if nw[0] >= 2:
                                sync.wait_ge(sCV, nw[0] - 1)
                            nm = "L1_ffn_w1" if p < 8 else "L1_ffn_w2"
                            r = p % 8
                            sync.dma_start(
                                out=wstg[nw[0] % 2][:, :],
                                in_=w[nm][e, 128 * r:128 * (r + 1), :]
                                ).then_inc(sW, 16)
                            nw[0] += 1
                    sync.wait_ge(sCV, 16 * 8 * (grp + 1))
                    bar.hit()

            @block.vector
            def _(vector):
                bar = Bar(vector, sBAR)
                sv = Ser(vector, sSV)
                sv.do(vector.memset(eps[:, :], 1.0e-5))
                vector.wait_ge(sW, 16 * 8)
                sv.do(vector.tensor_copy(gwsb[:, :], gwf[:, :]))
                bar.hit()
                ncv = [0]
                for grp in range(NG):
                    for tt in range(GT):
                        g = GT * grp + tt
                        vector.wait_ge(sSTD, g + 1)
                        ln_v1(vector, sv, stat, eps)
                        vector.sem_inc(sVAR, 1)
                        if grp >= 1:
                            vector.wait_ge(sST, 16 * (GT * (grp - 1) + tt + 1))
                        sv.do(vector.tensor_copy(acc[:, D * tt:D * (tt + 1)],
                                                 xt[g % 2][:, :]))
                        vector.wait_ge(sSTD2, g + 1)
                        if g >= 2:
                            vector.wait_ge(sTE, 8 * (g - 1))
                        ln_v2(vector, sv, stat, xt[g % 2], lnb[g % 2])
                        vector.sem_inc(sLN, 1)
                        for k in range(8):
                            tr = 8 * g + k
                            vector.wait_ge(sTE, tr + 1)
                            ins = vector.tensor_copy(
                                lnTg[:, 1024 * k + 128 * tt:1024 * k + 128 * (tt + 1)],
                                PT[tr % 2][:, :])
                            sv.done(ins, sTR)
                        vector.wait_ge(sRT, g + 1)
                        sv.do(vector.tensor_copy(rsc[:, :], PF[:, 2048:2056]))
                        sv.do(vector.reduce_max(tkm[:, 0:1], rsc[:, :], axis=AX.X))
                        sv.do(vector.tensor_single_scalar(tk[:, :], rsc[:, :],
                                                          tkm[:, 0:1], ALU.is_equal))
                        sv.do(vector.scalar_tensor_tensor(tk[:, :], tk[:, :], -1e30,
                                                          rsc[:, :], ALU.mult, ALU.add))
                        sv.do(vector.reduce_max(tkm[:, 1:2], tk[:, :], axis=AX.X))
                        sv.do(vector.tensor_single_scalar(tk[:, :], rsc[:, :],
                                                          tkm[:, 1:2], ALU.is_ge))
                        ins = vector.tensor_mul(comb[:, 8 * tt:8 * (tt + 1)],
                                                tk[:, :], rsc[:, :])
                        sv.done(ins, sTP2)
                    bar.hit()
                    def conv_expert(e):
                        for p in range(16):
                            i2 = ncv[0]
                            vector.wait_ge(sW, 16 * 11 + 16 * (i2 + 1))
                            tgt = w1sb[e % 2] if p < 8 else w2sb[e % 2]
                            ins = vector.tensor_copy(
                                tgt[:, D * (p % 8):D * (p % 8 + 1)],
                                wstg[i2 % 2][:, :])
                            sv.done(ins, sCV)
                            ncv[0] += 1
                    conv_expert(0)
                    conv_expert(1)
                    for e in range(8):
                        i = widx(grp, e)
                        for tt in range(GT):
                            ie = i * GT + tt
                            vector.wait_ge(sEO, 2 * ie + 2)
                            for nb in range(2):
                                ins = vector.scalar_tensor_tensor(
                                    acc[:, D * tt + 512 * nb:D * tt + 512 * (nb + 1)],
                                    PF[:, 1024 + 512 * nb:1024 + 512 * (nb + 1)],
                                    comb[:, 8 * tt + e:8 * tt + e + 1],
                                    acc[:, D * tt + 512 * nb:D * tt + 512 * (nb + 1)],
                                    ALU.mult, ALU.add)
                                if nb == 1:
                                    sv.done(ins, sAC)
                                else:
                                    sv.do(ins)
                        if e + 2 < 8:
                            conv_expert(e + 2)
                    bar.hit()

            @block.scalar
            def _(scalar):
                bar = Bar(scalar, sBAR)
                ss = Ser(scalar, sSS)
                bar.hit()
                for grp in range(NG):
                    for tt in range(GT):
                        g = GT * grp + tt
                        scalar.wait_ge(sLD, 16 * (g + 1))
                        ln_s1(scalar, ss, xt[g % 2], sq, stat)
                        scalar.sem_inc(sSTD, 1)
                        scalar.wait_ge(sVAR, g + 1)
                        ln_s2(scalar, ss, stat)
                        scalar.sem_inc(sSTD2, 1)
                    bar.hit()
                    for e in range(8):
                        i = widx(grp, e)
                        for tt in range(GT):
                            ie = i * GT + tt
                            scalar.wait_ge(sHP, 2 * ie + 2)
                            if ie >= 2:
                                scalar.wait_ge(sTH, 8 * (ie - 1))
                            ins = scalar.activation(hsb[ie % 2][:, :],
                                                    PF[:, 0:1024], AF.Relu)
                            ss.done(ins, sRL)
                            if ie >= 2:
                                scalar.wait_ge(sEO, 2 * (ie - 1))
                            for k in range(8):
                                th = 8 * ie + k
                                scalar.wait_ge(sTH, th + 1)
                                ins = scalar.activation(
                                    hT[ie % 2][:, 128 * k:128 * (k + 1)],
                                    PT[th % 2][:, :], AF.Copy)
                                ss.done(ins, sTHE)
                    bar.hit()

            @block.tensor
            def _(tensor):
                bar = Bar(tensor, sBAR)
                bar.hit()
                for grp in range(NG):
                    for tt in range(GT):
                        g = GT * grp + tt
                        tensor.wait_ge(sLN, g + 1)
                        for k in range(8):
                            tr = 8 * g + k
                            if tr >= 2:
                                tensor.wait_ge(sTR, tr - 1)
                            ins = tensor.transpose(PT[tr % 2][:, :],
                                                   lnb[g % 2][:, 128 * k:128 * (k + 1)],
                                                   ident[:, :])
                            ins.then_inc(sTE, 1)
                        tensor.wait_ge(sTR, 8 * (g + 1))
                        if g >= 1:
                            tensor.wait_ge(sTP2, g)
                        for chd in range(8):
                            tensor.matmul(
                                PF[:, 2048:2056],
                                lnTg[:, 1024 * chd + 128 * tt:1024 * chd + 128 * (tt + 1)],
                                gwsb[:, 8 * chd:8 * (chd + 1)],
                                start=(chd == 0), stop=False)
                        ins = tensor.matmul(PF[:, 2048:2056], onesf[0:1, :],
                                            gbrow[0:1, :], start=False, stop=True,
                                            skip_group_check=True)
                        ins.then_inc(sRT, 1)
                    bar.hit(wait=(sRT, GT * (grp + 1)))
                    for e in range(8):
                        i = widx(grp, e)
                        tensor.wait_ge(sCV, 16 * (i + 1))
                        tensor.wait_ge(sRB, 32 * (i + 1))
                        for tt in range(GT):
                            ie = i * GT + tt
                            if ie >= 1:
                                tensor.wait_ge(sRL, ie)
                            for nb in range(2):
                                for chd in range(8):
                                    tensor.matmul(
                                        PF[:, 512 * nb:512 * (nb + 1)],
                                        lnTg[:, 1024 * chd + 128 * tt:
                                             1024 * chd + 128 * (tt + 1)],
                                        w1sb[e % 2][:, D * chd + 512 * nb:
                                                    D * chd + 512 * (nb + 1)],
                                        start=(chd == 0), stop=False)
                                ins = tensor.matmul(
                                    PF[:, 512 * nb:512 * (nb + 1)], onesf[0:1, :],
                                    b1row[e % 2][0:1, 512 * nb:512 * (nb + 1)],
                                    start=False, stop=True, skip_group_check=True)
                                ins.then_inc(sHP, 1)
                            tensor.wait_ge(sRL, ie + 1)
                            for k in range(8):
                                th = 8 * ie + k
                                if th >= 2:
                                    tensor.wait_ge(sTHE, th - 1)
                                ins = tensor.transpose(
                                    PT[th % 2][:, :],
                                    hsb[ie % 2][:, 128 * k:128 * (k + 1)],
                                    ident[:, :])
                                ins.then_inc(sTH, 1)
                            tensor.wait_ge(sTHE, 8 * (ie + 1))
                            if ie >= 1:
                                tensor.wait_ge(sAC, ie)
                            for nb in range(2):
                                for chd in range(8):
                                    tensor.matmul(
                                        PF[:, 1024 + 512 * nb:1024 + 512 * (nb + 1)],
                                        hT[ie % 2][:, 128 * chd:128 * (chd + 1)],
                                        w2sb[e % 2][:, D * chd + 512 * nb:
                                                    D * chd + 512 * (nb + 1)],
                                        start=(chd == 0), stop=False)
                                ins = tensor.matmul(
                                    PF[:, 1024 + 512 * nb:1024 + 512 * (nb + 1)],
                                    onesf[0:1, :],
                                    b2row[e % 2][0:1, 512 * nb:512 * (nb + 1)],
                                    start=False, stop=True, skip_group_check=True)
                                ins.then_inc(sEO, 1)
                    bar.hit(wait=(sEO, 2 * GT * 8 * (grp + 1)))

            @block.gpsimd
            def _(gpsimd):
                bar = Bar(gpsimd, sBAR)
                sg = Ser(gpsimd, sSG)
                bar.hit()
                for grp in range(NG):
                    bar.hit()
                    for e in range(8):
                        i = widx(grp, e)
                        if i >= 2:
                            gpsimd.wait_ge(sHP, 2 * GT * (i - 1))
                            gpsimd.wait_ge(sEO, 2 * GT * (i - 1))
                        gpsimd.dma_start(out=b1row[e % 2][0:1, :],
                                         in_=b1all[e:e + 1, :]).then_inc(sRB, 16)
                        gpsimd.dma_start(out=b2row[e % 2][0:1, :],
                                         in_=b2all[e:e + 1, :]).then_inc(sRB, 16)
                    gpsimd.wait_ge(sAC, GT * 8 * (grp + 1))
                    for tt in range(GT):
                        g = GT * grp + tt
                        gpsimd.dma_start(out=dst[128 * g:128 * (g + 1), :],
                                         in_=acc[:, D * tt:D * (tt + 1)]
                                         ).then_inc(sST, 16)
                    gpsimd.wait_ge(sST, 16 * GT * (grp + 1))
                    bar.hit()

    c.boundary()


def st_l2_attn(c, src, dst):
    nc = c.nc
    w = c.dram
    sem = c.sem
    names = ["sLD", "sW", "sCV", "sGM", "sBC", "sSTD", "sVAR", "sSTD2", "sLN",
             "sTE", "sTR", "sQP", "sQS", "sQT", "sQTE", "sVP", "sVE", "sSC",
             "sSIM", "sERF", "sATT", "sMSK", "sMSK2", "sTA", "sTAE", "sAV",
             "sOUT", "sST", "sX2", "sBAR", "sSV", "sSS", "sSG"]
    (sLD, sW, sCV, sGM, sBC, sSTD, sVAR, sSTD2, sLN, sTE, sTR, sQP, sQS, sQT,
     sQTE, sVP, sVE, sSC, sSIM, sERF, sATT, sMSK, sMSK2, sTA, sTAE, sAV, sOUT,
     sST, sX2, sBAR, sSV, sSS, sSG) = (sem(x) for x in names)

    SC_E = 0.7978845608028654
    B_E = -0.5641895835477563

    with ExitStack() as ctx:
        sb = lambda name, shape, dt: ctx.enter_context(nc.sbuf_tensor(name, shape, dt))
        xt = [sb(f"l2a_xt{i}", [128, D], F32) for i in range(2)]
        sq = sb("l2a_sq", [128, D], F32)
        stat = sb("l2a_st", [128, 8], F32)
        eps = sb("l2a_eps", [128, 1], F32)
        bec = sb("l2a_bec", [128, 1], F32)
        lnb = [sb(f"l2a_lnb{i}", [128, D], BF16) for i in range(2)]
        lnT = [sb(f"l2a_lnT{i}", [128, 8 * 128], BF16) for i in range(2)]
        qks = sb("l2a_qks", [128, 128], F32)
        qbf = sb("l2a_qbf", [128, 128], BF16)
        kbf = sb("l2a_kbf", [128, 128], BF16)
        qT = sb("l2a_qT", [128, S], BF16)
        kT = sb("l2a_kT", [128, S], BF16)
        Vt = sb("l2a_V", [128, 16 * D], BF16)
        gmat = sb("l2a_gmat", [128, 4224], F32)
        ohsb = sb("l2a_oh", [32, 4096], F32)
        rel = sb("l2a_rel", [32, 1], F32)
        wqk = sb("l2a_wqk", [128, 8 * 128], BF16)
        wv = sb("l2a_wv", [128, 8 * D], BF16)
        wstg = sb("l2a_wstg", [128, D], F32)
        bqkr = sb("l2a_bqk", [1, 128], F32)
        bvr = sb("l2a_bv", [1, D], F32)
        gab = sb("l2a_gab", [2, 128], F32)
        beb = sb("l2a_beb", [2, 128], F32)
        garow = [sb(f"l2a_gar{i}", [1, 128], F32) for i in range(2)]
        berow = [sb(f"l2a_ber{i}", [1, 128], F32) for i in range(2)]
        gbc = [sb(f"l2a_gbc{i}", [128, 128], F32) for i in range(2)]
        bbc = [sb(f"l2a_bbc{i}", [128, 128], F32) for i in range(2)]
        sim = sb("l2a_sim", [128, S], F32)
        erf = sb("l2a_erf", [128, S], F32)
        att = [sb(f"l2a_att{i}", [128, S], BF16) for i in range(2)]
        attT = [sb(f"l2a_attT{i}", [128, 128], BF16) for i in range(2)]
        outt = [sb(f"l2a_out{i}", [128, D], F32) for i in range(2)]
        PF = ctx.enter_context(nc.psum_tensor("l2a_pf", [128, 3072], F32))
        PT = [ctx.enter_context(nc.psum_tensor(f"l2a_pt{i}", [128, 128], BF16))
              for i in range(2)]
        ident, onesf, trilb = c.ident, c.onesf, c.tril

        with nc.Block() as block:
            @block.sync
            def _(sync):
                bar = Bar(sync, sBAR)
                for i in range(8):
                    if i >= 1:
                        sync.wait_ge(sCV, i)
                    sync.dma_start(out=wstg[:, :],
                                   in_=w["L2_attn_Wv"][128 * i:128 * (i + 1), :]
                                   ).then_inc(sW, 16)
                for i in range(8):
                    sync.wait_ge(sCV, 8 + i)
                    sync.dma_start(out=wstg[:, 0:128],
                                   in_=w["L2_attn_Wqk"][128 * i:128 * (i + 1), :]
                                   ).then_inc(sW, 16)
                sync.dma_start(out=bqkr[0:1, :],
                               in_=w["L2_attn_bqk"][None, :]).then_inc(sW, 16)
                sync.dma_start(out=bvr[0:1, :],
                               in_=w["L2_attn_bv"][None, :]).then_inc(sW, 16)
                sync.dma_start(out=gab[:, :],
                               in_=w["L2_attn_gamma"][:, :]).then_inc(sW, 16)
                sync.dma_start(out=beb[:, :],
                               in_=w["L2_attn_beta"][:, :]).then_inc(sW, 16)
                sync.dma_start(out=rel[:, :],
                               in_=w["L2_attn_rel_emb"][:, :]).then_inc(sW, 16)
                sync.dma_start(out=ohsb[:, :],
                               in_=c.dram["cohrev"][:, :]).then_inc(sW, 16)
                sync.wait_ge(sCV, 16)
                bar.hit()
                for s in range(NSEQ):
                    for t in range(TT):
                        g = TT * s + t
                        if g >= 2:
                            sync.wait_ge(sLN, g - 1)
                        sync.dma_start(out=xt[g % 2][:, :],
                                       in_=src[128 * g:128 * (g + 1), :]
                                       ).then_inc(sLD, 16)
                    sync.wait_ge(sLD, 16 * TT * (s + 1))
                    bar.hit()
                    for qt in range(TT):
                        g = TT * s + qt
                        if g >= 2:
                            sync.wait_ge(sOUT, g - 1)
                        sync.dma_start(out=xt[qt % 2][:, :],
                                       in_=src[128 * g:128 * (g + 1), :]
                                       ).then_inc(sX2, 16)
                    sync.wait_ge(sX2, 16 * TT * (s + 1))
                    bar.hit()

            @block.vector
            def _(vector):
                bar = Bar(vector, sBAR)
                sv = Ser(vector, sSV)
                sv.do(vector.memset(eps[:, :], 1.0e-5))
                sv.do(vector.memset(bec[:, :], B_E))
                for i in range(8):
                    vector.wait_ge(sW, 16 * (i + 1))
                    ins = vector.tensor_copy(wv[:, D * i:D * (i + 1)], wstg[:, :])
                    sv.done(ins, sCV)
                for i in range(8):
                    vector.wait_ge(sW, 16 * (9 + i))
                    ins = vector.tensor_copy(wqk[:, 128 * i:128 * (i + 1)],
                                             wstg[:, 0:128])
                    sv.done(ins, sCV)
                vector.wait_ge(sBC, 66)
                sv.do(vector.tensor_copy(gbc[0][:, :], PF[:, 0:128]))
                sv.do(vector.tensor_copy(gbc[1][:, :], PF[:, 128:256]))
                sv.do(vector.tensor_copy(bbc[0][:, :], PF[:, 256:384]))
                ins = vector.tensor_copy(bbc[1][:, :], PF[:, 384:512])
                sv.done(ins, sBC)
                vector.wait_ge(sGM, 8)
                ins = vector.memset(gmat[0:1, 4096:4224], 0.0)
                sv.done(ins, sGM)
                bar.hit()
                for s in range(NSEQ):
                    for t in range(TT):
                        g = TT * s + t
                        vector.wait_ge(sSTD, g + 1)
                        ln_v1(vector, sv, stat, eps)
                        vector.sem_inc(sVAR, 1)
                        vector.wait_ge(sSTD2, g + 1)
                        if g >= 2:
                            vector.wait_ge(sTE, 8 * (g - 1))
                        ln_v2(vector, sv, stat, xt[g % 2], lnb[g % 2])
                        vector.sem_inc(sLN, 1)
                        for k in range(8):
                            tr = 8 * g + k
                            vector.wait_ge(sTE, tr + 1)
                            ins = vector.tensor_copy(
                                lnT[g % 2][:, 128 * k:128 * (k + 1)],
                                PT[tr % 2][:, :])
                            sv.done(ins, sTR)
                        vector.wait_ge(sQS, g + 1)
                        if g >= 1:
                            vector.wait_ge(sQT, 2 * g)
                        sv.do(vector.tensor_tensor(qbf[:, :], qks[:, :],
                                                   gbc[0][:, :], ALU.mult))
                        sv.do(vector.tensor_add(qbf[:, :], qbf[:, :], bbc[0][:, :]))
                        sv.do(vector.tensor_tensor(kbf[:, :], qks[:, :],
                                                   gbc[1][:, :], ALU.mult))
                        ins = vector.tensor_add(kbf[:, :], kbf[:, :], bbc[1][:, :])
                        sv.done(ins, sATT)
                        for j in range(2):
                            tq = 2 * g + j
                            vector.wait_ge(sQT, tq + 1)
                            dstT = qT if j == 0 else kT
                            ins = vector.tensor_copy(
                                dstT[:, 128 * t:128 * (t + 1)], PT[tq % 2][:, :])
                            sv.done(ins, sQTE)
                    bar.hit()
                    for qt in range(TT):
                        g = TT * s + qt
                        L = 128 * (qt + 1)
                        c0 = 2047 - 128 * qt
                        vector.wait_ge(sSC, g + 1)
                        if g >= 1:
                            vector.wait_ge(sERF, g)
                        ins = vector.scalar_tensor_tensor(
                            sim[:, 0:L], PF[:, 0:L], 1.0 / 2048.0,
                            gmat[:, c0:c0 + L], ALU.mult, ALU.add)
                        sv.done(ins, sSIM)
                        vector.wait_ge(sERF, g + 1)
                        if g >= 2:
                            vector.wait_ge(sTA, tau2_l2(g - 1))
                        ins = vector.tensor_scalar(att[qt % 2][:, 0:L], erf[:, 0:L],
                                                   0.5, 0.5, ALU.mult, ALU.add)
                        sv.done(ins, sMSK)
                        t2 = tau2_l2(g) - (qt + 1)
                        for kt in range(qt + 1):
                            vector.wait_ge(sTA, t2 + 1)
                            ins = vector.tensor_copy(attT[t2 % 2][:, :],
                                                     PT[t2 % 2][:, :])
                            sv.done(ins, sTAE)
                            t2 += 1
                        vector.wait_ge(sAV, 2 * (g + 1))
                        vector.wait_ge(sX2, 16 * TT * s + 16 * (qt + 1))
                        if g >= 2:
                            vector.wait_ge(sST, 16 * (g - 1))
                        for nb in range(2):
                            ins = vector.scalar_tensor_tensor(
                                outt[qt % 2][:, 512 * nb:512 * (nb + 1)],
                                PF[:, 2048 + 512 * nb:2048 + 512 * (nb + 1)],
                                1.0, xt[qt % 2][:, 512 * nb:512 * (nb + 1)],
                                ALU.mult, ALU.add)
                            if nb == 1:
                                sv.done(ins, sOUT)
                            else:
                                sv.do(ins)
                    bar.hit()

            @block.scalar
            def _(scalar):
                bar = Bar(scalar, sBAR)
                ss = Ser(scalar, sSS)
                for cpc in range(8):
                    scalar.wait_ge(sGM if False else sQP, cpc + 1)
                    ins = scalar.activation(gmat[0:1, 512 * cpc:512 * (cpc + 1)],
                                            PF[:, 512:1024][0:1, 0:512],
                                            AF.Copy, scale=float(np.sqrt(128.0)))
                    ss.done(ins, sGM)
                bar.hit()
                for s in range(NSEQ):
                    for t in range(TT):
                        g = TT * s + t
                        scalar.wait_ge(sLD, 16 * (g + 1))
                        ln_s1(scalar, ss, xt[g % 2], sq, stat)
                        scalar.sem_inc(sSTD, 1)
                        scalar.wait_ge(sVAR, g + 1)
                        ln_s2(scalar, ss, stat)
                        scalar.sem_inc(sSTD2, 1)
                        if g >= 1:
                            scalar.wait_ge(sATT, g)
                        scalar.wait_ge(sQP, 8 + g + 1)
                        ins = scalar.activation(qks[:, :], PF[:, 0:128], AF.Silu)
                        ss.done(ins, sQS)
                        for nb in range(2):
                            scalar.wait_ge(sVP, 2 * g + nb + 1)
                            ins = scalar.activation(
                                Vt[:, D * t + 512 * nb:D * t + 512 * (nb + 1)],
                                PF[:, 2048 + 512 * nb:2048 + 512 * (nb + 1)],
                                AF.Silu)
                            ss.done(ins, sVE)
                    bar.hit()
                    for qt in range(TT):
                        g = TT * s + qt
                        L = 128 * (qt + 1)
                        scalar.wait_ge(sSIM, g + 1)
                        if g >= 1:
                            scalar.wait_ge(sMSK, g)
                        ins = scalar.activation(erf[:, 0:L], sim[:, 0:L], AF.Erf,
                                                scale=SC_E, bias=bec[:, 0:1])
                        ss.done(ins, sERF)
                    bar.hit()

            @block.tensor
            def _(tensor):
                bar = Bar(tensor, sBAR)
                tensor.wait_ge(sW, 16 * 22)
                for cpc in range(8):
                    if cpc >= 1:
                        tensor.wait_ge(sGM, cpc)
                    ins = tensor.matmul(PF[0:1, 512:1024][0:1, 0:512],
                                        rel[:, 0:1],
                                        ohsb[:, 512 * cpc:512 * (cpc + 1)],
                                        start=True, stop=True)
                    ins.then_inc(sQP, 1)
                tensor.wait_ge(sBC, 64)
                tensor.matmul(PF[:, 0:128], onesf[0:1, :], garow[0][0:1, :],
                              start=True, stop=True)
                tensor.matmul(PF[:, 128:256], onesf[0:1, :], garow[1][0:1, :],
                              start=True, stop=True)
                tensor.matmul(PF[:, 256:384], onesf[0:1, :], berow[0][0:1, :],
                              start=True, stop=True)
                ins = tensor.matmul(PF[:, 384:512], onesf[0:1, :], berow[1][0:1, :],
                                    start=True, stop=True)
                ins.then_inc(sBC, 2)
                tensor.wait_ge(sGM, 8)
                bar.hit(wait=(sBC, 66))
                for s in range(NSEQ):
                    for t in range(TT):
                        g = TT * s + t
                        tensor.wait_ge(sLN, g + 1)
                        if g >= 1:
                            tensor.wait_ge(sQTE, 2 * g)
                        for k in range(8):
                            tr = 8 * g + k
                            if tr >= 2:
                                tensor.wait_ge(sTR, tr - 1)
                            ins = tensor.transpose(PT[tr % 2][:, :],
                                                   lnb[g % 2][:, 128 * k:128 * (k + 1)],
                                                   ident[:, :])
                            ins.then_inc(sTE, 1)
                        tensor.wait_ge(sTR, 8 * (g + 1))
                        if g >= 1:
                            tensor.wait_ge(sQS, g)
                        for ch in range(8):
                            tensor.matmul(PF[:, 0:128],
                                          lnT[g % 2][:, 128 * ch:128 * (ch + 1)],
                                          wqk[:, 128 * ch:128 * (ch + 1)],
                                          start=(ch == 0), stop=False)
                        ins = tensor.matmul(PF[:, 0:128], onesf[0:1, :],
                                            bqkr[0:1, :], start=False, stop=True,
                                            skip_group_check=True)
                        ins.then_inc(sQP, 1)
                        for nb in range(2):
                            if g >= 1:
                                tensor.wait_ge(sVE, 2 * (g - 1) + nb + 1)
                            for ch in range(8):
                                tensor.matmul(
                                    PF[:, 2048 + 512 * nb:2048 + 512 * (nb + 1)],
                                    lnT[g % 2][:, 128 * ch:128 * (ch + 1)],
                                    wv[:, D * ch + 512 * nb:D * ch + 512 * (nb + 1)],
                                    start=(ch == 0), stop=False)
                            ins = tensor.matmul(
                                PF[:, 2048 + 512 * nb:2048 + 512 * (nb + 1)],
                                onesf[0:1, :], bvr[0:1, 512 * nb:512 * (nb + 1)],
                                start=False, stop=True, skip_group_check=True)
                            ins.then_inc(sVP, 1)
                        tensor.wait_ge(sATT, g + 1)
                        for j in range(2):
                            tq = 2 * g + j
                            if tq >= 2:
                                tensor.wait_ge(sQTE, tq - 1)
                            ins = tensor.transpose(PT[tq % 2][:, :],
                                                   (qbf if j == 0 else kbf)[:, :],
                                                   ident[:, :])
                            ins.then_inc(sQT, 1)
                    bar.hit(wait=(sQTE, 2 * TT * (s + 1)))
                    for qt in range(TT):
                        g = TT * s + qt
                        L = 128 * (qt + 1)
                        if g >= 1:
                            tensor.wait_ge(sSIM, g)
                        nkb = (L + 511) // 512
                        for kb in range(nkb):
                            nn = min(512, L - 512 * kb)
                            ins = tensor.matmul(
                                PF[:, 512 * kb:512 * kb + nn],
                                qT[:, 128 * qt:128 * (qt + 1)],
                                kT[:, 512 * kb:512 * kb + nn],
                                start=True, stop=True)
                        ins.then_inc(sSC, 1)
                        tensor.wait_ge(sMSK, g + 1)
                        t2 = tau2_l2(g) - (qt + 1)
                        if g >= 1:
                            tensor.wait_ge(sOUT, g)
                        for kt in range(qt + 1):
                            if kt == qt:
                                tensor.wait_ge(sMSK2, g + 1)
                            if t2 >= 2:
                                tensor.wait_ge(sTAE, t2 - 1)
                            ins = tensor.transpose(
                                PT[t2 % 2][:, :],
                                att[qt % 2][:, 128 * kt:128 * (kt + 1)],
                                ident[:, :])
                            ins.then_inc(sTA, 1)
                            tensor.wait_ge(sTAE, t2 + 1)
                            for nb in range(2):
                                ins = tensor.matmul(
                                    PF[:, 2048 + 512 * nb:2048 + 512 * (nb + 1)],
                                    attT[t2 % 2][:, :],
                                    Vt[:, D * kt + 512 * nb:D * kt + 512 * (nb + 1)],
                                    start=(kt == 0), stop=(kt == qt))
                            t2 += 1
                        ins.then_inc(sAV, 2)
                    bar.hit(wait=(sAV, 2 * TT * (s + 1)))

            @block.gpsimd
            def _(gpsimd):
                bar = Bar(gpsimd, sBAR)
                sg = Ser(gpsimd, sSG)
                gpsimd.wait_ge(sW, 16 * 22)
                gpsimd.dma_start(out=garow[0][0:1, :], in_=gab[0:1, :]).then_inc(sBC, 16)
                gpsimd.dma_start(out=garow[1][0:1, :], in_=gab[1:2, :]).then_inc(sBC, 16)
                gpsimd.dma_start(out=berow[0][0:1, :], in_=beb[0:1, :]).then_inc(sBC, 16)
                gpsimd.dma_start(out=berow[1][0:1, :], in_=beb[1:2, :]).then_inc(sBC, 16)
                gpsimd.wait_ge(sGM, 9)
                for p in range(1, 128):
                    gpsimd.dma_start(out=gmat[p:p + 1, p:4224],
                                     in_=gmat[0:1, 0:4224 - p]).then_inc(sGM, 16)
                gpsimd.wait_ge(sGM, 9 + 127 * 16)
                bar.hit()
                nst = 0
                for s in range(NSEQ):
                    bar.hit()
                    for qt in range(TT):
                        g = TT * s + qt
                        L = 128 * (qt + 1)
                        gpsimd.wait_ge(sMSK, g + 1)
                        ins = gpsimd.tensor_mul(att[qt % 2][:, 128 * qt:128 * (qt + 1)],
                                                att[qt % 2][:, 128 * qt:128 * (qt + 1)],
                                                trilb[:, :])
                        sg.done(ins, sMSK2)
                        gpsimd.wait_ge(sOUT, g + 1)
                        gpsimd.dma_start(out=dst[128 * g:128 * (g + 1), :],
                                         in_=outt[qt % 2][:, :]).then_inc(sST, 16)
                        nst += 1
                    gpsimd.wait_ge(sST, 16 * nst)
                    bar.hit()

    c.boundary()


def tau2_l2(g):
    n = 0
    for gg in range(g + 1):
        n += (gg % TT) + 1
    return n


def st_l2_ffn(c, src, dst):
    nc = c.nc
    w = c.dram
    sem = c.sem
    names = ["sLD", "sW", "sCV", "sSTD", "sVAR", "sSTD2", "sLN", "sTE", "sTR",
             "sXF", "sRT", "sRE", "sTP2", "sSUB", "sCV8", "sBR", "sBR2", "sH1",
             "sH1E", "sT1", "sT1E", "sGP", "sGS", "sH2", "sT2", "sT2E", "sDN",
             "sAC", "sRM1", "sRM2", "sRM3", "sEPI", "sLN2", "sTE2", "sTR2",
             "sOUT", "sST", "sX2", "sBAR", "sSV", "sSS", "sSG", "sBR3"]
    (sLD, sW, sCV, sSTD, sVAR, sSTD2, sLN, sTE, sTR, sXF, sRT, sRE, sTP2,
     sSUB, sCV8, sBR, sBR2, sH1, sH1E, sT1, sT1E, sGP, sGS, sH2, sT2, sT2E,
     sDN, sAC, sRM1, sRM2, sRM3, sEPI, sLN2, sTE2, sTR2, sOUT, sST, sX2,
     sBAR, sSV, sSS, sSG, sBR3) = (sem(x) for x in names)

    GT = 4
    NG = NT // GT
    NE = 8

    with ExitStack() as ctx:
        sb = lambda name, shape, dt: ctx.enter_context(nc.sbuf_tensor(name, shape, dt))
        xt = [sb(f"l2f_xt{i}", [128, D], F32) for i in range(2)]
        stat = sb("l2f_st", [128, 8], F32)
        eps = sb("l2f_eps", [128, 1], F32)
        eps6 = sb("l2f_eps6", [128, 1], F32)
        neg6 = sb("l2f_neg6", [128, 6], F32)
        lnb = [sb(f"l2f_lnb{i}", [128, D], BF16) for i in range(2)]
        XF = sb("l2f_XF", [128, GT * D], F32)
        XT = sb("l2f_XT", [128, 8 * GT * 128], BF16)
        h1g = sb("l2f_h1g", [128, GT * D], BF16)
        h1T = sb("l2f_h1T", [128, 8 * GT * 128], BF16)
        h2s = [sb(f"l2f_h2s{i}", [128, D], BF16) for i in range(2)]
        h2T = h1T
        gs = sb("l2f_gs", [128, D], BF16)
        wsub = [sb(f"l2f_ws{i}", [128, 8 * D], BF16) for i in range(2)]
        wstg2b = [sb(f"l2f_wstg{i}", [128, D], F32) for i in range(2)]
        wstg = wstg2b[0]
        shWup = sb("l2f_shup", [128, 8 * 2048], BF16)
        shWg = sb("l2f_shg", [128, 8 * D], BF16)
        shWdn = sb("l2f_shdn", [128, 8 * D], BF16)
        shbup = sb("l2f_shbup", [1, 2048], BF16)
        shbg = sb("l2f_shbg", [1, D], BF16)
        shbdn = sb("l2f_shbdn", [1, D], BF16)
        brup = [sb(f"l2f_brup{i}", [1, D], BF16) for i in range(2)]
        brg = [sb(f"l2f_brg{i}", [1, D], BF16) for i in range(2)]
        brdn = [sb(f"l2f_brdn{i}", [1, D], BF16) for i in range(2)]
        brstg = sb("l2f_brstg", [1, D], F32)
        rwb = sb("l2f_rwb", [128, 96], BF16)
        rwf = sb("l2f_rwf", [128, 96], F32)
        rbrow = [sb(f"l2f_rbr{i}", [1, 6], F32) for i in range(2)]
        gbrow = [sb(f"l2f_gbr{i}", [1, 6], F32) for i in range(2)]
        nwstg = sb("l2f_nwstg", [1, D], F32)
        gb_bc = [sb(f"l2f_gbbc{i}", [128, 6], F32) for i in range(2)]
        nw_bc = [sb(f"l2f_nwbc{i}", [128, D], BF16) for i in range(2)]
        rsc = sb("l2f_rsc", [128, 6], F32)
        bsd = sb("l2f_bsd", [128, 6], F32)
        tk = sb("l2f_tk", [128, 6], F32)
        tkm = sb("l2f_tkm", [128, 4], F32)
        CMB = sb("l2f_cmb", [128, GT * 6], F32)
        PF = ctx.enter_context(nc.psum_tensor("l2f_pf", [128, 2560], F32))
        PT = [ctx.enter_context(nc.psum_tensor(f"l2f_pt{i}", [128, 128], BF16))
              for i in range(2)]
        ident, onesf, onesb = c.ident, c.onesf, c.onesb
        EXPW = ["L2_ffn_exp_Wup", "L2_ffn_exp_Wg", "L2_ffn_exp_Wdown"]

        def xts(ch_, tt):
            return slice(512 * ch_ + 128 * tt, 512 * ch_ + 128 * (tt + 1))

        with nc.Block() as block:
            @block.sync
            def _(sync):
                bar = Bar(sync, sBAR)
                sync.dma_start(out=rwf[:, 0:48].rearrange("p (c e) -> p c e", c=8),
                               in_=w["L2_ffn_router_W"][0].rearrange(
                                   "(c p) e -> p c e", p=128)).then_inc(sW, 16)
                sync.dma_start(out=rwf[:, 48:96].rearrange("p (c e) -> p c e", c=8),
                               in_=w["L2_ffn_router_W"][1].rearrange(
                                   "(c p) e -> p c e", p=128)).then_inc(sW, 16)
                np_ = [0]

                def ldw(dram_slice, row=False):
                    if np_[0] >= 1:
                        sync.wait_ge(sCV, np_[0])
                    out = wstg[0:1, :dram_slice.shape[-1]] if row else wstg[:, :]
                    sync.dma_start(out=out, in_=dram_slice).then_inc(sW, 16)
                    np_[0] += 1
                for i in range(8):
                    ldw(w["L2_ffn_share_Wup"][128 * i:128 * (i + 1), 0:1024])
                for i in range(8):
                    ldw(w["L2_ffn_share_Wup"][128 * i:128 * (i + 1), 1024:2048])
                for i in range(8):
                    ldw(w["L2_ffn_share_Wg"][128 * i:128 * (i + 1), :])
                for i in range(8):
                    ldw(w["L2_ffn_share_Wdown"][128 * i:128 * (i + 1), :])
                ldw(w["L2_ffn_share_bup"][None, 0:1024], row=True)
                ldw(w["L2_ffn_share_bup"][None, 1024:2048], row=True)
                ldw(w["L2_ffn_share_bg"][None, :], row=True)
                ldw(w["L2_ffn_share_bdown"][None, :], row=True)
                sync.wait_ge(sCV, 36)
                bar.hit()
                nsub = [0]
                for grp in range(NG):
                    for tt in range(GT):
                        g = GT * grp + tt
                        if g >= 2:
                            sync.wait_ge(sXF, g - 1)
                        sync.dma_start(out=xt[g % 2][:, :],
                                       in_=src[128 * g:128 * (g + 1), :]
                                       ).then_inc(sLD, 16)
                    sync.wait_ge(sLD, 16 * GT * (grp + 1))
                    bar.hit()
                    for chk in range(2):
                        for e in range(6):
                            for smi in range(3):
                                m = nsub[0]
                                for p in range(8):
                                    if m >= 2 and p == 0:
                                        sync.wait_ge(sSUB, m - 1)
                                    if 8 * m + p >= 2:
                                        sync.wait_ge(sCV8, 8 * m + p - 1)
                                    sync.dma_start(
                                        out=wstg2b[(8 * m + p) % 2][:, :],
                                        in_=w[EXPW[smi]][e, 128 * p:128 * (p + 1), :]
                                        ).then_inc(sRE, 16)
                                nsub[0] += 1
                        bar.hit()
                        if chk == 1:
                            for tt in range(GT):
                                g = GT * grp + tt
                                if g >= 2:
                                    sync.wait_ge(sOUT, g - 1)
                                sync.dma_start(out=xt[tt % 2][:, :],
                                               in_=src[128 * g:128 * (g + 1), :]
                                               ).then_inc(sX2, 16)
                            sync.wait_ge(sX2, 16 * GT * (grp + 1))
                        bar.hit()

            @block.vector
            def _(vector):
                bar = Bar(vector, sBAR)
                sv = Ser(vector, sSV)
                sv.do(vector.memset(eps[:, :], 1.0e-5))
                sv.do(vector.memset(eps6[:, :], 1.0e-6))
                sv.do(vector.memset(neg6[:, :], NEG))
                ncv = [0]

                def cvt(tgt, row=False):
                    vector.wait_ge(sW, 16 * (ncv[0] + 3))
                    src_ = wstg[0:1, :tgt.shape[-1]] if row else wstg[:, :]
                    ins = vector.tensor_copy(tgt, src_)
                    sv.done(ins, sCV)
                    ncv[0] += 1
                for i in range(8):
                    cvt(shWup[:, 2048 * i:2048 * i + 1024])
                for i in range(8):
                    cvt(shWup[:, 2048 * i + 1024:2048 * (i + 1)])
                for i in range(8):
                    cvt(shWg[:, D * i:D * (i + 1)])
                for i in range(8):
                    cvt(shWdn[:, D * i:D * (i + 1)])
                cvt(shbup[0:1, 0:1024], row=True)
                cvt(shbup[0:1, 1024:2048], row=True)
                cvt(shbg[0:1, :], row=True)
                cvt(shbdn[0:1, :], row=True)
                vector.wait_ge(sW, 16 * 2)
                sv.do(vector.tensor_copy(rwb[:, :], rwf[:, :]))
                vector.wait_ge(sBR, 98)
                sv.do(vector.tensor_copy(gb_bc[0][:, :], PF[:, 2048:2054]))
                sv.do(vector.tensor_copy(gb_bc[1][:, :], PF[:, 2054:2060]))
                sv.do(vector.tensor_copy(nw_bc[0][:, :], PF[:, 0:1024]))
                sv.do(vector.tensor_copy(nw_bc[1][:, :], PF[:, 1024:2048]))
                bar.hit()
                for grp in range(NG):
                    for tt in range(GT):
                        g = GT * grp + tt
                        vector.wait_ge(sSTD, g + 1)
                        ln_v1(vector, sv, stat, eps)
                        vector.sem_inc(sVAR, 1)
                        vector.wait_ge(sSTD2, g + 1)
                        if g >= 2:
                            vector.wait_ge(sTE, 8 * (g - 1))
                        ln_v2(vector, sv, stat, xt[g % 2], lnb[g % 2])
                        vector.sem_inc(sLN, 1)
                        ins = vector.tensor_scalar(XF[:, D * tt:D * (tt + 1)],
                                                   xt[g % 2][:, :], stat[:, 5:6],
                                                   stat[:, 7:8], ALU.mult,
                                                   ALU.subtract)
                        sv.done(ins, sXF)
                        for k in range(8):
                            tr = 8 * g + k
                            vector.wait_ge(sTE, tr + 1)
                            ins = vector.tensor_copy(XT[:, xts(k, tt)],
                                                     PT[tr % 2][:, :])
                            sv.done(ins, sTR)
                    bar.hit()
                    for chk in range(2):
                        cgi = 2 * grp + chk
                        for tt in range(GT):
                            rt = cgi * GT + tt
                            vector.wait_ge(sRT, rt + 1)
                            sv.do(vector.tensor_copy(rsc[:, :], PF[:, 2048:2054]))
                            sv.do(vector.tensor_add(bsd[:, :], rsc[:, :],
                                                    gb_bc[chk][:, :]))
                            sv.do(vector.reduce_max(tkm[:, 0:1], bsd[:, :],
                                                    axis=AX.X))
                            sv.do(vector.tensor_single_scalar(
                                tk[:, :], bsd[:, :], tkm[:, 0:1], ALU.is_equal))
                            sv.do(vector.scalar_tensor_tensor(
                                tk[:, :], tk[:, :], -1e30, bsd[:, :],
                                ALU.mult, ALU.add))
                            sv.do(vector.reduce_max(tkm[:, 1:2], tk[:, :],
                                                    axis=AX.X))
                            sv.do(vector.tensor_single_scalar(
                                tk[:, :], bsd[:, :], tkm[:, 1:2], ALU.is_ge))
                            sv.do(vector.tensor_scalar_add(bsd[:, :], rsc[:, :],
                                                           1.0e4))
                            sv.do(vector.tensor_mul(tk[:, :], tk[:, :], bsd[:, :]))
                            sv.do(vector.tensor_scalar_add(tk[:, :], tk[:, :],
                                                           -1.0e4))
                            sv.do(vector.reduce_max(tkm[:, 2:3], tk[:, :],
                                                    axis=AX.X))
                            ins = vector.tensor_scalar_mul(tkm[:, 3:4],
                                                           tkm[:, 2:3], -1.0)
                            sv.done(ins, sTP2)
                            vector.wait_ge(sRM1, rt + 1)
                            sv.do(vector.reciprocal(tkm[:, 2:3], tkm[:, 0:1]))
                            ins = vector.tensor_scalar_mul(
                                CMB[:, 6 * tt:6 * (tt + 1)], bsd[:, :],
                                tkm[:, 2:3])
                            sv.done(ins, sRM2)
                        for e in range(NE):
                            ieb = cgi * NE + e
                            for tt in range(GT):
                                q = ieb * GT + tt
                                for k in range(8):
                                    t1 = 8 * q + k
                                    vector.wait_ge(sT1, t1 + 1)
                                    ins = vector.tensor_copy(
                                        h1T[:, xts(k, tt)], PT[t1 % 2][:, :])
                                    sv.done(ins, sT1E)
                            for tt in range(GT):
                                q = ieb * GT + tt
                                vector.wait_ge(sGS, q + 1)
                                if q >= 2:
                                    vector.wait_ge(sT2, 8 * (q - 1))
                                ins = vector.tensor_mul(
                                    h2s[q % 2][:, :], gs[:, :],
                                    h1g[:, D * tt:D * (tt + 1)])
                                sv.done(ins, sH2)
                                for k in range(8):
                                    t2_ = 8 * q + k
                                    vector.wait_ge(sT2, t2_ + 1)
                                    ins = vector.tensor_copy(
                                        h2T[:, xts(k, tt)], PT[t2_ % 2][:, :])
                                    sv.done(ins, sT2E)
                            for tt in range(GT):
                                q = ieb * GT + tt
                                vector.wait_ge(sDN, q + 1)
                                vector.wait_ge(sRM2, cgi * GT + tt + 1)
                                for nb in range(2):
                                    xsl = XF[:, D * tt + 512 * nb:
                                             D * tt + 512 * (nb + 1)]
                                    psl = PF[:, 1024 + 512 * nb:
                                             1024 + 512 * (nb + 1)]
                                    if e < 6:
                                        ins = vector.scalar_tensor_tensor(
                                            xsl, psl,
                                            CMB[:, 6 * tt + e:6 * tt + e + 1],
                                            xsl, ALU.mult, ALU.add)
                                    else:
                                        ins = vector.scalar_tensor_tensor(
                                            xsl, psl, 1.0, xsl, ALU.mult, ALU.add)
                                    if nb == 1:
                                        sv.done(ins, sAC)
                                    else:
                                        sv.do(ins)
                        bar.hit()
                        for tt in range(GT):
                            g = GT * grp + tt
                            et = cgi * GT + tt
                            vector.wait_ge(sRM3, et + 1)
                            sv.do(vector.reciprocal(stat[:, 5:6], stat[:, 4:5]))
                            ins = vector.tensor_scalar_mul(
                                XF[:, D * tt:D * (tt + 1)],
                                XF[:, D * tt:D * (tt + 1)], stat[:, 5:6])
                            sv.do(ins)
                            ins = vector.tensor_mul(XF[:, D * tt:D * (tt + 1)],
                                                    XF[:, D * tt:D * (tt + 1)],
                                                    nw_bc[chk][:, :])
                            sv.done(ins, sEPI)
                            if chk == 0:
                                wcnt = grp * GT + tt
                                if wcnt >= 2:
                                    vector.wait_ge(sTE2, 8 * (wcnt - 1))
                                ins = vector.tensor_copy(lnb[tt % 2][:, :],
                                                         XF[:, D * tt:D * (tt + 1)])
                                sv.done(ins, sLN2)
                                for k in range(8):
                                    tr2 = 8 * wcnt + k
                                    vector.wait_ge(sTE2, tr2 + 1)
                                    ins = vector.tensor_copy(XT[:, xts(k, tt)],
                                                             PT[tr2 % 2][:, :])
                                    sv.done(ins, sTR2)
                            else:
                                vector.wait_ge(sX2, 16 * GT * grp + 16 * (tt + 1))
                                ins = vector.tensor_add(XF[:, D * tt:D * (tt + 1)],
                                                        XF[:, D * tt:D * (tt + 1)],
                                                        xt[tt % 2][:, :])
                                sv.done(ins, sOUT)
                        bar.hit()

            @block.scalar
            def _(scalar):
                bar = Bar(scalar, sBAR)
                ss = Ser(scalar, sSS)
                bar.hit()
                for grp in range(NG):
                    for tt in range(GT):
                        g = GT * grp + tt
                        scalar.wait_ge(sLD, 16 * (g + 1))
                        if g >= 2:
                            scalar.wait_ge(sXF, g - 1)
                        ln_s1(scalar, ss, xt[g % 2], XF[:, D * tt:D * (tt + 1)], stat)
                        scalar.sem_inc(sSTD, 1)
                        scalar.wait_ge(sVAR, g + 1)
                        ln_s2(scalar, ss, stat)
                        scalar.sem_inc(sSTD2, 1)
                    bar.hit()
                    for chk in range(2):
                        cgi = 2 * grp + chk
                        for tt in range(GT):
                            rt = cgi * GT + tt
                            scalar.wait_ge(sTP2, rt + 1)
                            ins = scalar.activation(bsd[:, :], tk[:, :], AF.Exp,
                                                    bias=tkm[:, 3:4],
                                                    accum_out=tkm[:, 0:1])
                            ss.done(ins, sRM1)
                        for e in range(NE):
                            ieb = cgi * NE + e
                            for tt in range(GT):
                                q = ieb * GT + tt
                                scalar.wait_ge(sH1, q + 1)
                                scalar.wait_ge(sH2, GT * ieb)
                                ins = scalar.activation(
                                    h1g[:, D * tt:D * (tt + 1)],
                                    PF[:, 0:1024], AF.Copy)
                                ss.done(ins, sH1E)
                            for tt in range(GT):
                                q = ieb * GT + tt
                                scalar.wait_ge(sGP, q + 1)
                                scalar.wait_ge(sH2, q)
                                ins = scalar.activation(gs[:, :], PF[:, 0:1024],
                                                        AF.Silu)
                                ss.done(ins, sGS)
                        bar.hit()
                        for tt in range(GT):
                            et = cgi * GT + tt
                            ss.do(scalar.activation(h1g[:, 0:D],
                                                    XF[:, D * tt:D * (tt + 1)],
                                                    AF.Square,
                                                    accum_out=stat[:, 2:3]))
                            a4 = scalar.activation(stat[:, 4:5], stat[:, 2:3],
                                                   AF.Sqrt, scale=1.0 / D,
                                                   bias=eps6[:, 0:1])
                            ss.do(a4)
                            scalar.sem_inc(sRM3, 1)
                        bar.hit()

            @block.tensor
            def _(tensor):
                bar = Bar(tensor, sBAR)
                tensor.wait_ge(sBR, 80)
                tensor.matmul(PF[:, 2048:2054], onesf[0:1, :], gbrow[0][0:1, :],
                              start=True, stop=True)
                tensor.matmul(PF[:, 2054:2060], onesf[0:1, :], gbrow[1][0:1, :],
                              start=True, stop=True)
                tensor.matmul(PF[:, 0:512], onesf[0:1, :], nwstg[0:1, 0:512],
                              start=True, stop=True)
                ins = tensor.matmul(PF[:, 512:1024], onesf[0:1, :],
                                    nwstg[0:1, 512:1024], start=True, stop=True)
                ins.then_inc(sBR, 1)
                tensor.wait_ge(sBR, 97)
                tensor.matmul(PF[:, 1024:1536], onesf[0:1, :],
                              nwstg[0:1, 0:512], start=True, stop=True)
                ins = tensor.matmul(PF[:, 1536:2048], onesf[0:1, :],
                                    nwstg[0:1, 512:1024], start=True, stop=True)
                ins.then_inc(sBR, 1)
                bar.hit(wait=(sBR, 98))
                for grp in range(NG):
                    for tt in range(GT):
                        g = GT * grp + tt
                        tensor.wait_ge(sLN, g + 1)
                        for k in range(8):
                            tr = 8 * g + k
                            if tr >= 2:
                                tensor.wait_ge(sTR, tr - 1)
                            ins = tensor.transpose(PT[tr % 2][:, :],
                                                   lnb[g % 2][:, 128 * k:128 * (k + 1)],
                                                   ident[:, :])
                            ins.then_inc(sTE, 1)
                    bar.hit(wait=(sTR, 8 * GT * (grp + 1)))
                    for chk in range(2):
                        cgi = 2 * grp + chk
                        for tt in range(GT):
                            rt = cgi * GT + tt
                            if rt >= 1:
                                tensor.wait_ge(sTP2, rt)
                            for ch8 in range(8):
                                tensor.matmul(
                                    PF[:, 2048:2054], XT[:, xts(ch8, tt)],
                                    rwb[:, 48 * chk + 6 * ch8:48 * chk + 6 * (ch8 + 1)],
                                    start=(ch8 == 0), stop=False)
                            ins = tensor.matmul(PF[:, 2048:2054], onesf[0:1, :],
                                                rbrow[chk][0:1, :], start=False,
                                                stop=True, skip_group_check=True)
                            ins.then_inc(sRT, 1)
                        for e in range(NE):
                            ieb = cgi * NE + e
                            shared = e >= 6
                            ns = e - 6
                            if not shared:
                                m1 = cgi * 18 + 3 * e
                                tensor.wait_ge(sCV8, 8 * (m1 + 1))
                                tensor.wait_ge(sBR3, 3 * (cgi * 6 + e + 1))
                            for tt in range(GT):
                                q = ieb * GT + tt
                                if tt == 0:
                                    tensor.wait_ge(sGS, GT * ieb)
                                else:
                                    tensor.wait_ge(sH1E, q)
                                for nb in range(2):
                                    for ch8 in range(8):
                                        if shared:
                                            rhs = shWup[:, 2048 * ch8 + 1024 * ns +
                                                        512 * nb:2048 * ch8 +
                                                        1024 * ns + 512 * (nb + 1)]
                                        else:
                                            rhs = wsub[m1 % 2][:, D * ch8 + 512 * nb:
                                                               D * ch8 + 512 * (nb + 1)]
                                        tensor.matmul(
                                            PF[:, 512 * nb:512 * (nb + 1)],
                                            XT[:, xts(ch8, tt)], rhs,
                                            start=(ch8 == 0), stop=False)
                                    brow = (shbup[0:1, 1024 * ns + 512 * nb:
                                                  1024 * ns + 512 * (nb + 1)]
                                            if shared else
                                            brup[e % 2][0:1, 512 * nb:512 * (nb + 1)])
                                    ins = tensor.matmul(
                                        PF[:, 512 * nb:512 * (nb + 1)],
                                        onesb[0:1, :], brow, start=False,
                                        stop=True, skip_group_check=True)
                                ins.then_inc(sH1, 1)
                                if not shared and tt == GT - 1:
                                    tensor.wait_ge(sH1, q + 1)
                                    tensor.sem_inc(sSUB, 1)
                                tensor.wait_ge(sH1E, q + 1)
                                for k in range(8):
                                    t1 = 8 * q + k
                                    if t1 >= 2:
                                        tensor.wait_ge(sT1E, t1 - 1)
                                    ins = tensor.transpose(
                                        PT[t1 % 2][:, :],
                                        h1g[:, D * tt + 128 * k:D * tt + 128 * (k + 1)],
                                        ident[:, :])
                                    ins.then_inc(sT1, 1)
                            if not shared:
                                m2 = m1 + 1
                                tensor.wait_ge(sCV8, 8 * (m2 + 1))
                            for tt in range(GT):
                                q = ieb * GT + tt
                                tensor.wait_ge(sT1E, 8 * (q + 1))
                                if tt == 0:
                                    tensor.wait_ge(sH1E, GT * (ieb + 1))
                                else:
                                    tensor.wait_ge(sGS, q)
                                for nb in range(2):
                                    for ch8 in range(8):
                                        rhs = (shWg[:, D * ch8 + 512 * nb:
                                                    D * ch8 + 512 * (nb + 1)]
                                               if shared else
                                               wsub[m2 % 2][:, D * ch8 + 512 * nb:
                                                            D * ch8 + 512 * (nb + 1)])
                                        tensor.matmul(
                                            PF[:, 512 * nb:512 * (nb + 1)],
                                            h1T[:, xts(ch8, tt)], rhs,
                                            start=(ch8 == 0), stop=False)
                                    brow = (shbg[0:1, 512 * nb:512 * (nb + 1)]
                                            if shared else
                                            brg[e % 2][0:1, 512 * nb:512 * (nb + 1)])
                                    ins = tensor.matmul(
                                        PF[:, 512 * nb:512 * (nb + 1)],
                                        onesb[0:1, :], brow, start=False,
                                        stop=True, skip_group_check=True)
                                ins.then_inc(sGP, 1)
                                if not shared and tt == GT - 1:
                                    tensor.wait_ge(sGP, q + 1)
                                    tensor.sem_inc(sSUB, 1)
                                tensor.wait_ge(sH2, q + 1)
                                for k in range(8):
                                    t2_ = 8 * q + k
                                    if t2_ >= 2:
                                        tensor.wait_ge(sT2E, t2_ - 1)
                                    ins = tensor.transpose(
                                        PT[t2_ % 2][:, :],
                                        h2s[q % 2][:, 128 * k:128 * (k + 1)],
                                        ident[:, :])
                                    ins.then_inc(sT2, 1)
                            if not shared:
                                m3 = m1 + 2
                                tensor.wait_ge(sCV8, 8 * (m3 + 1))
                            for tt in range(GT):
                                q = ieb * GT + tt
                                tensor.wait_ge(sT2E, 8 * (q + 1))
                                tensor.wait_ge(sAC, q)
                                for nb in range(2):
                                    for ch8 in range(8):
                                        rhs = (shWdn[:, D * ch8 + 512 * nb:
                                                     D * ch8 + 512 * (nb + 1)]
                                               if shared else
                                               wsub[m3 % 2][:, D * ch8 + 512 * nb:
                                                            D * ch8 + 512 * (nb + 1)])
                                        tensor.matmul(
                                            PF[:, 1024 + 512 * nb:
                                               1024 + 512 * (nb + 1)],
                                            h2T[:, xts(ch8, tt)], rhs,
                                            start=(ch8 == 0), stop=False)
                                    brow = (shbdn[0:1, 512 * nb:512 * (nb + 1)]
                                            if shared else
                                            brdn[e % 2][0:1, 512 * nb:512 * (nb + 1)])
                                    ins = tensor.matmul(
                                        PF[:, 1024 + 512 * nb:1024 + 512 * (nb + 1)],
                                        onesb[0:1, :], brow, start=False,
                                        stop=True, skip_group_check=True)
                                ins.then_inc(sDN, 1)
                                if not shared and tt == GT - 1:
                                    tensor.wait_ge(sDN, q + 1)
                                    tensor.sem_inc(sSUB, 1)
                        bar.hit(wait=(sDN, GT * NE * (cgi + 1)))
                        if chk == 0:
                            for tt in range(GT):
                                wcnt = grp * GT + tt
                                tensor.wait_ge(sLN2, wcnt + 1)
                                for k in range(8):
                                    tr2 = 8 * wcnt + k
                                    if tr2 >= 2:
                                        tensor.wait_ge(sTR2, tr2 - 1)
                                    ins = tensor.transpose(
                                        PT[tr2 % 2][:, :],
                                        lnb[tt % 2][:, 128 * k:128 * (k + 1)],
                                        ident[:, :])
                                    ins.then_inc(sTE2, 1)
                            bar.hit(wait=(sTR2, 8 * GT * (grp + 1)))
                        else:
                            bar.hit()

            @block.gpsimd
            def _(gpsimd):
                bar = Bar(gpsimd, sBAR)
                sg = Ser(gpsimd, sSG)
                gpsimd.dma_start(out=rbrow[0][0:1, :],
                                 in_=w["L2_ffn_router_b"][0:1, :]).then_inc(sBR, 16)
                gpsimd.dma_start(out=rbrow[1][0:1, :],
                                 in_=w["L2_ffn_router_b"][1:2, :]).then_inc(sBR, 16)
                gpsimd.dma_start(out=gbrow[0][0:1, :],
                                 in_=w["L2_ffn_gate_bias"][0:1, :]).then_inc(sBR, 16)
                gpsimd.dma_start(out=gbrow[1][0:1, :],
                                 in_=w["L2_ffn_gate_bias"][1:2, :]).then_inc(sBR, 16)
                gpsimd.dma_start(out=nwstg[0:1, :],
                                 in_=w["L2_ffn_norm_w"][0:1, :]).then_inc(sBR, 16)
                gpsimd.wait_ge(sBR, 81)
                gpsimd.dma_start(out=nwstg[0:1, :],
                                 in_=w["L2_ffn_norm_w"][1:2, :]).then_inc(sBR, 16)
                bar.hit()
                mg = [0]
                for grp in range(NG):
                    bar.hit()
                    for chk in range(2):
                        cgi = 2 * grp + chk
                        for e in range(6):
                            iexp = cgi * 6 + e
                            if iexp >= 2:
                                pe = iexp - 2
                                ieb_prev = (pe // 6) * NE + (pe % 6)
                                gpsimd.wait_ge(sDN, GT * (ieb_prev + 1))
                            for ri, (tgt, nm2) in enumerate(
                                    [(brup[e % 2], "L2_ffn_exp_bup"),
                                     (brg[e % 2], "L2_ffn_exp_bg"),
                                     (brdn[e % 2], "L2_ffn_exp_bdown")]):
                                rr = 3 * iexp + ri
                                gpsimd.dma_start(out=brstg[0:1, :],
                                                 in_=w[nm2][e:e + 1, :]
                                                 ).then_inc(sBR2, 16)
                                gpsimd.wait_ge(sBR2, 16 * (rr + 1))
                                ins = gpsimd.tensor_copy(tgt[0:1, :],
                                                         brstg[0:1, :])
                                sg.done(ins, sBR3)
                            for smi in range(3):
                                m = mg[0]
                                if m >= 2:
                                    gpsimd.wait_ge(sSUB, m - 1)
                                for p in range(8):
                                    gpsimd.wait_ge(sRE, 16 * (8 * m + p + 1))
                                    ins = gpsimd.tensor_copy(
                                        wsub[m % 2][:, D * p:D * (p + 1)],
                                        wstg2b[(8 * m + p) % 2][:, :])
                                    sg.done(ins, sCV8)
                                mg[0] += 1
                        bar.hit()
                        if chk == 1:
                            for tt in range(GT):
                                g = GT * grp + tt
                                gpsimd.wait_ge(sOUT, g + 1)
                                gpsimd.dma_start(out=dst[128 * g:128 * (g + 1), :],
                                                 in_=XF[:, D * tt:D * (tt + 1)]
                                                 ).then_inc(sST, 16)
                            gpsimd.wait_ge(sST, 16 * GT * (grp + 1))
                        bar.hit()

    c.boundary()


def st_l3_attn(c, src, dst):
    nc = c.nc
    w = c.dram
    sem = c.sem
    names = ["sLD", "sW", "sCV", "sWB", "sCB", "sTW", "sTWE", "sWQ", "sCQ",
             "sPE", "sPEE", "sSTD", "sVAR", "sSTD2", "sLN", "sTR", "sTE",
             "sVP", "sVE", "sQK", "sQE2", "sRP", "sRPE", "sSC", "sMX", "sEX",
             "sRC", "sSCL", "sTA", "sTAE", "sAV", "sAE", "sU", "sOUT", "sST",
             "sX2", "sBAR", "sSV", "sSS", "sSG", "sQD"]
    (sLD, sW, sCV, sWB, sCB, sTW, sTWE, sWQ, sCQ, sPE, sPEE, sSTD, sVAR,
     sSTD2, sLN, sTR, sTE, sVP, sVE, sQK, sQE2, sRP, sRPE, sSC, sMX, sEX,
     sRC, sSCL, sTA, sTAE, sAV, sAE, sU, sOUT, sST, sX2, sBAR, sSV, sSS,
     sSG, sQD) = (sem(x) for x in names)
    sDG = sem("sDG")

    ESC = 1.0 / math.sqrt(96.0)
    itemsB = []
    tau_after = []
    tau = 0
    for s_ in range(NSEQ):
        for qt in range(TT):
            for h in range(16):
                H = (s_ * TT + qt) * 16 + h
                tau += qt + 1
                itemsB.append((s_, qt, h, H, (qt + 1) * 128, s_ * TT + qt, H // 2))
                tau_after.append(tau)

    def tau_start(H):
        return tau_after[H] - (itemsB[H][1] + 1)

    def bitems(s_, gb):
        lo = s_ * 256 + gb * 64
        return itemsB[lo:lo + 64]

    with ExitStack() as ctx:
        sb = lambda name, shape, dt: ctx.enter_context(nc.sbuf_tensor(name, shape, dt))
        xt = [sb(f"l3a_xt{i}", [128, D], F32) for i in range(2)]
        outt0 = sb("l3a_out0", [128, D], F32)
        outt = [outt0, outt0]
        sq = sb("l3a_sq", [128, D], F32)
        stat = sb("l3a_st", [128, 8], F32)
        eps = sb("l3a_eps", [128, 1], F32)
        lnb0 = sb("l3a_lnb0", [128, D], BF16)
        lnb = [lnb0, lnb0]
        lnT = sb("l3a_lnT", [128, 8 * 512], BF16)
        Vt = sb("l3a_V", [128, 16 * D], BF16)
        QTg = sb("l3a_QT", [128, 8 * 512], BF16)
        KT = sb("l3a_KT", [128, 8 * S], BF16)
        qRTg = sb("l3a_qRT", [128, 8 * 512], BF16)
        kRT4 = sb("l3a_kRT4", [128, S], BF16)
        ropet = sb("l3a_ropet", [128, S], BF16)
        att0 = sb("l3a_att", [128, S], BF16)
        att = [att0, att0]
        attT = [sb(f"l3a_attT{i}", [128, 128], BF16) for i in range(2)]
        dgm = [sb(f"l3a_dgm{i}", [128, 128], BF16) for i in range(2)]
        avh = sb("l3a_avh", [128, 8 * 128], BF16)
        sm = sb("l3a_sm", [128, 8], F32)
        QEw = sb("l3a_QEw", [128, 8 * D], BF16)
        KEw = sb("l3a_KEw", [128, 8 * D], BF16)
        VEw = sb("l3a_VEw", [128, 8 * D], BF16)
        QREw = sb("l3a_QREw", [128, 8 * 512], BF16)
        wkr4 = sb("l3a_wkr4", [128, 8 * 128], BF16)
        wow = sb("l3a_wo", [128, 8 * D], BF16)
        wuk = sb("l3a_wuk", [128, 2 * D], BF16)
        wuv = sb("l3a_wuv", [128, 2 * D], BF16)
        wtp = Vt[:, 0:32 * 128]
        wblk = sq[:, 0:128]
        wbb = KT[:, 0:128]
        wqs = [KT[:, 1024:2048], KT[:, 2048:3072]]
        wrs = [KT[:, 3072:3584], KT[:, 3584:4096]]
        wstg = xt[0]
        wstg2 = outt[1][:, 0:512]
        PF = ctx.enter_context(nc.psum_tensor("l3a_pf", [128, 3072], F32))
        PT = [ctx.enter_context(nc.psum_tensor(f"l3a_pt{i}", [128, 128], BF16))
              for i in range(2)]
        ident = c.ident

        with nc.Block() as block:
            @block.sync
            def _(sync):
                bar = Bar(sync, sBAR)
                nw = [0]
                for nm, cols, n in [("L3_attn_W_UK", 1024, 2),
                                    ("L3_attn_W_UV", 1024, 2),
                                    ("W_KR4", 128, 8),
                                    ("L3_attn_W_O", 1024, 8)]:
                    for i in range(n):
                        if nw[0] >= 1:
                            sync.wait_ge(sCV, nw[0])
                        sync.dma_start(out=wstg[:, 0:cols],
                                       in_=w[nm][128 * i:128 * (i + 1), :]
                                       ).then_inc(sW, 16)
                        nw[0] += 1
                for hh in range(2):
                    sync.wait_ge(sCV, nw[0])
                    sync.dma_start(out=wstg[:, :],
                                   in_=c.dram["cropet"][:, 1024 * hh:1024 * (hh + 1)]
                                   ).then_inc(sW, 16)
                    nw[0] += 1
                nb_ = [0]
                for d in range(8):
                    for dc in range(2):
                        if nb_[0] >= 1:
                            sync.wait_ge(sCB, nb_[0])
                        sync.dma_start(out=wblk[:, :],
                                       in_=w["L3_attn_W_DKV"][128 * d:128 * (d + 1),
                                                              128 * dc:128 * (dc + 1)]
                                       ).then_inc(sWB, 16)
                        nb_[0] += 1
                nq = [0]
                for d in range(8):
                    for dc in range(32):
                        if nb_[0] >= 1:
                            sync.wait_ge(sCB, nb_[0])
                        sync.dma_start(out=wblk[:, :],
                                       in_=w["L3_attn_W_DQ"][128 * d:128 * (d + 1),
                                                             128 * dc:128 * (dc + 1)]
                                       ).then_inc(sWB, 16)
                        nb_[0] += 1
                    for dc in range(32):
                        if nq[0] >= 2:
                            sync.wait_ge(sCQ, nq[0] - 1)
                        sync.dma_start(out=wstg[:, :],
                                       in_=w["L3_attn_W_UQ"][128 * dc:128 * (dc + 1), :]
                                       ).then_inc(sWQ, 16)
                        sync.dma_start(out=wstg2[:, :],
                                       in_=w["L3_attn_W_QR"][128 * dc:128 * (dc + 1), :]
                                       ).then_inc(sWQ, 16)
                        nq[0] += 2
                bar.hit()
                for s_ in range(NSEQ):
                    for gb in range(4):
                        for t in range(4 * gb, 4 * gb + 4):
                            g = TT * s_ + t
                            if g >= 2:
                                sync.wait_ge(sLN, g - 1)
                            sync.dma_start(out=xt[g % 2][:, :],
                                           in_=src[128 * g:128 * (g + 1), :]
                                           ).then_inc(sLD, 16)
                        sync.wait_ge(sLD, 16 * (TT * s_ + 4 * gb + 4))
                        bar.hit()
                        for qt in range(4 * gb, 4 * gb + 4):
                            g = TT * s_ + qt
                            if g >= 2:
                                sync.wait_ge(sOUT, g - 1)
                            sync.dma_start(out=xt[qt % 2][:, :],
                                           in_=src[128 * g:128 * (g + 1), :]
                                           ).then_inc(sX2, 16)
                        sync.wait_ge(sX2, 16 * (TT * s_ + 4 * gb + 4))
                        bar.hit()

            @block.vector
            def _(vector):
                bar = Bar(vector, sBAR)
                sv = Ser(vector, sSV)
                sv.do(vector.memset(eps[:, :], 1.0e-5))
                ncv = [0]
                for tgt, cols, n in [(wuk, 1024, 2), (wuv, 1024, 2),
                                     (wkr4, 128, 8), (wow, 1024, 8)]:
                    for i in range(n):
                        vector.wait_ge(sW, 16 * (ncv[0] + 1))
                        ins = vector.tensor_copy(tgt[:, cols * i:cols * (i + 1)],
                                                 wstg[:, 0:cols])
                        sv.done(ins, sCV)
                        ncv[0] += 1
                for hh in range(2):
                    vector.wait_ge(sW, 16 * (ncv[0] + 1))
                    ins = vector.tensor_copy(ropet[:, 1024 * hh:1024 * (hh + 1)],
                                             wstg[:, :])
                    sv.done(ins, sCV)
                    ncv[0] += 1
                nb_ = [0]
                ntw = [0]
                npe = [0]

                def blk():
                    vector.wait_ge(sWB, 16 * (nb_[0] + 1))
                    if nb_[0] >= 1:
                        vector.wait_ge(sTW, nb_[0])
                    ins = vector.tensor_copy(wbb[:, :], wblk[:, :])
                    sv.done(ins, sCB)
                    nb_[0] += 1
                    tr = ntw[0]
                    vector.wait_ge(sTW, tr + 1)
                    ins = vector.tensor_copy(wtp[:, 128 * (tr % 32):
                                                 128 * (tr % 32) + 128]
                                             if False else
                                             wtp[:, 128 * ((tr) % 32):
                                                 128 * ((tr) % 32 + 1)],
                                             PT[tr % 2][:, :])
                    sv.done(ins, sTWE)
                    ntw[0] += 1
                for d in range(8):
                    for dc in range(2):
                        blk()
                    for tgt in (KEw, VEw):
                        for ch in range(2):
                            vector.wait_ge(sPE, npe[0] + 1)
                            ins = vector.tensor_copy(
                                tgt[:, D * d + 512 * ch:D * d + 512 * (ch + 1)],
                                PF[:, 512 * (npe[0] % 4):512 * (npe[0] % 4) + 512])
                            sv.done(ins, sPEE)
                            npe[0] += 1
                nq = [0]
                for d in range(8):
                    vector.wait_ge(sPE, 32 + 3 * d)
                    for dc in range(32):
                        blk()
                    for dc in range(32):
                        gd = 32 * d + dc
                        if gd >= 2:
                            vector.wait_ge(sQD, gd - 1)
                        vector.wait_ge(sWQ, 16 * (nq[0] + 1))
                        ins = vector.tensor_copy(wqs[dc % 2][:, :], wstg[:, :])
                        sv.done(ins, sCQ)
                        nq[0] += 1
                        vector.wait_ge(sWQ, 16 * (nq[0] + 1))
                        ins = vector.tensor_copy(wrs[dc % 2][:, :], wstg2[:, :])
                        sv.done(ins, sCQ)
                        nq[0] += 1
                    for ch in range(2):
                        vector.wait_ge(sPE, npe[0] + 1)
                        ins = vector.tensor_copy(
                            QEw[:, D * d + 512 * ch:D * d + 512 * (ch + 1)],
                            PF[:, 512 * (npe[0] % 4):512 * (npe[0] % 4) + 512])
                        sv.done(ins, sPEE)
                        npe[0] += 1
                    vector.wait_ge(sPE, npe[0] + 1)
                    ins = vector.tensor_copy(
                        QREw[:, 512 * d:512 * (d + 1)],
                        PF[:, 512 * (npe[0] % 4):512 * (npe[0] % 4) + 512])
                    sv.done(ins, sPEE)
                    npe[0] += 1
                bar.hit()
                for s_ in range(NSEQ):
                    for gb in range(4):
                        for t in range(4 * gb, 4 * gb + 4):
                            g = TT * s_ + t
                            vector.wait_ge(sSTD, g + 1)
                            ln_v1(vector, sv, stat, eps)
                            vector.sem_inc(sVAR, 1)
                            vector.wait_ge(sSTD2, g + 1)
                            if g >= 1:
                                vector.wait_ge(sTE, 8 * g)
                            ln_v2(vector, sv, stat, xt[g % 2], lnb[g % 2])
                            vector.sem_inc(sLN, 1)
                            for k in range(8):
                                tr = 8 * g + k
                                vector.wait_ge(sTR, tr + 1)
                                ins = vector.tensor_copy(
                                    lnT[:, 512 * k + 128 * (t % 4):
                                        512 * k + 128 * (t % 4 + 1)],
                                    PT[tr % 2][:, :])
                                sv.done(ins, sTE)
                        gqi = s_ * 4 + gb
                        for hp in range(8):
                            rp = gqi * 9 + hp
                            vector.wait_ge(sRP, rp + 1)
                            ins = vector.tensor_tensor(
                                qRTg[:, 512 * hp:512 * (hp + 1)],
                                PF[:, 2048 + 512 * (hp % 2):
                                   2048 + 512 * (hp % 2) + 512],
                                ropet[:, 512 * gb:512 * (gb + 1)], ALU.mult)
                            sv.done(ins, sRPE)
                        rp = gqi * 9 + 8
                        vector.wait_ge(sRP, rp + 1)
                        ins = vector.tensor_tensor(
                            kRT4[:, 512 * gb:512 * (gb + 1)],
                            PF[:, 2048:2560],
                            ropet[:, 512 * gb:512 * (gb + 1)], ALU.mult)
                        sv.done(ins, sRPE)
                        bar.hit()
                        for (s2, qt, h, H, L, qtg, pr) in bitems(s_, gb):
                            vector.wait_ge(sSC, H + 1)
                            sv.do(vector.reduce_max(sm[:, 0:1], PF[:, 0:L],
                                                    axis=AX.X))
                            ins = vector.tensor_scalar_mul(sm[:, 1:2],
                                                           sm[:, 0:1], -ESC)
                            sv.done(ins, sMX)
                            vector.wait_ge(sEX, H + 1)
                            sv.do(vector.reciprocal(sm[:, 3:4], sm[:, 2:3]))
                            ins = vector.tensor_scalar_mul(att[h % 2][:, 0:L],
                                                           att[h % 2][:, 0:L],
                                                           sm[:, 3:4])
                            sv.done(ins, sDG)
                            t2 = tau_start(H)
                            for kt in range(qt + 1):
                                vector.wait_ge(sTA, t2 + 1)
                                ins = vector.tensor_copy(attT[t2 % 2][:, :],
                                                         PT[t2 % 2][:, :])
                                sv.done(ins, sTAE)
                                t2 += 1
                            if h == 15:
                                vector.wait_ge(sU, 2 * (qtg + 1))
                                vector.wait_ge(sX2, 16 * (qt + 1) + 16 * TT * s_)
                                if qtg >= 1:
                                    vector.wait_ge(sST, 16 * qtg)
                                for nb2 in range(2):
                                    ins = vector.scalar_tensor_tensor(
                                        outt[qt % 2][:, 512 * nb2:512 * (nb2 + 1)],
                                        PF[:, 2048 + 512 * nb2:
                                           2048 + 512 * (nb2 + 1)],
                                        1.0,
                                        xt[qt % 2][:, 512 * nb2:512 * (nb2 + 1)],
                                        ALU.mult, ALU.add)
                                    sv.do(ins)
                                vector.sem_inc(sOUT, 1)
                        bar.hit()

            @block.scalar
            def _(scalar):
                bar = Bar(scalar, sBAR)
                ss = Ser(scalar, sSS)
                bar.hit()
                for s_ in range(NSEQ):
                    for gb in range(4):
                        for t in range(4 * gb, 4 * gb + 4):
                            g = TT * s_ + t
                            scalar.wait_ge(sLD, 16 * (g + 1))
                            ln_s1(scalar, ss, xt[g % 2], sq, stat)
                            scalar.sem_inc(sSTD, 1)
                            scalar.wait_ge(sVAR, g + 1)
                            ln_s2(scalar, ss, stat)
                            scalar.sem_inc(sSTD2, 1)
                            for nb2 in range(2):
                                scalar.wait_ge(sVP, 2 * g + nb2 + 1)
                                ins = scalar.activation(
                                    Vt[:, D * t + 512 * nb2:D * t + 512 * (nb2 + 1)],
                                    PF[:, 2560:3072], AF.Copy)
                                ss.done(ins, sVE)
                        gqi = s_ * 4 + gb
                        for wi in range(2):
                            for hc in range(8):
                                qv = gqi * 16 + wi * 8 + hc
                                scalar.wait_ge(sQK, qv + 1)
                                if wi == 0:
                                    tgt = QTg[:, 512 * hc:512 * (hc + 1)]
                                else:
                                    tgt = KT[:, 2048 * hc + 512 * gb:
                                             2048 * hc + 512 * (gb + 1)]
                                ins = scalar.activation(
                                    tgt, PF[:, 512 * (qv % 4):512 * (qv % 4) + 512],
                                    AF.Copy)
                                ss.done(ins, sQE2)
                        bar.hit()
                        for (s2, qt, h, H, L, qtg, pr) in bitems(s_, gb):
                            scalar.wait_ge(sMX, H + 1)
                            if H >= 1:
                                scalar.wait_ge(sTA, tau_after[H - 1])
                            ins = scalar.activation(att[h % 2][:, 0:L],
                                                    PF[:, 0:L], AF.Exp,
                                                    bias=sm[:, 1:2], scale=ESC,
                                                    accum_out=sm[:, 2:3])
                            ss.done(ins, sEX)
                            if h % 2 == 1:
                                hc = h // 2
                                scalar.wait_ge(sAV, 2 * (pr + 1))
                                if qtg >= 1:
                                    scalar.wait_ge(sU, 2 * qtg)
                                ins = scalar.activation(
                                    avh[:, 128 * hc:128 * (hc + 1)],
                                    PF[:, 2048 + 512 * (pr % 2):
                                       2048 + 512 * (pr % 2) + 128],
                                    AF.Copy)
                                ss.done(ins, sAE)
                        bar.hit()

            @block.tensor
            def _(tensor):
                bar = Bar(tensor, sBAR)
                nb_ = [0]
                ntw = [0]
                npe = [0]

                def tblk():
                    tr = ntw[0]
                    tensor.wait_ge(sCB, nb_[0] + 1)
                    nb_[0] += 1
                    if tr >= 2:
                        tensor.wait_ge(sTWE, tr - 1)
                    ins = tensor.transpose(PT[tr % 2][:, :], wbb[:, :],
                                           ident[:, :])
                    ins.then_inc(sTW, 1)
                    ntw[0] += 1

                for d in range(8):
                    for dc in range(2):
                        tblk()
                    tensor.wait_ge(sTWE, ntw[0])
                    tensor.wait_ge(sCV, 4)
                    for ti, wres in [(0, wuk), (1, wuv)]:
                        for ch in range(2):
                            if npe[0] >= 4:
                                tensor.wait_ge(sPEE, npe[0] - 3)
                            for dc in range(2):
                                ins = tensor.matmul(
                                    PF[:, 512 * (npe[0] % 4):512 * (npe[0] % 4) + 512],
                                    wtp[:, 128 * ((ntw[0] - 2 + dc) % 32):
                                        128 * ((ntw[0] - 2 + dc) % 32 + 1)],
                                    wres[:, D * dc + 512 * ch:D * dc + 512 * (ch + 1)],
                                    start=(dc == 0), stop=(dc == 1))
                            ins.then_inc(sPE, 1)
                            npe[0] += 1
                nq = [0]
                for d in range(8):
                    for dc in range(32):
                        tblk()
                    tensor.wait_ge(sTWE, ntw[0])
                    if npe[0] >= 4:
                        pass
                    for dc in range(32):
                        tensor.wait_ge(sCQ, nq[0] + 2)
                        base = (ntw[0] - 32 + dc) % 32
                        if dc == 0:
                            tensor.wait_ge(sPEE, npe[0])
                        for ch in range(2):
                            tensor.matmul(
                                PF[:, 512 * ((npe[0] + ch) % 4):
                                   512 * ((npe[0] + ch) % 4) + 512],
                                wtp[:, 128 * base:128 * (base + 1)],
                                wqs[dc % 2][:, 512 * ch:512 * (ch + 1)],
                                start=(dc == 0), stop=(dc == 31))
                        ins = tensor.matmul(
                            PF[:, 512 * ((npe[0] + 2) % 4):
                               512 * ((npe[0] + 2) % 4) + 512],
                            wtp[:, 128 * base:128 * (base + 1)],
                            wrs[dc % 2][:, :],
                            start=(dc == 0), stop=(dc == 31))
                        ins.then_inc(sQD, 1)
                        nq[0] += 2
                        if dc == 31:
                            tensor.wait_ge(sQD, 32 * (d + 1))
                            tensor.sem_inc(sPE, 3)
                            npe[0] += 3
                bar.hit(wait=(sPEE, npe[0]))
                for s_ in range(NSEQ):
                    for gb in range(4):
                        gqi = s_ * 4 + gb
                        for t in range(4 * gb, 4 * gb + 4):
                            g = TT * s_ + t
                            tensor.wait_ge(sLN, g + 1)
                            for k in range(8):
                                tr = 8 * g + k
                                if tr >= 2:
                                    tensor.wait_ge(sTE, tr - 1)
                                ins = tensor.transpose(
                                    PT[tr % 2][:, :],
                                    lnb[g % 2][:, 128 * k:128 * (k + 1)],
                                    ident[:, :])
                                ins.then_inc(sTR, 1)
                            tensor.wait_ge(sTE, 8 * (g + 1))
                            for nb2 in range(2):
                                if 2 * g + nb2 >= 1:
                                    tensor.wait_ge(sVE, 2 * g + nb2)
                                for ch in range(8):
                                    ins = tensor.matmul(
                                        PF[:, 2560:3072],
                                        lnT[:, 512 * ch + 128 * (t % 4):
                                            512 * ch + 128 * (t % 4 + 1)],
                                        VEw[:, D * ch + 512 * nb2:
                                            D * ch + 512 * (nb2 + 1)],
                                        start=(ch == 0), stop=(ch == 7))
                                ins.then_inc(sVP, 1)
                        for wi in range(2):
                            wsb = QEw if wi == 0 else KEw
                            for hc in range(8):
                                qv = gqi * 16 + wi * 8 + hc
                                if qv >= 4:
                                    tensor.wait_ge(sQE2, qv - 3)
                                for ch in range(8):
                                    ins = tensor.matmul(
                                        PF[:, 512 * (qv % 4):512 * (qv % 4) + 512],
                                        wsb[:, D * ch + 128 * hc:
                                            D * ch + 128 * (hc + 1)],
                                        lnT[:, 512 * ch:512 * (ch + 1)],
                                        start=(ch == 0), stop=(ch == 7))
                                ins.then_inc(sQK, 1)
                        tensor.wait_ge(sVE, 2 * (TT * s_ + 4 * gb + 4))
                        for hp in range(8):
                            rp = gqi * 9 + hp
                            if rp >= 2:
                                tensor.wait_ge(sRPE, rp - 1)
                            psl = PF[:, 2048 + 512 * (hp % 2):
                                     2048 + 512 * (hp % 2) + 512]
                            for sub2 in range(2):
                                h_ = 2 * hp + sub2
                                rows = slice(64 * sub2, 64 * sub2 + 32)
                                for ch in range(8):
                                    ins = tensor.matmul(
                                        psl[rows, :],
                                        QREw[:, 512 * ch + 32 * h_:
                                             512 * ch + 32 * (h_ + 1)],
                                        lnT[:, 512 * ch:512 * (ch + 1)],
                                        start=(ch == 0), stop=(ch == 7))
                            ins.then_inc(sRP, 1)
                        rp = gqi * 9 + 8
                        tensor.wait_ge(sRPE, rp - 1)
                        for ch in range(8):
                            ins = tensor.matmul(
                                PF[:, 2048:2560],
                                wkr4[:, 128 * ch:128 * (ch + 1)],
                                lnT[:, 512 * ch:512 * (ch + 1)],
                                start=(ch == 0), stop=(ch == 7))
                        ins.then_inc(sRP, 1)
                        bar.hit(wait=(sRPE, 9 * (gqi + 1)))
                        for (s2, qt, h, H, L, qtg, pr) in bitems(s_, gb):
                            hc, hb = h // 2, (h % 2) * 64
                            tensor.wait_ge(sEX, H)
                            nkb = (L + 511) // 512
                            for kb in range(nkb):
                                nn = min(512, L - 512 * kb)
                                tensor.matmul(
                                    PF[:, 512 * kb:512 * kb + nn],
                                    QTg[hb:hb + 64,
                                        512 * hc + 128 * (qt % 4):
                                        512 * hc + 128 * (qt % 4 + 1)],
                                    KT[hb:hb + 64, 2048 * hc + 512 * kb:
                                       2048 * hc + 512 * kb + nn],
                                    start=True, stop=False)
                                rb = 64 * (h % 2)
                                ins = tensor.matmul(
                                    PF[:, 512 * kb:512 * kb + nn],
                                    qRTg[rb:rb + 32,
                                         512 * (h // 2) + 128 * (qt % 4):
                                         512 * (h // 2) + 128 * (qt % 4 + 1)],
                                    kRT4[rb:rb + 32, 512 * kb:512 * kb + nn],
                                    start=False, stop=(kb != nkb - 1))
                            ins = tensor.matmul(
                                PF[:, 128 * qt:128 * (qt + 1)],
                                ident[:, :], c.mdiagb[:, :],
                                start=False, stop=True, skip_group_check=True)
                            ins.then_inc(sSC, 1)
                            tensor.wait_ge(sDG, H + 1)
                            if h % 2 == 0 and pr >= 2:
                                tensor.wait_ge(sAE, pr - 1)
                            if h == 0 and qtg >= 1:
                                tensor.wait_ge(sOUT, qtg)
                            t2 = tau_start(H)
                            for kt in range(qt + 1):
                                if t2 >= 2:
                                    tensor.wait_ge(sTAE, t2 - 1)
                                ins = tensor.transpose(
                                    PT[t2 % 2][:, :],
                                    att[h % 2][:, 128 * kt:128 * (kt + 1)],
                                    ident[:, :])
                                ins.then_inc(sTA, 1)
                                tensor.wait_ge(sTAE, t2 + 1)
                                ins = tensor.matmul(
                                    PF[hb:hb + 64,
                                       2048 + 512 * (pr % 2):
                                       2048 + 512 * (pr % 2) + 128],
                                    Vt[:, D * kt + 64 * h:D * kt + 64 * (h + 1)],
                                    attT[t2 % 2][:, :],
                                    start=(kt == 0), stop=(kt == qt))
                                t2 += 1
                            ins.then_inc(sAV, 1)
                            if h == 15:
                                tensor.wait_ge(sAE, 8 * (qtg + 1))
                                for nb2 in range(2):
                                    for hc2 in range(8):
                                        ins = tensor.matmul(
                                            PF[:, 2048 + 512 * nb2:
                                               2048 + 512 * (nb2 + 1)],
                                            avh[:, 128 * hc2:128 * (hc2 + 1)],
                                            wow[:, D * hc2 + 512 * nb2:
                                                D * hc2 + 512 * (nb2 + 1)],
                                            start=(hc2 == 0), stop=(hc2 == 7))
                                    ins.then_inc(sU, 1)
                        bar.hit(wait=(sU, 2 * (TT * s_ + 4 * gb + 4)))

            @block.gpsimd
            def _(gpsimd):
                bar = Bar(gpsimd, sBAR)
                sg = Ser(gpsimd, sSG)
                bar.hit()
                nst = 0
                for s_ in range(NSEQ):
                    for gb in range(4):
                        bar.hit()
                        for (s2, qt, h, H, L, qtg, pr) in bitems(s_, gb):
                            if h == 15:
                                gpsimd.wait_ge(sOUT, qtg + 1)
                                g = TT * s_ + qt
                                gpsimd.dma_start(
                                    out=dst[128 * g:128 * (g + 1), :],
                                    in_=outt[qt % 2][:, :]).then_inc(sST, 16)
                                nst += 1
                        gpsimd.wait_ge(sST, 16 * nst)
                        bar.hit()

    c.boundary()


def st_l3_ffn(c, src, dst):
    nc = c.nc
    w = c.dram
    sem = c.sem
    names = ["sLD", "sW", "sCV", "sSTD", "sVAR", "sSTD2", "sLN", "sTE", "sTR",
             "sRT", "sTP2", "sGP", "sSI", "sUP", "sHM", "sTH", "sTHE", "sEO",
             "sAC", "sST", "sBAR", "sSV", "sSS", "sSG", "sZF", "sZT", "sZE"]
    (sLD, sW, sCV, sSTD, sVAR, sSTD2, sLN, sTE, sTR, sRT, sTP2, sGP, sSI, sUP,
     sHM, sTH, sTHE, sEO, sAC, sST, sBAR, sSV, sSS, sSG, sZF, sZT, sZE) = (
        sem(x) for x in names)

    GT = 16
    NG = NT // GT
    NU = 16
    IH = 1408
    NIC = 11

    with ExitStack() as ctx:
        sb = lambda name, shape, dt: ctx.enter_context(nc.sbuf_tensor(name, shape, dt))
        xt = [sb(f"l3f_xt{i}", [128, D], F32) for i in range(2)]
        hg = sb("l3f_hg", [128, IH], F32)
        sq = hg[:, 0:D]
        stat = sb("l3f_st", [128, 8], F32)
        eps = sb("l3f_eps", [128, 1], F32)
        lnb = [sb(f"l3f_lnb{i}", [128, D], BF16) for i in range(2)]
        lnTg = sb("l3f_lnTg", [128, 8 * GT * 128], BF16)
        acc = sb("l3f_acc", [128, GT * D], F32)
        comb = sb("l3f_comb", [128, GT * 8], F32)
        rsc = sb("l3f_rsc", [128, 8], F32)
        tkm = sb("l3f_tkm", [128, 1], F32)
        rwf = sb("l3f_rwf", [128, 64], F32)
        zf = sb("l3f_zf", [128, D], F32)
        zTf = sb("l3f_zTf", [128, D], F32)
        wstg = [sb(f"l3f_wstg{i}", [128, IH], F32) for i in range(2)]
        wg = sb("l3f_wg", [128, 8 * IH], BF16)
        wu = sb("l3f_wu", [128, 8 * IH], BF16)
        wd = sb("l3f_wd", [128, NIC * D], BF16)
        hsb = sb("l3f_hsb", [128, IH], BF16)
        hT = sb("l3f_hT", [128, NIC * 128], BF16)
        PF = ctx.enter_context(nc.psum_tensor("l3f_pf", [128, 2560], F32))
        PT = [ctx.enter_context(nc.psum_tensor(f"l3f_pt{i}", [128, 128], BF16))
              for i in range(2)]
        PTF = ctx.enter_context(nc.psum_tensor("l3f_ptf", [128, 128], F32))
        ident, identf = c.ident, c.identf
        GSUB = [(0, 512), (512, 512), (1024, 384)]
        DSUB = [(1536, 512), (2048, 512)]

        def chunk_src(e, half, j):
            if j < 8:
                return w["L3_ffn_gate_wT"][e, 128 * j:128 * (j + 1),
                                           IH * half:IH * (half + 1)], IH
            if j < 16:
                r = j - 8
                return w["L3_ffn_up_wT"][e, 128 * r:128 * (r + 1),
                                         IH * half:IH * (half + 1)], IH
            ic = j - 16
            r0 = IH * half + 128 * ic
            return w["L3_ffn_down_wT"][e, r0:r0 + 128, :], D

        with nc.Block() as block:
            @block.sync
            def _(sync):
                bar = Bar(sync, sBAR)
                for chd in range(8):
                    sync.dma_start(out=rwf[:, 8 * chd:8 * (chd + 1)],
                                   in_=w["L3_ffn_router_wT"][128 * chd:128 * (chd + 1), :]
                                   ).then_inc(sW, 16)
                sync.wait_ge(sW, 16 * 8)
                bar.hit()
                gl = [0]
                for grp in range(NG):
                    for tt in range(GT):
                        g = GT * grp + tt
                        if g >= 2:
                            sync.wait_ge(sLN, g - 1)
                        sync.dma_start(out=xt[g % 2][:, :],
                                       in_=src[128 * g:128 * (g + 1), :]
                                       ).then_inc(sLD, 16)
                    sync.wait_ge(sLD, 16 * GT * (grp + 1))
                    bar.hit()
                    for u in range(NU):
                        e, half = u // 2, u % 2
                        for j in range(27):
                            if gl[0] >= 2:
                                sync.wait_ge(sCV, gl[0] - 1)
                            srcap, width = chunk_src(e, half, j)
                            sync.dma_start(out=wstg[gl[0] % 2][:, 0:width],
                                           in_=srcap).then_inc(sW, 16)
                            gl[0] += 1
                    sync.wait_ge(sCV, 27 * NU * (grp + 1))
                    bar.hit()

            @block.vector
            def _(vector):
                bar = Bar(vector, sBAR)
                sv = Ser(vector, sSV)
                sv.do(vector.memset(eps[:, :], 1.0e-5))
                bar.hit()
                gl = [0]
                for grp in range(NG):
                    for tt in range(GT):
                        g = GT * grp + tt
                        vector.wait_ge(sSTD, g + 1)
                        ln_v1(vector, sv, stat, eps)
                        vector.sem_inc(sVAR, 1)
                        if grp >= 1:
                            vector.wait_ge(sST, 16 * (GT * (grp - 1) + tt + 1))
                        sv.do(vector.tensor_copy(acc[:, D * tt:D * (tt + 1)],
                                                 xt[g % 2][:, :]))
                        vector.wait_ge(sSTD2, g + 1)
                        if g >= 2:
                            vector.wait_ge(sTE, 8 * (g - 1))
                        ln_v2(vector, sv, stat, xt[g % 2], lnb[g % 2])
                        if g >= 1:
                            vector.wait_ge(sZT, 8 * g)
                        ins = vector.tensor_scalar(zf[:, :], xt[g % 2][:, :],
                                                   stat[:, 5:6], stat[:, 7:8],
                                                   ALU.mult, ALU.subtract)
                        sv.done(ins, sZF)
                        vector.sem_inc(sLN, 1)
                        for k in range(8):
                            tr = 8 * g + k
                            vector.wait_ge(sTE, tr + 1)
                            ins = vector.tensor_copy(
                                lnTg[:, GT * 128 * k + 128 * tt:
                                     GT * 128 * k + 128 * (tt + 1)],
                                PT[tr % 2][:, :])
                            sv.done(ins, sTR)
                        vector.wait_ge(sRT, g + 1)
                        sv.do(vector.tensor_copy(rsc[:, :], PF[:, 1408:1416]))
                        sv.do(vector.reduce_max(tkm[:, 0:1], rsc[:, :], axis=AX.X))
                        ins = vector.tensor_single_scalar(
                            comb[:, 8 * tt:8 * (tt + 1)], rsc[:, :],
                            tkm[:, 0:1], ALU.is_equal)
                        sv.done(ins, sTP2)
                    bar.hit()
                    for u in range(NU):
                        e = u // 2
                        for j in range(27):
                            vector.wait_ge(sW, 16 * (8 + gl[0] + 1))
                            width = IH if j < 16 else D
                            if j < 8:
                                tgt = wg[:, IH * j:IH * (j + 1)]
                            elif j < 16:
                                tgt = wu[:, IH * (j - 8):IH * (j - 7)]
                            else:
                                tgt = wd[:, D * (j - 16):D * (j - 15)]
                            ins = vector.tensor_copy(tgt,
                                                     wstg[gl[0] % 2][:, 0:width])
                            sv.done(ins, sCV)
                            gl[0] += 1
                        for tt in range(GT):
                            it = GT * (NU * grp + u) + tt
                            vector.wait_ge(sUP, 3 * (it + 1))
                            ins = vector.tensor_mul(hsb[:, :], PF[:, 0:IH],
                                                    hg[:, :])
                            sv.done(ins, sHM)
                            vector.wait_ge(sEO, 2 * (it + 1))
                            for nb in range(2):
                                ins = vector.scalar_tensor_tensor(
                                    acc[:, D * tt + 512 * nb:D * tt + 512 * (nb + 1)],
                                    PF[:, DSUB[nb][0]:DSUB[nb][0] + 512],
                                    comb[:, 8 * tt + e:8 * tt + e + 1],
                                    acc[:, D * tt + 512 * nb:D * tt + 512 * (nb + 1)],
                                    ALU.mult, ALU.add)
                                if nb == 1:
                                    sv.done(ins, sAC)
                                else:
                                    sv.do(ins)
                    bar.hit()

            @block.scalar
            def _(scalar):
                bar = Bar(scalar, sBAR)
                ss = Ser(scalar, sSS)
                bar.hit()
                for grp in range(NG):
                    for tt in range(GT):
                        g = GT * grp + tt
                        scalar.wait_ge(sLD, 16 * (g + 1))
                        ln_s1(scalar, ss, xt[g % 2], sq, stat)
                        scalar.sem_inc(sSTD, 1)
                        scalar.wait_ge(sVAR, g + 1)
                        ln_s2(scalar, ss, stat)
                        scalar.sem_inc(sSTD2, 1)
                        for k in range(8):
                            zt = 8 * g + k
                            scalar.wait_ge(sZT, zt + 1)
                            ins = scalar.activation(
                                zTf[:, 128 * k:128 * (k + 1)],
                                PTF[:, :], AF.Copy)
                            ss.done(ins, sZE)
                    bar.hit()
                    for u in range(NU):
                        for tt in range(GT):
                            it = GT * (NU * grp + u) + tt
                            scalar.wait_ge(sGP, 3 * (it + 1))
                            ins = scalar.activation(hg[:, :], PF[:, 0:IH],
                                                    AF.Silu)
                            ss.done(ins, sSI)
                            for k in range(NIC):
                                th = NIC * it + k
                                scalar.wait_ge(sTH, th + 1)
                                ins = scalar.activation(
                                    hT[:, 128 * k:128 * (k + 1)],
                                    PT[th % 2][:, :], AF.Copy)
                                ss.done(ins, sTHE)
                    bar.hit()

            @block.tensor
            def _(tensor):
                bar = Bar(tensor, sBAR)
                bar.hit()
                for grp in range(NG):
                    for tt in range(GT):
                        g = GT * grp + tt
                        tensor.wait_ge(sLN, g + 1)
                        for k in range(8):
                            tr = 8 * g + k
                            if tr >= 2:
                                tensor.wait_ge(sTR, tr - 1)
                            ins = tensor.transpose(PT[tr % 2][:, :],
                                                   lnb[g % 2][:, 128 * k:128 * (k + 1)],
                                                   ident[:, :])
                            ins.then_inc(sTE, 1)
                        tensor.wait_ge(sZF, g + 1)
                        for k in range(8):
                            zt = 8 * g + k
                            if zt >= 1:
                                tensor.wait_ge(sZE, zt)
                            ins = tensor.transpose(PTF[:, :],
                                                   zf[:, 128 * k:128 * (k + 1)],
                                                   identf[:, :])
                            ins.then_inc(sZT, 1)
                        tensor.wait_ge(sZE, 8 * (g + 1))
                        if g >= 1:
                            tensor.wait_ge(sTP2, g)
                        for chd in range(8):
                            ins = tensor.matmul(
                                PF[:, 1408:1416],
                                zTf[:, 128 * chd:128 * (chd + 1)],
                                rwf[:, 8 * chd:8 * (chd + 1)],
                                start=(chd == 0), stop=(chd == 7))
                        ins.then_inc(sRT, 1)
                    bar.hit(wait=(sRT, GT * (grp + 1)))
                    for u in range(NU):
                        gu = NU * grp + u
                        tensor.wait_ge(sCV, 27 * (gu + 1))
                        for tt in range(GT):
                            it = GT * gu + tt
                            if it >= 1:
                                tensor.wait_ge(sHM, it)
                            for nb in range(3):
                                off, wid = GSUB[nb]
                                for chd in range(8):
                                    ins = tensor.matmul(
                                        PF[:, off:off + wid],
                                        lnTg[:, GT * 128 * chd + 128 * tt:
                                             GT * 128 * chd + 128 * (tt + 1)],
                                        wg[:, IH * chd + off:IH * chd + off + wid],
                                        start=(chd == 0), stop=(chd == 7))
                                ins.then_inc(sGP, 1)
                            tensor.wait_ge(sSI, it + 1)
                            for nb in range(3):
                                off, wid = GSUB[nb]
                                for chd in range(8):
                                    ins = tensor.matmul(
                                        PF[:, off:off + wid],
                                        lnTg[:, GT * 128 * chd + 128 * tt:
                                             GT * 128 * chd + 128 * (tt + 1)],
                                        wu[:, IH * chd + off:IH * chd + off + wid],
                                        start=(chd == 0), stop=(chd == 7))
                                ins.then_inc(sUP, 1)
                            tensor.wait_ge(sHM, it + 1)
                            for k in range(NIC):
                                th = NIC * it + k
                                if th >= 2:
                                    tensor.wait_ge(sTHE, th - 1)
                                ins = tensor.transpose(
                                    PT[th % 2][:, :],
                                    hsb[:, 128 * k:128 * (k + 1)],
                                    ident[:, :])
                                ins.then_inc(sTH, 1)
                            tensor.wait_ge(sTHE, NIC * (it + 1))
                            if it >= 1:
                                tensor.wait_ge(sAC, it)
                            for nb in range(2):
                                off, wid = DSUB[nb]
                                for ic in range(NIC):
                                    ins = tensor.matmul(
                                        PF[:, off:off + wid],
                                        hT[:, 128 * ic:128 * (ic + 1)],
                                        wd[:, D * ic + 512 * nb:
                                           D * ic + 512 * (nb + 1)],
                                        start=(ic == 0), stop=(ic == NIC - 1))
                                ins.then_inc(sEO, 1)
                    bar.hit(wait=(sEO, 2 * GT * NU * (grp + 1)))

            @block.gpsimd
            def _(gpsimd):
                bar = Bar(gpsimd, sBAR)
                bar.hit()
                for grp in range(NG):
                    bar.hit()
                    gpsimd.wait_ge(sAC, GT * NU * (grp + 1))
                    for tt in range(GT):
                        g = GT * grp + tt
                        gpsimd.dma_start(out=dst[128 * g:128 * (g + 1), :],
                                         in_=acc[:, D * tt:D * (tt + 1)]
                                         ).then_inc(sST, 16)
                    gpsimd.wait_ge(sST, 16 * GT * (grp + 1))
                    bar.hit()

    c.boundary()


_COPY_N = [0]


def st_copy(c, src, dst):
    nc = c.nc
    sem = c.sem
    sLD, sCP, sST = sem("sLD"), sem("sCP"), sem("sST")
    _COPY_N[0] += 1
    u = _COPY_N[0]
    with ExitStack() as ctx:
        t = [ctx.enter_context(nc.sbuf_tensor(f"cp{u}_t{i}", [128, D], F32))
             for i in range(2)]
        with nc.Block() as block:
            @block.sync
            def _(sync):
                for g in range(NT):
                    if g >= 2:
                        sync.wait_ge(sST, 16 * (g - 1))
                    sync.dma_start(out=t[g % 2][:, :],
                                   in_=src[128 * g:128 * (g + 1), :]).then_inc(sLD, 16)
                sync.wait_ge(sLD, 16 * NT)

            @block.gpsimd
            def _(gpsimd):
                for g in range(NT):
                    gpsimd.wait_ge(sLD, 16 * (g + 1))
                    gpsimd.dma_start(out=dst[128 * g:128 * (g + 1), :],
                                     in_=t[g % 2][:, :]).then_inc(sST, 16)
                gpsimd.wait_ge(sST, 16 * NT)
    c.boundary()


STAGE_FNS = {}


def build_program(stage_names, in_specs):
    nc = bass.Bass("TRN2", target_bir_lowering=False)
    es = ExitStack()
    c = Ctx(nc, es)
    for nm, (shape, dt) in in_specs.items():
        c.dram[nm] = nc.dram_tensor(nm, list(shape), dt, kind="ExternalInput").ap()
    xa = nc.dram_tensor("xa", [NT * 128, D], F32, kind="Internal").ap()
    xb = nc.dram_tensor("xb", [NT * 128, D], F32, kind="Internal").ap()
    y = nc.dram_tensor("y", [NT * 128, D], F32, kind="ExternalOutput").ap()
    c.xa, c.xb, c.y = xa, xb, y

    with es:
        c.ident = es.enter_context(nc.sbuf_tensor("g_ident", [128, 128], BF16))
        c.identf = es.enter_context(nc.sbuf_tensor("g_identf", [128, 128], F32))
        c.onesf = es.enter_context(nc.sbuf_tensor("g_onesf", [1, 128], F32))
        c.onesb = es.enter_context(nc.sbuf_tensor("g_onesb", [1, 128], BF16))
        c.mdiag = es.enter_context(nc.sbuf_tensor("g_mdiag", [128, 128], F32))
        c.mdiagb = es.enter_context(nc.sbuf_tensor("g_mdiagb", [128, 128], BF16))
        c.tril = es.enter_context(nc.sbuf_tensor("g_tril", [128, 128], BF16))

        preamble(nc)
        s0 = c.sem("sLD")
        nc.sync.dma_start(out=c.ident[:, :], in_=c.dram["cid"][:, :]).then_inc(s0, 16)
        nc.sync.dma_start(out=c.onesf[:, :], in_=c.dram["conesf"][:, :]).then_inc(s0, 16)
        nc.sync.dma_start(out=c.onesb[:, :], in_=c.dram["conesb"][:, :]).then_inc(s0, 16)
        nc.sync.dma_start(out=c.mdiag[:, :], in_=c.dram["cmdiag"][:, :]).then_inc(s0, 16)
        nc.sync.dma_start(out=c.tril[:, :], in_=c.dram["ctril"][:, :]).then_inc(s0, 16)
        nc.sync.dma_start(out=c.identf[:, :], in_=c.dram["cidf"][:, :]).then_inc(s0, 16)
        nc.sync.dma_start(out=c.mdiagb[:, :], in_=c.dram["cmdiagb"][:, :]).then_inc(s0, 16)
        nc.sync.wait_ge(s0, 112)
        c.boundary()

        bufs = [c.dram["xin"], xa, xb]
        cur = 0
        for i, nm in enumerate(stage_names):
            srcb = bufs[cur]
            dstb = y if i == len(stage_names) - 1 else (xa if cur != 1 else xb)
            with nc.named_scope(f"st_{i}_{nm}"):
                STAGE_FNS[nm](c, srcb, dstb)
            cur = 1 if dstb is xa else (2 if dstb is xb else 0)
    return nc


STAGE_FNS["l0_attn"] = st_l0_attn
STAGE_FNS["l0_ffn"] = st_l0_ffn
STAGE_FNS["copy"] = st_copy
STAGE_FNS["l1_attn"] = st_l1_attn
STAGE_FNS["l1_ffn"] = st_l1_ffn
STAGE_FNS["l2_attn"] = st_l2_attn
STAGE_FNS["l2_ffn"] = st_l2_ffn
STAGE_FNS["l3_attn"] = st_l3_attn
STAGE_FNS["l3_ffn"] = st_l3_ffn

ALL_STAGES = ["l0_attn", "l0_ffn", "l1_attn", "l1_ffn",
              "l2_attn", "l2_ffn", "l3_attn", "l3_ffn"]

WEIGHT_NAMES = [
    "L0_attn_w_q", "L0_attn_w_k", "L0_attn_w_v", "L0_attn_w_u", "L0_attn_b_u",
]


def stage_weight_names(stage_names):
    pref = {"l0_attn": ["L0_attn_w_q", "L0_attn_w_k", "L0_attn_w_v",
                        "L0_attn_w_u", "L0_attn_b_u"],
            "l0_ffn": ["L0_ffn_w_gate", "L0_ffn_w_up", "L0_ffn_w_down"],
            "copy": [],
            "l1_attn": ["L1_attn_decay", "L1_attn_W", "L1_attn_b",
                        "L1_attn_gamma", "L1_attn_beta"],
            "l1_ffn": ["L1_ffn_gate_w", "L1_ffn_gate_b", "L1_ffn_w1",
                       "L1_ffn_b1", "L1_ffn_w2", "L1_ffn_b2"],
            "l2_attn": ["L2_attn_Wqk", "L2_attn_bqk", "L2_attn_gamma",
                        "L2_attn_beta", "L2_attn_Wv", "L2_attn_bv",
                        "L2_attn_rel_emb"],
            "l2_ffn": ["L2_ffn_router_W", "L2_ffn_router_b", "L2_ffn_gate_bias",
                       "L2_ffn_exp_Wup", "L2_ffn_exp_bup", "L2_ffn_exp_Wg",
                       "L2_ffn_exp_bg", "L2_ffn_exp_Wdown", "L2_ffn_exp_bdown",
                       "L2_ffn_share_Wup", "L2_ffn_share_bup", "L2_ffn_share_Wg",
                       "L2_ffn_share_bg", "L2_ffn_share_Wdown",
                       "L2_ffn_share_bdown", "L2_ffn_norm_w"],
            "l3_attn": ["L3_attn_W_DQ", "L3_attn_W_UQ", "L3_attn_W_DKV",
                        "L3_attn_W_UK", "L3_attn_W_UV", "L3_attn_W_QR",
                        "W_KR4", "L3_attn_W_O"],
            "l3_ffn": ["L3_ffn_router_wT", "L3_ffn_gate_wT", "L3_ffn_up_wT",
                       "L3_ffn_down_wT"]}
    out = []
    for s in stage_names:
        out += pref[s]
    return out


def _run_once(stage_names, x, inputs, trace=False):
    consts = HOST_CONSTS
    wn = stage_weight_names(stage_names)
    in_specs = {"xin": ((NT * 128, D), F32)}
    host_w = {}
    for nm in wn:
        if nm == "W_KR4":
            arr = np.ascontiguousarray(
                np.tile(np.asarray(inputs["L3_attn_W_KR"]), (1, 4)))
        elif nm == "L3_ffn_router_wT":
            arr = np.ascontiguousarray(np.asarray(inputs["L3_ffn_router_w"]).T)
        elif nm in ("L3_ffn_gate_wT", "L3_ffn_up_wT", "L3_ffn_down_wT"):
            arr = np.ascontiguousarray(
                np.asarray(inputs[nm[:-1]]).transpose(0, 2, 1))
        else:
            arr = np.ascontiguousarray(np.asarray(inputs[nm]))
        if arr.ndim == 0:
            arr = arr.reshape(1, 1)
        host_w[nm] = arr
        dt = I32 if arr.dtype == np.int32 else F32
        in_specs[nm] = (arr.shape, dt)
    for nm, arr in consts.items():
        in_specs[nm] = (arr.shape, BF16 if arr.dtype == BF else F32)

    nc = build_program(stage_names, in_specs)

    in_maps = []
    for core in range(8):
        m = {"xin": x[core]}
        m.update(host_w)
        m.update(consts)
        in_maps.append(m)
    res = run_bass_kernel_spmd(nc, in_maps, core_ids=list(range(8)), trace=trace)
    return [r["y"] for r in res.results]


def run_chain(stage_names, inputs, trace=False, split=False):
    x = np.ascontiguousarray(np.asarray(inputs["L0_attn_x"], dtype=np.float32))
    shards = [x[4 * core:4 * core + 4].reshape(NT * 128, D) for core in range(8)]
    if split:
        for nm in stage_names:
            shards = _run_once([nm], shards, inputs)
    else:
        shards = _run_once(stage_names, shards, inputs, trace=trace)
    return np.concatenate([s.reshape(4, S, D) for s in shards], axis=0)


def kernel(**inputs):
    return run_chain(ALL_STAGES, inputs)
